# version 01
# baseline (speedup 1.0000x reference)
import os
import numpy as np
import ml_dtypes
from contextlib import ExitStack

import concourse.bass as bass
import concourse.mybir as mybir
from concourse.bass_utils import run_bass_kernel_spmd

F32 = mybir.dt.float32
BF16 = mybir.dt.bfloat16
AF = mybir.ActivationFunctionType
ALU = mybir.AluOpType
AX = mybir.AxisListType

B, S, E = 4, 2048, 4096
T = 1024
NCH = 32
NQH, NKV, HD = 32, 8, 128
NE, RANK = 28, 8
NEG = -1e30

SCALE = float(np.float32(1.0) / np.sqrt(np.float32(128.0)))
EXPB = -20.0
EPS = 1e-5
C_INV4096 = float(np.float32(1.0 / 4096.0))


class Sem:
    def __init__(self, nc, stack, name):
        self.h = stack.enter_context(nc.semaphore(name))
        self.n = 0

    def tick(self, ins, k=1):
        ins.then_inc(self.h, k)
        self.n += k
        return self.n


def build_program(stage=4):
    nc = bass.Bass("TRN2", target_bir_lowering=False)

    XT = nc.dram_tensor("XT", [E, T], F32, kind="ExternalInput").ap()
    WIN = nc.dram_tensor("WIN", [E, 6144], F32, kind="ExternalInput").ap()
    WOUT = nc.dram_tensor("WOUT", [E, E], F32, kind="ExternalInput").ap()
    BW = nc.dram_tensor("BW", [E, E], F32, kind="ExternalInput").ap()
    AGW = nc.dram_tensor("AGW", [E, 252], F32, kind="ExternalInput").ap()
    BRE = nc.dram_tensor("BRE", [224, E], F32, kind="ExternalInput").ap()
    COS = nc.dram_tensor("COS", [128, T], F32, kind="ExternalInput").ap()
    SIN = nc.dram_tensor("SIN", [128, T], F32, kind="ExternalInput").ap()
    MK0 = nc.dram_tensor("MK0", [128, 128], F32, kind="ExternalInput").ap()
    MK1 = nc.dram_tensor("MK1", [128, 128], F32, kind="ExternalInput").ap()
    CONSTS = nc.dram_tensor("CONSTS", [128, 8], F32, kind="ExternalInput").ap()
    IDENT = nc.dram_tensor("IDENT", [128, 128], BF16, kind="ExternalInput").ap()
    OUT = nc.dram_tensor("OUT", [E, T], F32, kind="ExternalOutput").ap() if stage >= 4 else None
    DBG = nc.dram_tensor("DBG", [128, NCH * T], BF16, kind="ExternalOutput").ap() if stage != 4 else None

    spill = nc.dram_tensor("spill", [128, NCH * T], BF16, kind="Internal").ap()
    ccKi = nc.dram_tensor("ccKi", [128, 8192], BF16, kind="Internal")
    ccVi = nc.dram_tensor("ccVi", [128, 8192], BF16, kind="Internal")
    ccKo = nc.dram_tensor("ccKo", [256, 8192], BF16, kind="Internal")
    ccVo = nc.dram_tensor("ccVo", [256, 8192], BF16, kind="Internal")

    with ExitStack() as st:
        def sb(name, shape, dt):
            return st.enter_context(nc.sbuf_tensor(name, shape, dt))

        A = sb("A", [128, NCH * T], BF16)
        Bb = sb("Bb", [128, NCH * T], BF16)
        cosq = sb("cosq", [128, T], BF16)
        sinq = sb("sinq", [128, T], BF16)
        tri0 = sb("tri0", [128, 128], F32)
        tri1 = sb("tri1", [128, 128], F32)
        consts = sb("consts", [128, 8], F32)
        ident = sb("ident", [128, 128], BF16)
        ones_c = sb("ones_c", [128, 1], BF16)
        ones_r = sb("ones_r", [1, 128], BF16)
        wstream = sb("wstream", [128, NCH * 512], BF16)

        nc.const_aps.register(F32, EXPB, consts[:, 0:1])
        nc.const_aps.register(F32, SCALE, consts[:, 1:2])
        nc.const_aps.register(F32, -1.0, consts[:, 2:3])
        nc.const_aps.register(F32, EPS, consts[:, 3:4])
        nc.const_aps.register(F32, -0.5, consts[:, 4:5])

        sw = Sem(nc, st, "sw")
        ss = Sem(nc, st, "ss")
        sg = Sem(nc, st, "sg")
        sp = Sem(nc, st, "sp")
        sv = Sem(nc, st, "sv")
        sa = Sem(nc, st, "sa")
        scc = Sem(nc, st, "scc")

        ss.tick(nc.sync.dma_start(consts[:], CONSTS), 16)
        ss.tick(nc.sync.dma_start(tri0[:], MK0), 16)
        ss.tick(nc.sync.dma_start(tri1[:], MK1), 16)
        ss.tick(nc.sync.dma_start(ident[:], IDENT), 16)
        sw.tick(nc.gpsimd.dma_start(cosq[:], COS), 16)
        sw.tick(nc.gpsimd.dma_start(sinq[:], SIN), 16)
        for c in range(NCH):
            sw.tick(nc.gpsimd.dma_start(A[:, c * T:(c + 1) * T], XT[c * 128:(c + 1) * 128, :]), 16)
        sv.tick(nc.vector.memset(ones_c[:], 1.0))
        sv.tick(nc.vector.memset(ones_r[:], 1.0))
        base_loads = dict(sw=sw.n, ss=ss.n, sv=sv.n)

        nc.scalar.wait_ge(ss.h, base_loads["ss"])
        nc.vector.wait_ge(ss.h, base_loads["ss"])
        nc.tensor.wait_ge(sv.h, base_loads["sv"])

        def barrier(*engs):
            for eng in engs:
                eng.wait_ge(sp.h, sp.n)
                eng.wait_ge(sv.h, sv.n)
                eng.wait_ge(sa.h, sa.n)
                eng.wait_ge(sw.h, sw.n)
                eng.wait_ge(ss.h, ss.n)
                eng.wait_ge(sg.h, sg.n)

        def barrier_all():
            barrier(nc.gpsimd, nc.sync, nc.scalar, nc.vector, nc.tensor)


        class _LnCnt:
            n = 0

        def ln(src, dst):
            barrier_all()
            _LnCnt.n += 1
            u = f"_l{_LnCnt.n}"
            with ExitStack() as lst:
                xsq = [lst.enter_context(nc.sbuf_tensor(f"xsq{i}{u}", [128, T], BF16)) for i in range(2)]
                m_row = lst.enter_context(nc.sbuf_tensor("m_row" + u, [1, T], F32))
                v_row = lst.enter_context(nc.sbuf_tensor("v_row" + u, [1, T], F32))
                t_row = lst.enter_context(nc.sbuf_tensor("t_row" + u, [1, T], F32))
                rstd_row = lst.enter_context(nc.sbuf_tensor("rstd_row" + u, [1, T], BF16))
                mr_row = lst.enter_context(nc.sbuf_tensor("mr_row" + u, [1, T], BF16))
                rstd_b = lst.enter_context(nc.sbuf_tensor("rstd_b" + u, [128, T], BF16))
                mrb_b = lst.enter_context(nc.sbuf_tensor("mrb_b" + u, [128, T], BF16))
                with ExitStack() as pst:
                    Sp = pst.enter_context(nc.psum_tensor("Sp" + u, [1, T], F32))
                    Sq = pst.enter_context(nc.psum_tensor("Sq" + u, [1, T], F32))
                    sq_marks = [0, 0]
                    for c in range(NCH):
                        if c >= 2:
                            nc.scalar.wait_ge(sp.h, sq_marks[c % 2])
                        sqm = sa.tick(nc.scalar.activation(xsq[c % 2][:], src[:, c * T:(c + 1) * T], AF.Square))
                        for reg in range(2):
                            nc.tensor.matmul(Sp[0:1, reg * 512:(reg + 1) * 512], lhsT=ones_c[:],
                                             rhs=src[:, c * T + reg * 512: c * T + (reg + 1) * 512],
                                             start=(c == 0), stop=(c == NCH - 1))
                        nc.tensor.wait_ge(sa.h, sqm)
                        last = None
                        for reg in range(2):
                            last = nc.tensor.matmul(Sq[0:1, reg * 512:(reg + 1) * 512], lhsT=ones_c[:],
                                                    rhs=xsq[c % 2][:, reg * 512:(reg + 1) * 512],
                                                    start=(c == 0), stop=(c == NCH - 1))
                        sq_marks[c % 2] = sp.tick(last)
                    stats_done = sp.n
                    nc.vector.wait_ge(sp.h, stats_done)
                    nc.vector.tensor_scalar_mul(m_row[0:1, :], Sp[0:1, :], C_INV4096)
                    nc.vector.tensor_scalar_mul(v_row[0:1, :], Sq[0:1, :], C_INV4096)
                    nc.vector.tensor_tensor(t_row[0:1, :], m_row[0:1, :], m_row[0:1, :], ALU.mult)
                    rows1 = sv.tick(nc.vector.tensor_tensor(v_row[0:1, :], v_row[0:1, :], t_row[0:1, :], ALU.subtract))
                    nc.scalar.wait_ge(sv.h, rows1)
                    lnm = sa.tick(nc.scalar.activation(t_row[0:1, :], v_row[0:1, :], AF.Ln, bias=EPS))
                    nc.scalar.wait_ge(sa.h, lnm)
                    rsm = sa.tick(nc.scalar.activation(rstd_row[0:1, :], t_row[0:1, :], AF.Exp, scale=-0.5))
                    nc.vector.wait_ge(sa.h, rsm)
                    mrm = sv.tick(nc.vector.tensor_tensor(mr_row[0:1, :], m_row[0:1, :], rstd_row[0:1, :], ALU.mult))
                with ExitStack() as pst:
                    bp0 = pst.enter_context(nc.psum_tensor("bp0" + u, [128, T], F32))
                    bp1 = pst.enter_context(nc.psum_tensor("bp1" + u, [128, T], F32))
                    nc.tensor.wait_ge(sv.h, mrm)
                    last = None
                    for reg in range(2):
                        nc.tensor.matmul(bp0[:, reg * 512:(reg + 1) * 512], lhsT=ones_r[:],
                                         rhs=rstd_row[0:1, reg * 512:(reg + 1) * 512], start=True, stop=True)
                        last = nc.tensor.matmul(bp1[:, reg * 512:(reg + 1) * 512], lhsT=ones_r[:],
                                                rhs=mr_row[0:1, reg * 512:(reg + 1) * 512], start=True, stop=True)
                    bm = sp.tick(last)
                    nc.vector.wait_ge(sp.h, bm)
                    nc.vector.tensor_copy(rstd_b[:], bp0[:, :])
                    bcm = sv.tick(nc.vector.tensor_copy(mrb_b[:], bp1[:, :]))
                    if os.environ.get("KDUMPLN", "") == "1" and _LnCnt.n == int(os.environ.get("KDUMPLN_N", "3")) and DBG is not None:
                        lrows = lst.enter_context(nc.sbuf_tensor("lrows" + u, [1, 3072], F32))
                        nc.vector.tensor_copy(lrows[0:1, 0:1024], rstd_row[0:1, :])
                        nc.vector.tensor_copy(lrows[0:1, 1024:2048], m_row[0:1, :])
                        lrm = sv.tick(nc.vector.tensor_copy(lrows[0:1, 2048:3072], v_row[0:1, :]))
                        nc.gpsimd.wait_ge(sv.h, lrm)
                        sw.tick(nc.gpsimd.dma_start(DBG[0:1, 8192:11264], lrows[0:1, :]), 16)
                        sw.tick(nc.gpsimd.dma_start(DBG[:, 11264:12288], rstd_b[:, :]), 16)
                    nc.vector.wait_ge(sv.h, bcm)
                    last = None
                    for c in range(NCH):
                        cs = slice(c * T, (c + 1) * T)
                        nc.vector.tensor_tensor(dst[:, cs], src[:, cs], rstd_b[:], ALU.mult)
                        last = nc.vector.tensor_tensor(dst[:, cs], dst[:, cs], mrb_b[:], ALU.subtract)
                    return sv.tick(last)

        def rope_evac(psum_ap, dst_ap, width, qtmp, qswap):
            t0 = rope_evac.t0
            qm = sa.tick(nc.scalar.activation(qtmp[:, 0:width], psum_ap, AF.Copy))
            nc.vector.wait_ge(sa.h, qm)
            nc.vector.tensor_copy(qswap[0:64, 0:width], qtmp[64:128, 0:width])
            nc.vector.tensor_copy(qswap[64:128, 0:width], qtmp[0:64, 0:width])
            nc.vector.tensor_tensor(qtmp[:, 0:width], qtmp[:, 0:width], cosq[:, t0:t0 + width], ALU.mult)
            nc.vector.tensor_tensor(qswap[:, 0:width], qswap[:, 0:width], sinq[:, t0:t0 + width], ALU.mult)
            dm = sv.tick(nc.vector.tensor_tensor(dst_ap, qtmp[:, 0:width], qswap[:, 0:width], ALU.add))
            return qm, dm
        rope_evac.t0 = 0

        def ffn(src, resid, is_final, tag, parts=3):
            barrier_all()
            if os.environ.get("KDELAY", "") == "1" and tag == "f2":
                for eng in (nc.tensor, nc.vector, nc.scalar, nc.gpsimd, nc.sync):
                    for _ in range(30):
                        eng.nop(cycle_cnt=65535)
                barrier_all()
            with ExitStack() as fst:
                fsb = lambda n_, s, d: fst.enter_context(nc.sbuf_tensor(n_ + tag, s, d))
                bres0 = fsb("bres0", [112, E], BF16)
                bres1 = fsb("bres1", [112, E], BF16)
                hsT0 = fsb("hsT0", [112, T], BF16)
                hsT1 = fsb("hsT1", [112, T], BF16)
                g_sb = fsb("g_sb", [128, NE], F32)
                gx = fsb("gx", [128, 224], F32)
                eq = fsb("eq", [128, NE], F32)
                msk = fsb("msk", [128, NE], F32)
                ge_m = fsb("ge_m", [128, NE], F32)
                en = fsb("en", [128, NE], F32)
                hs = fsb("hs", [128, 224], BF16)
                m1 = fsb("m1", [128, 1], F32)
                negm1 = fsb("negm1", [128, 1], F32)
                m2 = fsb("m2", [128, 1], F32)
                e2 = fsb("e2", [128, 1], F32)
                den = fsb("den", [128, 1], F32)
                rec = fsb("rec", [128, 1], F32)

                if os.environ.get("KNOBRES", "") != "1":
                    sw.tick(nc.gpsimd.dma_start(bres0[:], BRE[0:112, :]), 16)
                    bres_mark = sw.tick(nc.gpsimd.dma_start(bres1[:], BRE[112:224, :]), 16)
                else:
                    bres_mark = sw.n

                if not (parts & 1):
                    hsT_mark = sv.tick(nc.vector.memset(hsT0[:], 0.0))
                    hsT_mark = sv.tick(nc.vector.memset(hsT1[:], 0.0))
                elif True:
                  with ExitStack() as hst:
                    agres = hst.enter_context(nc.sbuf_tensor("agres" + tag, [128, NCH * 252], BF16))
                    for c in range(NCH):
                        sw.tick(nc.gpsimd.dma_start(agres[:, c * 252:(c + 1) * 252],
                                                    AGW[c * 128:(c + 1) * 128, :]), 16)
                    ag_mark = sw.n
                    with ExitStack() as pst:
                        if os.environ.get("KPSHIFT", "") == "1" and tag == "f2":
                            pst.enter_context(nc.psum_tensor("dummy" + tag, [128, 1024], F32))
                        hp = pst.enter_context(nc.psum_tensor("hp" + tag, [128, 252], F32))
                        hT0 = pst.enter_context(nc.psum_tensor("hT0" + tag, [112, 128], BF16))
                        hT1 = pst.enter_context(nc.psum_tensor("hT1" + tag, [112, 128], BF16))
                        nc.tensor.wait_ge(sw.h, ag_mark)
                        hp_free = 0
                        hT_free = 0
                        for tt in range(int(os.environ.get("KTT", "8"))):
                            if os.environ.get("KHAMMER", "1") == "1":
                                barrier(nc.tensor, nc.vector, nc.scalar)
                            if tt > 0:
                                nc.tensor.wait_ge(sv.h, hp_free)
                            last = None
                            for c in range(NCH):
                                last = nc.tensor.matmul(
                                    hp[:, 0:252],
                                    lhsT=src[:, c * T + tt * 128: c * T + (tt + 1) * 128],
                                    rhs=agres[:, c * 252:(c + 1) * 252],
                                    start=(c == 0), stop=(c == NCH - 1))
                            hpm = sp.tick(last)
                            logits = hp[:, 224:252]
                            nc.vector.wait_ge(sp.h, hpm)
                            if ffn.simple_gate:
                                nc.vector.memset(gx[:], 0.001)
                                hsm = sv.tick(nc.vector.tensor_tensor(hs[:], hp[:, 0:224], gx[:], ALU.mult))
                                hp_free = hsm
                                if ffn.skip_transpose:
                                    nc.vector.wait_ge(sv.h, hsm)
                                    nc.vector.tensor_copy(hsT0[:, tt * 128:(tt + 1) * 128], hs[0:112, 0:128])
                                    hT_free = sv.tick(nc.vector.tensor_copy(hsT1[:, tt * 128:(tt + 1) * 128], hs[0:112, 96:224]))
                                    continue
                                nc.tensor.wait_ge(sv.h, hsm)
                                if tt > 0:
                                    nc.tensor.wait_ge(sv.h, hT_free)
                                nc.tensor.transpose(hT0[:, :], hs[:, 0:112], ident[:])
                                trm = sp.tick(nc.tensor.transpose(hT1[:, :], hs[:, 112:224], ident[:]))
                                nc.vector.wait_ge(sp.h, trm)
                                nc.vector.tensor_copy(hsT0[:, tt * 128:(tt + 1) * 128], hT0[:, :])
                                hT_free = sv.tick(nc.vector.tensor_copy(hsT1[:, tt * 128:(tt + 1) * 128], hT1[:, :]))
                                continue
                            m1m = sv.tick(nc.vector.tensor_reduce(m1[:], logits, AX.X, ALU.max))
                            nc.vector.wait_ge(sv.h, m1m)
                            nc.vector.tensor_scalar(eq[:], logits, m1[:], None, ALU.is_equal)
                            nc.vector.tensor_scalar_mul(negm1[:], m1[:], -1.0)
                            nc.vector.scalar_tensor_tensor(msk[:], eq[:], NEG, logits, ALU.mult, ALU.add)
                            m2m = sv.tick(nc.vector.tensor_reduce(m2[:], msk[:], AX.X, ALU.max))
                            nc.vector.wait_ge(sv.h, m2m)
                            nm = sv.tick(nc.vector.tensor_scalar(ge_m[:], logits, m2[:], None, ALU.is_ge))
                            nc.scalar.wait_ge(sv.h, nm)
                            enm = sa.tick(nc.scalar.activation(en[:], logits, AF.Exp, bias=negm1[:, 0:1]))
                            e2m = sa.tick(nc.scalar.activation(e2[:], m2[:], AF.Exp, bias=negm1[:, 0:1]))
                            nc.vector.wait_ge(sa.h, e2m)
                            dnm = sv.tick(nc.vector.tensor_scalar_add(den[:], e2[:], 1.0))
                            nc.vector.wait_ge(sv.h, dnm)
                            rcm = sv.tick(nc.vector.reciprocal(rec[:], den[:]))
                            nc.vector.tensor_tensor(g_sb[:], en[:], ge_m[:], ALU.mult)
                            nc.vector.wait_ge(sv.h, rcm)
                            nc.vector.tensor_scalar(g_sb[:], g_sb[:], rec[:], None, ALU.mult)
                            for rr in range(8):
                                nc.vector.tensor_copy(gx[:, rr:224:8], g_sb[:])
                            if os.environ.get("KDUMPG", "") == "1" and tag == "f2" and tt == 0 and DBG is not None:
                                sw.tick(nc.gpsimd.dma_start(DBG[:, 4096:4096 + 252], agres[:, 0:252]), 16)
                                sw.tick(nc.gpsimd.dma_start(DBG[:, 4352:4352 + 1024], src[:, 0:1024]), 16)
                                gdump = hst.enter_context(nc.sbuf_tensor("gdump", [128, 64], F32))
                                nc.vector.tensor_copy(gdump[:, 0:28], logits)
                                nc.vector.tensor_copy(gdump[:, 28:56], g_sb[:])
                                nc.vector.tensor_copy(gdump[:, 56:57], m1[:])
                                nc.vector.tensor_copy(gdump[:, 57:58], m2[:])
                                nc.vector.tensor_copy(gdump[:, 58:59], rec[:])
                                gdm = sv.tick(nc.vector.tensor_copy(gdump[:, 59:60], negm1[:]))
                                nc.gpsimd.wait_ge(sv.h, gdm)
                                sw.tick(nc.gpsimd.dma_start(DBG[:, 2048:2112], gdump[:, :]), 16)
                            hsm = sv.tick(nc.vector.tensor_tensor(hs[:], hp[:, 0:224], gx[:], ALU.mult))
                            hp_free = hsm
                            nc.tensor.wait_ge(sv.h, hsm)
                            if tt > 0:
                                nc.tensor.wait_ge(sv.h, hT_free)
                            nc.tensor.transpose(hT0[:, :], hs[:, 0:112], ident[:])
                            trm = sp.tick(nc.tensor.transpose(hT1[:, :], hs[:, 112:224], ident[:]))
                            nc.vector.wait_ge(sp.h, trm)
                            nc.vector.tensor_copy(hsT0[:, tt * 128:(tt + 1) * 128], hT0[:, :])
                            hT_free = sv.tick(nc.vector.tensor_copy(hsT1[:, tt * 128:(tt + 1) * 128], hT1[:, :]))
                        hsT_mark = hT_free
                if os.environ.get("KDUMPHST", "") == "1" and tag == "f2" and DBG is not None:
                    nc.sync.wait_ge(sv.h, sv.n)
                    ss.tick(nc.sync.dma_start(DBG[0:112, 0:1024], hsT0[:, :]), 16)
                    ss.tick(nc.sync.dma_start(DBG[0:112, 1024:2048], hsT1[:, :]), 16)

                if not (parts & 2):
                    return
                with ExitStack() as bst:
                    wbuf = wstream
                    barrier(nc.gpsimd)
                    ofp = [bst.enter_context(nc.sbuf_tensor("ofp" + tag, [128, 512], F32))] if is_final else []
                    with ExitStack() as pst:
                        psums = [pst.enter_context(nc.psum_tensor(f"pb{o}" + tag, [128, 512], F32)) for o in range(4)]
                        psum_free = [0, 0, 0, 0]
                        ofp_free = [0]
                        og_mark = {}
                        w_marks = {}
                        for og in range(8):
                            if og >= 1:
                                nc.gpsimd.wait_ge(sp.h, og_mark[og - 1])
                            for c in range(NCH):
                                sw.tick(nc.gpsimd.dma_start(
                                    wbuf[:, c * 512:(c + 1) * 512],
                                    BW[c * 128:(c + 1) * 128, og * 512:(og + 1) * 512]), 16)
                            w_marks[og] = sw.n
                            nc.tensor.wait_ge(sw.h, w_marks[og])
                            if og == 0:
                                nc.tensor.wait_ge(sw.h, bres_mark)
                                nc.tensor.wait_ge(sv.h, hsT_mark)
                            last = None
                            for tc in range(2):
                                for o in range(4):
                                    op = (og * 4 + o)
                                    if psum_free[o]:
                                        nc.tensor.wait_ge(sv.h, psum_free[o])
                                    for c in range(NCH):
                                        nc.tensor.matmul(
                                            psums[o][:, :],
                                            lhsT=wbuf[:, c * 512 + o * 128: c * 512 + (o + 1) * 128],
                                            rhs=src[:, c * T + tc * 512: c * T + tc * 512 + 512],
                                            start=(c == 0), stop=False)
                                    nc.tensor.matmul(psums[o][:, :], lhsT=bres0[:, op * 128:(op + 1) * 128],
                                                     rhs=hsT0[:, tc * 512:(tc + 1) * 512], start=False, stop=False)
                                    last = nc.tensor.matmul(psums[o][:, :], lhsT=bres1[:, op * 128:(op + 1) * 128],
                                                            rhs=hsT1[:, tc * 512:(tc + 1) * 512], start=False, stop=True)
                                    gm = sp.tick(last)
                                    nc.vector.wait_ge(sp.h, gm)
                                    dsl = slice(op * T + tc * 512, op * T + tc * 512 + 512)
                                    if not is_final:
                                        psum_free[o] = sv.tick(nc.vector.scalar_tensor_tensor(
                                            resid[:, dsl], psums[o][:, :], 0.5, resid[:, dsl], ALU.mult, ALU.add))
                                    else:
                                        if ofp_free[0]:
                                            nc.vector.wait_ge(ss.h, ofp_free[0])
                                        em = sv.tick(nc.vector.scalar_tensor_tensor(
                                            ofp[0][:], psums[o][:, :], 0.5, resid[:, dsl], ALU.mult, ALU.add))
                                        psum_free[o] = em
                                        nc.sync.wait_ge(sv.h, em)
                                        ofp_free[0] = ss.tick(nc.sync.dma_start(
                                            OUT[op * 128:(op + 1) * 128, tc * 512:tc * 512 + 512], ofp[0][:]), 16)
                            og_mark[og] = sp.n


        ffn.store_cnt = 0
        ffn.simple_gate = False
        ffn.skip_transpose = False
        nc.vector.wait_ge(sw.h, base_loads["sw"])
        nc.scalar.wait_ge(sw.h, base_loads["sw"])
        nc.tensor.wait_ge(sw.h, base_loads["sw"])
        if stage == 0:
            nc.sync.wait_ge(sw.h, base_loads["sw"])
            ss.tick(nc.sync.dma_start(DBG, A[:, :]), 16)
            nc.sync.wait_ge(ss.h, ss.n)
            return nc
        if os.environ.get("KNOLN", "") == "1":
            for c in range(NCH):
                sv.tick(nc.vector.tensor_copy(Bb[:, c * T:(c + 1) * T], A[:, c * T:(c + 1) * T]))
        else:
            ln(A, Bb)
        nc.tensor.wait_ge(sv.h, sv.n)
        if stage == 1:
            nc.sync.wait_ge(sv.h, sv.n)
            ss.tick(nc.sync.dma_start(DBG, Bb[:, :]), 16)
            nc.sync.wait_ge(ss.h, ss.n)
            return nc

        ffn.simple_gate = (stage in (18, 19))
        ffn.skip_transpose = (stage == 19)
        _parts = {16: 1, 17: 2, 19: 1}.get(stage, 3)
        ffn(Bb, A, False, "f1", parts=_parts)
        ffn.simple_gate = False

        if stage in (15, 16, 17, 18, 19):
            nc.sync.wait_ge(sv.h, sv.n)
            ss.tick(nc.sync.dma_start(DBG, A[:, :]), 16)
            nc.sync.wait_ge(ss.h, ss.n)
            return nc
        if stage == 2:
            nc.sync.wait_ge(sv.h, sv.n)
            fm = ss.tick(nc.sync.dma_start(DBG, A[:, :]), 16)
            nc.sync.wait_ge(ss.h, ss.n)
            return nc

        if stage == 40:
            ln(A, Bb)
            nc.tensor.wait_ge(sv.h, sv.n)
            ffn(Bb, A, False, "f2")
            nc.sync.wait_ge(sv.h, sv.n)
            ss.tick(nc.sync.dma_start(DBG, A[:, :]), 16)
            nc.sync.wait_ge(ss.h, ss.n)
            return nc

        ln2_mark = ln(A, Bb)
        nc.tensor.wait_ge(sv.h, sv.n)
        nc.sync.wait_ge(sv.h, sv.n)
        spill_mark = ss.tick(nc.sync.dma_start(spill, A[:, :]), 16)

        barrier_all()
        with ExitStack() as kst:
            wbufk = wstream
            kstg = kst.enter_context(nc.sbuf_tensor("kstg", [128, 4096], BF16))
            vstg = kst.enter_context(nc.sbuf_tensor("vstg", [128, 4096], BF16))
            vtms = kst.enter_context(nc.sbuf_tensor("vtms", [128, 4096], BF16))
            qtmp0 = kst.enter_context(nc.sbuf_tensor("qtmp0", [128, T], BF16))
            qswap0 = kst.enter_context(nc.sbuf_tensor("qswap0", [128, T], BF16))
            with ExitStack() as pst:
                kp = [pst.enter_context(nc.psum_tensor(f"kp{i}", [128, T], F32)) for i in range(2)]
                vtp = [pst.enter_context(nc.psum_tensor(f"vtp{i}", [128, 128], BF16)) for i in range(2)]
                kp_free = [0, 0]
                kp_free_a = [0, 0]
                vtp_free = [0, 0]
                kg_mark = 0
                for kg in range(4):
                    cols0 = 4096 + kg * 512
                    is_v = kg >= 2
                    if kg >= 1:
                        nc.gpsimd.wait_ge(sp.h, kg_mark)
                        nc.vector.wait_ge(sg.h, sg.n)
                        nc.scalar.wait_ge(sg.h, sg.n)
                    for c in range(NCH):
                        sw.tick(nc.gpsimd.dma_start(
                            wbufk[:, c * 512:(c + 1) * 512],
                            WIN[c * 128:(c + 1) * 128, cols0:cols0 + 512]), 16)
                    wkm = sw.n
                    nc.tensor.wait_ge(sw.h, wkm)
                    stg_store = 0
                    for h4 in range(4):
                        pb = h4 % 2
                        if kp_free[pb]:
                            nc.tensor.wait_ge(sv.h, kp_free[pb])
                            nc.tensor.wait_ge(sa.h, kp_free_a[pb])
                        last = None
                        for tc in range(2):
                            for c in range(NCH):
                                last = nc.tensor.matmul(
                                    kp[pb][:, tc * 512:(tc + 1) * 512],
                                    lhsT=wbufk[:, c * 512 + h4 * 128: c * 512 + (h4 + 1) * 128],
                                    rhs=Bb[:, c * T + tc * 512: c * T + tc * 512 + 512],
                                    start=(c == 0), stop=(c == NCH - 1))
                        km = sp.tick(last)
                        if not is_v:
                            nc.scalar.wait_ge(sp.h, km)
                            rope_evac.t0 = 0
                            am, dm = rope_evac(kp[pb][:, 0:T], kstg[:, h4 * T:(h4 + 1) * T], T, qtmp0, qswap0)
                            kp_free[pb] = dm
                            kp_free_a[pb] = am
                        else:
                            nc.scalar.wait_ge(sp.h, km)
                            vm = sa.tick(nc.scalar.activation(vstg[:, h4 * T:(h4 + 1) * T], kp[pb][:, 0:T], AF.Copy))
                            kp_free[pb] = 0
                            kp_free_a[pb] = vm
                            nc.tensor.wait_ge(sa.h, vm)
                            for j in range(8):
                                tb = j % 2
                                if vtp_free[tb]:
                                    nc.tensor.wait_ge(sv.h, vtp_free[tb])
                                trm = sp.tick(nc.tensor.transpose(
                                    vtp[tb][:, :], vstg[:, h4 * T + j * 128: h4 * T + (j + 1) * 128], ident[:]))
                                nc.vector.wait_ge(sp.h, trm)
                                vtp_free[tb] = sv.tick(nc.vector.tensor_copy(
                                    vtms[:, h4 * T + j * 128: h4 * T + (j + 1) * 128], vtp[tb][:, :]))
                    kg_mark = sp.n
                    if not is_v:
                        nc.scalar.wait_ge(sv.h, sv.n)
                        sg.tick(nc.scalar.dma_start(ccKi.ap()[:, kg * 4096:(kg + 1) * 4096], kstg[:]), 16)
                    else:
                        nc.scalar.wait_ge(sv.h, sv.n)
                        sg.tick(nc.scalar.dma_start(ccVi.ap()[:, (kg - 2) * 4096:(kg - 1) * 4096], vtms[:]), 16)

        nc.gpsimd.wait_ge(sg.h, sg.n)
        groups = [[0, 1], [2, 3], [4, 5], [6, 7]]
        scc.tick(nc.gpsimd.collective_compute(
            "AllGather", ALU.bypass, replica_groups=groups,
            ins=[ccKi.ap().opt()], outs=[ccKo.ap().opt()]), 1)
        scc.tick(nc.gpsimd.collective_compute(
            "AllGather", ALU.bypass, replica_groups=groups,
            ins=[ccVi.ap().opt()], outs=[ccVo.ap().opt()]), 1)

        barrier_all()
        with ExitStack() as ast:
            asb = lambda n_, s, d: ast.enter_context(nc.sbuf_tensor(n_, s, d))
            kbuf = asb("kbuf", [128, 2048], BF16)
            vbuf = asb("vbuf", [128, 2048], BF16)
            wq = wstream
            qh = [asb(f"qh{i}", [128, T], BF16) for i in range(2)]
            pt = [asb(f"pt{i}", [128, T], BF16) for i in range(2)]
            qtmp = asb("qtmp", [128, T], BF16)
            qswap = asb("qswap", [128, T], BF16)
            rz_sb = asb("rz_sb", [128, T], F32)
            rzrow = asb("rzrow", [1, T], F32)
            rz2row = asb("rz2row", [1, T], BF16)

            def load_wq(g):
                for c in range(NCH):
                    sw.tick(nc.gpsimd.dma_start(
                        wq[:, c * 512:(c + 1) * 512],
                        WIN[c * 128:(c + 1) * 128, g * 512:(g + 1) * 512]), 16)
                return sw.n

            def q_mms(Qp, g2):
                last = None
                for tc in range(2):
                    for c in range(NCH):
                        last = nc.tensor.matmul(
                            Qp[:, tc * 512:(tc + 1) * 512],
                            lhsT=wq[:, c * 512 + g2 * 128: c * 512 + (g2 + 1) * 128],
                            rhs=Bb[:, c * T + tc * 512: c * T + tc * 512 + 512],
                            start=(c == 0), stop=(c == NCH - 1))
                return sp.tick(last)

            wq_mark = load_wq(0)
            kv_load_mark = 0
            qrope_marks = {}
            qrope_amarks = {}
            with ExitStack() as pst0:
                Qp0 = pst0.enter_context(nc.psum_tensor("Qpre", [128, T], F32))
                nc.tensor.wait_ge(sw.h, wq_mark)
                nc.tensor.wait_ge(sv.h, sv.n)
                nc.tensor.wait_ge(sa.h, sa.n)
                q0m = q_mms(Qp0, 0)
                nc.scalar.wait_ge(sp.h, q0m)
                rope_evac.t0 = 0
                am, dm = rope_evac(Qp0[:, 0:T], qh[0][:, 0:T], T, qtmp, qswap)
                qrope_marks[0] = dm
                qrope_amarks[0] = am

            aevac_mark = 0
            rzc_mark = 0
            ln_mark = 0
            wq_loaded_group = 0
            for h in range(NQH):
                kh, g2 = h // 4, h % 4
                cur = h % 2
                if g2 == 0:
                    nc.sync.wait_ge(scc.h, 2)
                    nc.sync.wait_ge(sp.h, sp.n)
                    ss.tick(nc.sync.dma_start(kbuf[:, 0:1024], ccKo.ap()[0:128, kh * 1024:(kh + 1) * 1024]), 16)
                    ss.tick(nc.sync.dma_start(kbuf[:, 1024:2048], ccKo.ap()[128:256, kh * 1024:(kh + 1) * 1024]), 16)
                    ss.tick(nc.sync.dma_start(vbuf[:, 0:1024], ccVo.ap()[0:128, kh * 1024:(kh + 1) * 1024]), 16)
                    kv_load_mark = ss.tick(nc.sync.dma_start(vbuf[:, 1024:2048], ccVo.ap()[128:256, kh * 1024:(kh + 1) * 1024]), 16)

                with ExitStack() as past:
                    attnp = past.enter_context(nc.psum_tensor(f"attnp{h}", [128, T], F32))
                    Zp = past.enter_context(nc.psum_tensor(f"Zp{h}", [1, T], F32))
                    exp_marks = {}
                    zpv_marks = {}
                    with ExitStack() as pst:
                        sT = [pst.enter_context(nc.psum_tensor(f"sT{i}_{h}", [128, T], F32)) for i in range(2)]
                        nc.tensor.wait_ge(sv.h, qrope_marks[h])
                        if rzc_mark:
                            nc.tensor.wait_ge(sv.h, rzc_mark)
                        if g2 == 0:
                            nc.tensor.wait_ge(ss.h, kv_load_mark)

                        def emit_zpv(idx):
                            i_, r_ = idx // 2, idx % 2
                            a_ = 128 * i_
                            mem = r_ * 8 + i_
                            nc.tensor.wait_ge(sa.h, exp_marks[idx])
                            if idx == 0:
                                if aevac_mark:
                                    nc.tensor.wait_ge(sv.h, aevac_mark)
                                if ln_mark:
                                    nc.tensor.wait_ge(sa.h, ln_mark)
                            last_ = None
                            for lo, hi in ((a_, 512), (max(a_, 512), 1024)):
                                if lo >= hi:
                                    continue
                                nc.tensor.matmul(Zp[0:1, lo:hi], lhsT=ones_c[:],
                                                 rhs=pt[idx % 2][:, lo - a_:hi - a_],
                                                 start=(idx == 0), stop=(idx == 15))
                                last_ = nc.tensor.matmul(attnp[:, lo:hi],
                                                         lhsT=vbuf[:, mem * 128:(mem + 1) * 128],
                                                         rhs=pt[idx % 2][:, lo - a_:hi - a_],
                                                         start=(idx == 0), stop=(idx == 15))
                            zpv_marks[idx] = sp.tick(last_)

                        for idx in range(16):
                            i, r = idx // 2, idx % 2
                            a = 128 * i
                            par = idx % 2
                            if idx >= 2:
                                nc.tensor.wait_ge(sa.h, exp_marks[idx - 2])
                            last = None
                            for lo, hi in ((a, 512), (max(a, 512), 1024)):
                                if lo >= hi:
                                    continue
                                last = nc.tensor.matmul(sT[par][:, lo:hi],
                                                        lhsT=kbuf[:, r * 1024 + a: r * 1024 + a + 128],
                                                        rhs=qh[cur][:, lo:hi], start=True, stop=True)
                            stm = sp.tick(last)
                            nc.vector.wait_ge(sp.h, stm)
                            mkm = sv.tick(nc.vector.tensor_tensor(
                                sT[par][:, a:a + 128], sT[par][:, a:a + 128],
                                (tri0 if r == 0 else tri1)[:], ALU.add))
                            nc.scalar.wait_ge(sv.h, mkm)
                            if idx >= 2:
                                nc.scalar.wait_ge(sp.h, zpv_marks[idx - 2])
                            exp_marks[idx] = sa.tick(nc.scalar.activation(
                                pt[par][:, 0:T - a], sT[par][:, a:T], AF.Exp, bias=EXPB, scale=SCALE))
                            if idx >= 1:
                                emit_zpv(idx - 1)
                        emit_zpv(15)

                    with ExitStack() as tst:
                        rzbp = tst.enter_context(nc.psum_tensor(f"rzbp{h}", [128, T], F32))
                        Qp = tst.enter_context(nc.psum_tensor(f"Qp{h}", [128, T], F32))
                        hn = h + 1
                        if hn < NQH:
                            nc.tensor.wait_ge(sw.h, wq_mark)
                            nc.tensor.wait_ge(sa.h, exp_marks[15])
                            qm = q_mms(Qp, hn % 4)
                            if hn % 4 == 3 and hn + 1 < NQH:
                                nc.gpsimd.wait_ge(sp.h, qm)
                                wq_mark = load_wq((hn + 1) // 4)
                                wq_loaded_group = (hn + 1) // 4
                        nc.scalar.wait_ge(sp.h, zpv_marks[15])
                        ln_mark = sa.tick(nc.scalar.activation(rzrow[0:1, :], Zp[0:1, :], AF.Ln))
                        nc.scalar.wait_ge(sa.h, ln_mark)
                        rzm = sa.tick(nc.scalar.activation(rz2row[0:1, :], rzrow[0:1, :], AF.Exp, scale=-1.0))
                        nc.tensor.wait_ge(sa.h, rzm)
                        last = None
                        for reg in range(2):
                            last = nc.tensor.matmul(rzbp[:, reg * 512:(reg + 1) * 512], lhsT=ones_r[:],
                                                    rhs=rz2row[0:1, reg * 512:(reg + 1) * 512], start=True, stop=True)
                        rzbm = sp.tick(last)
                        nc.vector.wait_ge(sp.h, rzbm)
                        rzc_mark = sv.tick(nc.vector.tensor_copy(rz_sb[:], rzbp[:, :]))
                        if h == 0:
                            nc.vector.wait_ge(ss.h, spill_mark)
                        aevac_mark = sv.tick(nc.vector.tensor_tensor(
                            A[:, h * T:(h + 1) * T], attnp[:, :], rz_sb[:], ALU.mult))
                        if hn < NQH:
                            nc.scalar.wait_ge(sp.h, qm)
                            rope_evac.t0 = 0
                            am, dm = rope_evac(Qp[:, 0:T], qh[hn % 2][:, 0:T], T, qtmp, qswap)
                            qrope_marks[hn] = dm
                            qrope_amarks[hn] = am

        barrier_all()
        with ExitStack() as ost:
            wbufo = wstream
            xold = [ost.enter_context(nc.sbuf_tensor(f"xold{i}", [128, 4096], BF16)) for i in range(2)]
            with ExitStack() as pst:
                psums = [pst.enter_context(nc.psum_tensor(f"po{o}", [128, 512], F32)) for o in range(4)]
                psum_free = [0, 0, 0, 0]
                xold_marks = [0, 0]
                xold_free = [0, 0]
                og_mark = {}
                nc.sync.wait_ge(ss.h, spill_mark)
                nc.tensor.wait_ge(sv.h, sv.n)
                nc.tensor.wait_ge(sa.h, sa.n)
                for og in range(8):
                    xb = og % 2
                    if og >= 1:
                        nc.gpsimd.wait_ge(sp.h, og_mark[og - 1])
                    if xold_free[xb]:
                        nc.sync.wait_ge(sv.h, xold_free[xb])
                    xold_marks[xb] = ss.tick(nc.sync.dma_start(
                        xold[xb][:], spill[:, og * 4096:(og + 1) * 4096]), 16)
                    for c in range(NCH):
                        sw.tick(nc.gpsimd.dma_start(
                            wbufo[:, c * 512:(c + 1) * 512],
                            WOUT[c * 128:(c + 1) * 128, og * 512:(og + 1) * 512]), 16)
                    wm = sw.n
                    nc.tensor.wait_ge(sw.h, wm)
                    for tc in range(2):
                        for o in range(4):
                            op = og * 4 + o
                            if psum_free[o]:
                                nc.tensor.wait_ge(sv.h, psum_free[o])
                            last = None
                            for c in range(NCH):
                                last = nc.tensor.matmul(
                                    psums[o][:, :],
                                    lhsT=wbufo[:, c * 512 + o * 128: c * 512 + (o + 1) * 128],
                                    rhs=A[:, c * T + tc * 512: c * T + tc * 512 + 512],
                                    start=(c == 0), stop=(c == NCH - 1))
                            gm = sp.tick(last)
                            nc.vector.wait_ge(sp.h, gm)
                            if tc == 0 and o == 0:
                                nc.vector.wait_ge(ss.h, xold_marks[xb])
                            dsl = slice(op * T + tc * 512, op * T + tc * 512 + 512)
                            lastv = nc.vector.tensor_tensor(
                                Bb[:, dsl], psums[o][:, :], xold[xb][:, (o * 2 + tc) * 512:(o * 2 + tc + 1) * 512],
                                ALU.add)
                            psum_free[o] = sv.tick(lastv)
                    xold_free[xb] = psum_free[0]
                    for o in range(4):
                        xold_free[xb] = max(xold_free[xb], psum_free[o])
                    og_mark[og] = sp.n

        if stage == 3:
            nc.sync.wait_ge(sv.h, sv.n)
            ss.tick(nc.sync.dma_start(DBG, Bb[:, :]), 16)
            nc.sync.wait_ge(ss.h, ss.n)
            return nc
        if stage == 34:
            nc.sync.wait_ge(sv.h, sv.n)
            ss.tick(nc.sync.dma_start(DBG, Bb[:, :]), 16)

        ln(Bb, A)
        nc.tensor.wait_ge(sv.h, sv.n)
        if stage == 35:
            nc.sync.wait_ge(sv.h, sv.n)
            ss.tick(nc.sync.dma_start(DBG, A[:, :]), 16)
        if stage == 36:
            ffn(A, Bb, False, "f2")
            nc.sync.wait_ge(sv.h, sv.n)
            ss.tick(nc.sync.dma_start(DBG, Bb[:, :]), 16)
            nc.sync.wait_ge(ss.h, ss.n)
            return nc
        ffn(A, Bb, True, "f2")

        nc.sync.wait_ge(ss.h, ss.n)
    return nc



def _host_tables():
    import jax
    import jax.numpy as jnp
    with jax.default_device(jax.devices("cpu")[0]):
        inv = 1.0 / (10000.0 ** (jnp.arange(0, HD, 2, dtype=jnp.float32) / HD))
        ang = jnp.arange(S, dtype=jnp.float32)[:, None] * inv[None, :]
        cosf = np.asarray(jnp.cos(ang), dtype=np.float32)
        sinf = np.asarray(jnp.sin(ang), dtype=np.float32)
    return cosf, sinf


def kernel(L0_attn_x, L0_attn_w_in, L0_attn_w_out, L0_ffn_base_W, L0_ffn_gate_W,
           L0_ffn_lora_A, L0_ffn_lora_B, _stage=4):
    x = np.asarray(L0_attn_x, np.float32)
    w_in = np.asarray(L0_attn_w_in, np.float32)
    w_out = np.asarray(L0_attn_w_out, np.float32)
    base_W = np.asarray(L0_ffn_base_W, np.float32)
    gate_W = np.asarray(L0_ffn_gate_W, np.float32)
    lora_A = np.asarray(L0_ffn_lora_A, np.float32)
    lora_B = np.asarray(L0_ffn_lora_B, np.float32)

    WIN = np.ascontiguousarray(w_in.T)
    WOUT = np.ascontiguousarray(w_out.T)
    BW = np.ascontiguousarray(base_W.T)
    AGW = np.ascontiguousarray(np.concatenate([lora_A.reshape(224, E).T, gate_W.T], axis=1))
    BRE = np.ascontiguousarray(lora_B.transpose(0, 2, 1).reshape(224, E))
    IDENT = np.eye(128, dtype=ml_dtypes.bfloat16)
    CONSTS = np.zeros((128, 8), np.float32)
    for i, v in enumerate([EXPB, SCALE, -1.0, EPS, -0.5]):
        CONSTS[:, i] = v

    cosf, sinf = _host_tables()
    qgrid, kgrid = np.meshgrid(np.arange(128), np.arange(128), indexing="xy")
    kk = np.arange(128)[:, None]
    qq = np.arange(128)[None, :]
    TRI = np.where(kk <= qq, 0.0, NEG).astype(np.float32)
    ZER = np.zeros((128, 128), np.float32)
    FULLM = np.full((128, 128), NEG, np.float32)

    nc = build_program(stage=_stage)
    in_maps = []
    pos_by_p = {}
    for p in range(2):
        pos = np.concatenate([np.arange(256 * j + 128 * p, 256 * j + 128 * p + 128) for j in range(8)])
        pos_by_p[p] = pos
    tables = {}
    for p in range(2):
        pos = pos_by_p[p]
        cd = np.empty((128, T), np.float32)
        sd = np.empty((128, T), np.float32)
        cd[0:64, :] = cosf[pos, :].T
        cd[64:128, :] = cosf[pos, :].T
        sd[0:64, :] = -sinf[pos, :].T
        sd[64:128, :] = sinf[pos, :].T
        tables[p] = (np.ascontiguousarray(cd), np.ascontiguousarray(sd))

    for c in range(8):
        b, p = c // 2, c % 2
        pos = pos_by_p[p]
        xT = np.ascontiguousarray(x[b][pos, :].T)
        cd, sd = tables[p]
        in_maps.append({
            "XT": xT, "WIN": WIN, "WOUT": WOUT, "BW": BW, "AGW": AGW, "BRE": BRE,
            "COS": cd, "SIN": sd,
            "MK0": TRI if p == 0 else ZER,
            "MK1": FULLM if p == 0 else TRI,
            "CONSTS": CONSTS, "IDENT": IDENT,
        })

    res = run_bass_kernel_spmd(nc, in_maps, core_ids=list(range(8)))

    if _stage != 4:
        return res

    out = np.empty((B, S, E), np.float32)
    for c in range(8):
        b, p = c // 2, c % 2
        o = np.asarray(res.results[c]["OUT"], np.float32)
        for j in range(8):
            out[b, 256 * j + 128 * p: 256 * j + 128 * p + 128, :] = o[:, j * 128:(j + 1) * 128].T
    return out


# version 02
# speedup vs baseline: 1.0833x; 1.0833x over previous
import os
import numpy as np
import ml_dtypes
from contextlib import ExitStack

import concourse.bass as bass
import concourse.mybir as mybir
from concourse.bass_utils import run_bass_kernel_spmd

F32 = mybir.dt.float32
BF16 = mybir.dt.bfloat16
AF = mybir.ActivationFunctionType
ALU = mybir.AluOpType
AX = mybir.AxisListType

B, S, E = 4, 2048, 4096
T = 1024
NCH = 32
NQH, NKV, HD = 32, 8, 128
NE, RANK = 28, 8
NEG = -1e30

SCALE = float(np.float32(1.0) / np.sqrt(np.float32(128.0)))
EXPB = -20.0
EPS = 1e-5
C_INV4096 = float(np.float32(1.0 / 4096.0))


class Sem:
    def __init__(self, nc, stack, name):
        self.h = stack.enter_context(nc.semaphore(name))
        self.n = 0

    def tick(self, ins, k=1):
        ins.then_inc(self.h, k)
        self.n += k
        return self.n


def build_program(stage=4):
    nc = bass.Bass("TRN2", target_bir_lowering=False)

    XT = nc.dram_tensor("XT", [E, T], F32, kind="ExternalInput").ap()
    WIN = nc.dram_tensor("WIN", [E, 6144], F32, kind="ExternalInput").ap()
    WOUT = nc.dram_tensor("WOUT", [E, E], F32, kind="ExternalInput").ap()
    BW = nc.dram_tensor("BW", [E, E], F32, kind="ExternalInput").ap()
    AGW = nc.dram_tensor("AGW", [E, 252], F32, kind="ExternalInput").ap()
    BRE = nc.dram_tensor("BRE", [224, E], F32, kind="ExternalInput").ap()
    COS = nc.dram_tensor("COS", [128, T], F32, kind="ExternalInput").ap()
    SIN = nc.dram_tensor("SIN", [128, T], F32, kind="ExternalInput").ap()
    MK0 = nc.dram_tensor("MK0", [128, 128], F32, kind="ExternalInput").ap()
    MK1 = nc.dram_tensor("MK1", [128, 128], F32, kind="ExternalInput").ap()
    CONSTS = nc.dram_tensor("CONSTS", [128, 8], F32, kind="ExternalInput").ap()
    IDENT = nc.dram_tensor("IDENT", [128, 128], BF16, kind="ExternalInput").ap()
    OUT = nc.dram_tensor("OUT", [E, T], F32, kind="ExternalOutput").ap() if stage >= 4 else None
    DBG = nc.dram_tensor("DBG", [128, NCH * T], BF16, kind="ExternalOutput").ap() if stage != 4 else None

    spill = nc.dram_tensor("spill", [128, NCH * T], BF16, kind="Internal").ap()
    ccKi = nc.dram_tensor("ccKi", [128, 8192], BF16, kind="Internal")
    ccVi = nc.dram_tensor("ccVi", [128, 8192], BF16, kind="Internal")
    ccKo = nc.dram_tensor("ccKo", [256, 8192], BF16, kind="Internal")
    ccVo = nc.dram_tensor("ccVo", [256, 8192], BF16, kind="Internal")

    with ExitStack() as st:
        def sb(name, shape, dt):
            return st.enter_context(nc.sbuf_tensor(name, shape, dt))

        A = sb("A", [128, NCH * T], BF16)
        Bb = sb("Bb", [128, NCH * T], BF16)
        cosq = sb("cosq", [128, T], BF16)
        sinq = sb("sinq", [128, T], BF16)
        tri0 = sb("tri0", [128, 128], F32)
        tri1 = sb("tri1", [128, 128], F32)
        consts = sb("consts", [128, 8], F32)
        ident = sb("ident", [128, 128], BF16)
        ones_c = sb("ones_c", [128, 1], BF16)
        ones_r = sb("ones_r", [1, 128], BF16)
        wstream = sb("wstream", [128, NCH * 512], BF16)

        nc.const_aps.register(F32, EXPB, consts[:, 0:1])
        nc.const_aps.register(F32, SCALE, consts[:, 1:2])
        nc.const_aps.register(F32, -1.0, consts[:, 2:3])
        nc.const_aps.register(F32, EPS, consts[:, 3:4])
        nc.const_aps.register(F32, -0.5, consts[:, 4:5])

        sw = Sem(nc, st, "sw")
        ss = Sem(nc, st, "ss")
        sg = Sem(nc, st, "sg")
        sp = Sem(nc, st, "sp")
        sv = Sem(nc, st, "sv")
        sa = Sem(nc, st, "sa")
        scc = Sem(nc, st, "scc")

        ss.tick(nc.sync.dma_start(consts[:], CONSTS), 16)
        ss.tick(nc.sync.dma_start(tri0[:], MK0), 16)
        ss.tick(nc.sync.dma_start(tri1[:], MK1), 16)
        ss.tick(nc.sync.dma_start(ident[:], IDENT), 16)
        sw.tick(nc.gpsimd.dma_start(cosq[:], COS), 16)
        sw.tick(nc.gpsimd.dma_start(sinq[:], SIN), 16)
        for c in range(NCH):
            sw.tick(nc.gpsimd.dma_start(A[:, c * T:(c + 1) * T], XT[c * 128:(c + 1) * 128, :]), 16)
        sv.tick(nc.vector.memset(ones_c[:], 1.0))
        sv.tick(nc.vector.memset(ones_r[:], 1.0))
        base_loads = dict(sw=sw.n, ss=ss.n, sv=sv.n)

        nc.scalar.wait_ge(ss.h, base_loads["ss"])
        nc.vector.wait_ge(ss.h, base_loads["ss"])
        nc.tensor.wait_ge(sv.h, base_loads["sv"])

        def barrier(*engs):
            for eng in engs:
                eng.wait_ge(sp.h, sp.n)
                eng.wait_ge(sv.h, sv.n)
                eng.wait_ge(sa.h, sa.n)
                eng.wait_ge(sw.h, sw.n)
                eng.wait_ge(ss.h, ss.n)
                eng.wait_ge(sg.h, sg.n)

        def barrier_all():
            barrier(nc.gpsimd, nc.sync, nc.scalar, nc.vector, nc.tensor)


        class _LnCnt:
            n = 0

        def ln(src, dst):
            barrier_all()
            _LnCnt.n += 1
            u = f"_l{_LnCnt.n}"
            with ExitStack() as lst:
                xsq = [lst.enter_context(nc.sbuf_tensor(f"xsq{i}{u}", [128, T], BF16)) for i in range(2)]
                m_row = lst.enter_context(nc.sbuf_tensor("m_row" + u, [1, T], F32))
                v_row = lst.enter_context(nc.sbuf_tensor("v_row" + u, [1, T], F32))
                t_row = lst.enter_context(nc.sbuf_tensor("t_row" + u, [1, T], F32))
                rstd_row = lst.enter_context(nc.sbuf_tensor("rstd_row" + u, [1, T], BF16))
                mr_row = lst.enter_context(nc.sbuf_tensor("mr_row" + u, [1, T], BF16))
                rstd_b = lst.enter_context(nc.sbuf_tensor("rstd_b" + u, [128, T], BF16))
                mrb_b = lst.enter_context(nc.sbuf_tensor("mrb_b" + u, [128, T], BF16))
                with ExitStack() as pst:
                    Sp = pst.enter_context(nc.psum_tensor("Sp" + u, [1, T], F32))
                    Sq = pst.enter_context(nc.psum_tensor("Sq" + u, [1, T], F32))
                    sq_marks = [0, 0]
                    for c in range(NCH):
                        if c >= 2:
                            nc.scalar.wait_ge(sp.h, sq_marks[c % 2])
                        sqm = sa.tick(nc.scalar.activation(xsq[c % 2][:], src[:, c * T:(c + 1) * T], AF.Square))
                        for reg in range(2):
                            nc.tensor.matmul(Sp[0:1, reg * 512:(reg + 1) * 512], lhsT=ones_c[:],
                                             rhs=src[:, c * T + reg * 512: c * T + (reg + 1) * 512],
                                             start=(c == 0), stop=(c == NCH - 1))
                        nc.tensor.wait_ge(sa.h, sqm)
                        last = None
                        for reg in range(2):
                            last = nc.tensor.matmul(Sq[0:1, reg * 512:(reg + 1) * 512], lhsT=ones_c[:],
                                                    rhs=xsq[c % 2][:, reg * 512:(reg + 1) * 512],
                                                    start=(c == 0), stop=(c == NCH - 1))
                        sq_marks[c % 2] = sp.tick(last)
                    stats_done = sp.n
                    nc.vector.wait_ge(sp.h, stats_done)
                    nc.vector.tensor_scalar_mul(m_row[0:1, :], Sp[0:1, :], C_INV4096)
                    nc.vector.tensor_scalar_mul(v_row[0:1, :], Sq[0:1, :], C_INV4096)
                    nc.vector.tensor_tensor(t_row[0:1, :], m_row[0:1, :], m_row[0:1, :], ALU.mult)
                    rows1 = sv.tick(nc.vector.tensor_tensor(v_row[0:1, :], v_row[0:1, :], t_row[0:1, :], ALU.subtract))
                    nc.scalar.wait_ge(sv.h, rows1)
                    lnm = sa.tick(nc.scalar.activation(t_row[0:1, :], v_row[0:1, :], AF.Ln, bias=EPS))
                    nc.scalar.wait_ge(sa.h, lnm)
                    rsm = sa.tick(nc.scalar.activation(rstd_row[0:1, :], t_row[0:1, :], AF.Exp, scale=-0.5))
                    nc.vector.wait_ge(sa.h, rsm)
                    mrm = sv.tick(nc.vector.tensor_tensor(mr_row[0:1, :], m_row[0:1, :], rstd_row[0:1, :], ALU.mult))
                with ExitStack() as pst:
                    bp0 = pst.enter_context(nc.psum_tensor("bp0" + u, [128, T], F32))
                    bp1 = pst.enter_context(nc.psum_tensor("bp1" + u, [128, T], F32))
                    nc.tensor.wait_ge(sv.h, mrm)
                    last = None
                    for reg in range(2):
                        nc.tensor.matmul(bp0[:, reg * 512:(reg + 1) * 512], lhsT=ones_r[:],
                                         rhs=rstd_row[0:1, reg * 512:(reg + 1) * 512], start=True, stop=True)
                        last = nc.tensor.matmul(bp1[:, reg * 512:(reg + 1) * 512], lhsT=ones_r[:],
                                                rhs=mr_row[0:1, reg * 512:(reg + 1) * 512], start=True, stop=True)
                    bm = sp.tick(last)
                    nc.vector.wait_ge(sp.h, bm)
                    nc.vector.tensor_copy(rstd_b[:], bp0[:, :])
                    bcm = sv.tick(nc.vector.tensor_copy(mrb_b[:], bp1[:, :]))
                    if os.environ.get("KDUMPLN", "") == "1" and _LnCnt.n == int(os.environ.get("KDUMPLN_N", "3")) and DBG is not None:
                        lrows = lst.enter_context(nc.sbuf_tensor("lrows" + u, [1, 3072], F32))
                        nc.vector.tensor_copy(lrows[0:1, 0:1024], rstd_row[0:1, :])
                        nc.vector.tensor_copy(lrows[0:1, 1024:2048], m_row[0:1, :])
                        lrm = sv.tick(nc.vector.tensor_copy(lrows[0:1, 2048:3072], v_row[0:1, :]))
                        nc.gpsimd.wait_ge(sv.h, lrm)
                        sw.tick(nc.gpsimd.dma_start(DBG[0:1, 8192:11264], lrows[0:1, :]), 16)
                        sw.tick(nc.gpsimd.dma_start(DBG[:, 11264:12288], rstd_b[:, :]), 16)
                    nc.vector.wait_ge(sv.h, bcm)
                    last = None
                    for c in range(NCH):
                        cs = slice(c * T, (c + 1) * T)
                        nc.vector.tensor_tensor(dst[:, cs], src[:, cs], rstd_b[:], ALU.mult)
                        last = nc.vector.tensor_tensor(dst[:, cs], dst[:, cs], mrb_b[:], ALU.subtract)
                    return sv.tick(last)

        def rope_evac(psum_ap, dst_ap, width, qtmp, qswap):
            t0 = rope_evac.t0
            qm = sa.tick(nc.scalar.activation(qtmp[:, 0:width], psum_ap, AF.Copy))
            nc.vector.wait_ge(sa.h, qm)
            nc.vector.tensor_copy(qswap[0:64, 0:width], qtmp[64:128, 0:width])
            nc.vector.tensor_copy(qswap[64:128, 0:width], qtmp[0:64, 0:width])
            nc.vector.tensor_tensor(qtmp[:, 0:width], qtmp[:, 0:width], cosq[:, t0:t0 + width], ALU.mult)
            nc.vector.tensor_tensor(qswap[:, 0:width], qswap[:, 0:width], sinq[:, t0:t0 + width], ALU.mult)
            dm = sv.tick(nc.vector.tensor_tensor(dst_ap, qtmp[:, 0:width], qswap[:, 0:width], ALU.add))
            return qm, dm
        rope_evac.t0 = 0

        def ffn(src, resid, is_final, tag, parts=3):
            barrier_all()
            with ExitStack() as fst:
                fsb = lambda n_, s, d: fst.enter_context(nc.sbuf_tensor(n_ + tag, s, d))
                bres0 = fsb("bres0", [112, E], BF16)
                bres1 = fsb("bres1", [112, E], BF16)
                hsT0 = fsb("hsT0", [112, T], BF16)
                hsT1 = fsb("hsT1", [112, T], BF16)
                g_sb = fsb("g_sb", [128, NE], F32)
                gx = fsb("gx", [128, 224], F32)
                eq = fsb("eq", [128, NE], F32)
                msk = fsb("msk", [128, NE], F32)
                ge_m = fsb("ge_m", [128, NE], F32)
                en = fsb("en", [128, NE], F32)
                hs = fsb("hs", [128, 224], BF16)
                m1 = fsb("m1", [128, 1], F32)
                negm1 = fsb("negm1", [128, 1], F32)
                m2 = fsb("m2", [128, 1], F32)
                e2 = fsb("e2", [128, 1], F32)
                den = fsb("den", [128, 1], F32)
                rec = fsb("rec", [128, 1], F32)

                if os.environ.get("KNOBRES", "") != "1":
                    sw.tick(nc.gpsimd.dma_start(bres0[:], BRE[0:112, :]), 16)
                    bres_mark = sw.tick(nc.gpsimd.dma_start(bres1[:], BRE[112:224, :]), 16)
                else:
                    bres_mark = sw.n

                if not (parts & 1):
                    hsT_mark = sv.tick(nc.vector.memset(hsT0[:], 0.0))
                    hsT_mark = sv.tick(nc.vector.memset(hsT1[:], 0.0))
                elif True:
                  with ExitStack() as hst:
                    agres = hst.enter_context(nc.sbuf_tensor("agres" + tag, [128, NCH * 252], BF16))
                    for c in range(NCH):
                        sw.tick(nc.gpsimd.dma_start(agres[:, c * 252:(c + 1) * 252],
                                                    AGW[c * 128:(c + 1) * 128, :]), 16)
                    ag_mark = sw.n
                    with ExitStack() as pst:
                        hps = [pst.enter_context(nc.psum_tensor(f"hp{i}" + tag, [128, 252], F32)) for i in range(2)]
                        hT0 = pst.enter_context(nc.psum_tensor("hT0" + tag, [112, 128], BF16))
                        hT1 = pst.enter_context(nc.psum_tensor("hT1" + tag, [112, 128], BF16))
                        nc.tensor.wait_ge(sw.h, ag_mark)
                        hp_frees = [0, 0]
                        hT_free = 0
                        for tt in range(int(os.environ.get("KTT", "8"))):
                            hp = hps[tt % 2]
                            if hp_frees[tt % 2]:
                                nc.tensor.wait_ge(sv.h, hp_frees[tt % 2])
                            last = None
                            for c in range(NCH):
                                last = nc.tensor.matmul(
                                    hp[:, 0:252],
                                    lhsT=src[:, c * T + tt * 128: c * T + (tt + 1) * 128],
                                    rhs=agres[:, c * 252:(c + 1) * 252],
                                    start=(c == 0), stop=(c == NCH - 1))
                            hpm = sp.tick(last)
                            logits = hp[:, 224:252]
                            nc.vector.wait_ge(sp.h, hpm)
                            if ffn.simple_gate:
                                nc.vector.memset(gx[:], 0.001)
                                hsm = sv.tick(nc.vector.tensor_tensor(hs[:], hp[:, 0:224], gx[:], ALU.mult))
                                hp_frees[tt % 2] = hsm
                                if ffn.skip_transpose:
                                    nc.vector.wait_ge(sv.h, hsm)
                                    nc.vector.tensor_copy(hsT0[:, tt * 128:(tt + 1) * 128], hs[0:112, 0:128])
                                    hT_free = sv.tick(nc.vector.tensor_copy(hsT1[:, tt * 128:(tt + 1) * 128], hs[0:112, 96:224]))
                                    continue
                                nc.tensor.wait_ge(sv.h, hsm)
                                if tt > 0:
                                    nc.tensor.wait_ge(sv.h, hT_free)
                                nc.tensor.transpose(hT0[:, :], hs[:, 0:112], ident[:])
                                trm = sp.tick(nc.tensor.transpose(hT1[:, :], hs[:, 112:224], ident[:]))
                                nc.vector.wait_ge(sp.h, trm)
                                nc.vector.tensor_copy(hsT0[:, tt * 128:(tt + 1) * 128], hT0[:, :])
                                hT_free = sv.tick(nc.vector.tensor_copy(hsT1[:, tt * 128:(tt + 1) * 128], hT1[:, :]))
                                continue
                            m1m = sv.tick(nc.vector.tensor_reduce(m1[:], logits, AX.X, ALU.max))
                            nc.vector.wait_ge(sv.h, m1m)
                            nc.vector.tensor_scalar(eq[:], logits, m1[:], None, ALU.is_equal)
                            nc.vector.tensor_scalar_mul(negm1[:], m1[:], -1.0)
                            nc.vector.scalar_tensor_tensor(msk[:], eq[:], NEG, logits, ALU.mult, ALU.add)
                            m2m = sv.tick(nc.vector.tensor_reduce(m2[:], msk[:], AX.X, ALU.max))
                            nc.vector.wait_ge(sv.h, m2m)
                            nm = sv.tick(nc.vector.tensor_scalar(ge_m[:], logits, m2[:], None, ALU.is_ge))
                            nc.scalar.wait_ge(sv.h, nm)
                            enm = sa.tick(nc.scalar.activation(en[:], logits, AF.Exp, bias=negm1[:, 0:1]))
                            e2m = sa.tick(nc.scalar.activation(e2[:], m2[:], AF.Exp, bias=negm1[:, 0:1]))
                            nc.vector.wait_ge(sa.h, e2m)
                            dnm = sv.tick(nc.vector.tensor_scalar_add(den[:], e2[:], 1.0))
                            nc.vector.wait_ge(sv.h, dnm)
                            rcm = sv.tick(nc.vector.reciprocal(rec[:], den[:]))
                            nc.vector.tensor_tensor(g_sb[:], en[:], ge_m[:], ALU.mult)
                            nc.vector.wait_ge(sv.h, rcm)
                            nc.vector.tensor_scalar(g_sb[:], g_sb[:], rec[:], None, ALU.mult)
                            for rr in range(8):
                                nc.vector.tensor_copy(gx[:, rr:224:8], g_sb[:])
                            if os.environ.get("KDUMPG", "") == "1" and tag == "f2" and tt == 0 and DBG is not None:
                                sw.tick(nc.gpsimd.dma_start(DBG[:, 4096:4096 + 252], agres[:, 0:252]), 16)
                                sw.tick(nc.gpsimd.dma_start(DBG[:, 4352:4352 + 1024], src[:, 0:1024]), 16)
                                gdump = hst.enter_context(nc.sbuf_tensor("gdump", [128, 64], F32))
                                nc.vector.tensor_copy(gdump[:, 0:28], logits)
                                nc.vector.tensor_copy(gdump[:, 28:56], g_sb[:])
                                nc.vector.tensor_copy(gdump[:, 56:57], m1[:])
                                nc.vector.tensor_copy(gdump[:, 57:58], m2[:])
                                nc.vector.tensor_copy(gdump[:, 58:59], rec[:])
                                gdm = sv.tick(nc.vector.tensor_copy(gdump[:, 59:60], negm1[:]))
                                nc.gpsimd.wait_ge(sv.h, gdm)
                                sw.tick(nc.gpsimd.dma_start(DBG[:, 2048:2112], gdump[:, :]), 16)
                            hsm = sv.tick(nc.vector.tensor_tensor(hs[:], hp[:, 0:224], gx[:], ALU.mult))
                            hp_frees[tt % 2] = hsm
                            nc.tensor.wait_ge(sv.h, hsm)
                            if tt > 0:
                                nc.tensor.wait_ge(sv.h, hT_free)
                            nc.tensor.transpose(hT0[:, :], hs[:, 0:112], ident[:])
                            trm = sp.tick(nc.tensor.transpose(hT1[:, :], hs[:, 112:224], ident[:]))
                            nc.vector.wait_ge(sp.h, trm)
                            nc.vector.tensor_copy(hsT0[:, tt * 128:(tt + 1) * 128], hT0[:, :])
                            hT_free = sv.tick(nc.vector.tensor_copy(hsT1[:, tt * 128:(tt + 1) * 128], hT1[:, :]))
                        hsT_mark = hT_free
                if os.environ.get("KDUMPHST", "") == "1" and tag == "f2" and DBG is not None:
                    nc.sync.wait_ge(sv.h, sv.n)
                    ss.tick(nc.sync.dma_start(DBG[0:112, 0:1024], hsT0[:, :]), 16)
                    ss.tick(nc.sync.dma_start(DBG[0:112, 1024:2048], hsT1[:, :]), 16)

                if not (parts & 2):
                    return
                with ExitStack() as bst:
                    barrier(nc.gpsimd)
                    ofp = [bst.enter_context(nc.sbuf_tensor("ofp" + tag, [128, 512], F32))] if is_final else []
                    with ExitStack() as pst:
                        psums = [pst.enter_context(nc.psum_tensor(f"pb{o}" + tag, [128, 512], F32)) for o in range(4)]
                        psum_free = [0, 0, 0, 0]
                        ofp_free = [0]
                        oh_mark = {}
                        w_marks = {}
                        wsrc = BW.rearrange("(c q) o -> q c o", q=128)
                        def issue_wdma(oh):
                            p_ = oh % 2
                            sw.tick(nc.gpsimd.dma_start(
                                wstream[:, p_ * 8192:(p_ + 1) * 8192].rearrange("q (c o) -> q c o", o=256),
                                wsrc[:, :, oh * 256:(oh + 1) * 256]), 16)
                            w_marks[oh] = sw.n
                        issue_wdma(0)
                        issue_wdma(1)
                        for oh in range(16):
                            p_ = oh % 2
                            if oh >= 2:
                                pass
                            nc.tensor.wait_ge(sw.h, w_marks[oh])
                            if oh == 0:
                                nc.tensor.wait_ge(sw.h, bres_mark)
                                nc.tensor.wait_ge(sv.h, hsT_mark)
                            for tc in range(2):
                                for o2 in range(2):
                                    o = o2
                                    op = oh * 2 + o2
                                    pi = (oh % 2) * 2 + o2
                                    if psum_free[pi]:
                                        nc.tensor.wait_ge(sv.h, psum_free[pi])
                                    for c in range(NCH):
                                        nc.tensor.matmul(
                                            psums[pi][:, :],
                                            lhsT=wstream[:, p_ * 8192 + c * 256 + o2 * 128: p_ * 8192 + c * 256 + (o2 + 1) * 128],
                                            rhs=src[:, c * T + tc * 512: c * T + tc * 512 + 512],
                                            start=(c == 0), stop=False)
                                    nc.tensor.matmul(psums[pi][:, :], lhsT=bres0[:, op * 128:(op + 1) * 128],
                                                     rhs=hsT0[:, tc * 512:(tc + 1) * 512], start=False, stop=False)
                                    last = nc.tensor.matmul(psums[pi][:, :], lhsT=bres1[:, op * 128:(op + 1) * 128],
                                                            rhs=hsT1[:, tc * 512:(tc + 1) * 512], start=False, stop=True)
                                    gm = sp.tick(last)
                                    nc.vector.wait_ge(sp.h, gm)
                                    dsl = slice(op * T + tc * 512, op * T + tc * 512 + 512)
                                    if not is_final:
                                        psum_free[pi] = sv.tick(nc.vector.scalar_tensor_tensor(
                                            resid[:, dsl], psums[pi][:, :], 0.5, resid[:, dsl], ALU.mult, ALU.add))
                                    else:
                                        if ofp_free[0]:
                                            nc.vector.wait_ge(ss.h, ofp_free[0])
                                        em = sv.tick(nc.vector.scalar_tensor_tensor(
                                            ofp[0][:], psums[pi][:, :], 0.5, resid[:, dsl], ALU.mult, ALU.add))
                                        psum_free[pi] = em
                                        nc.sync.wait_ge(sv.h, em)
                                        ofp_free[0] = ss.tick(nc.sync.dma_start(
                                            OUT[op * 128:(op + 1) * 128, tc * 512:tc * 512 + 512], ofp[0][:]), 16)
                            oh_mark[oh] = sp.n
                            if oh + 2 < 16:
                                nc.gpsimd.wait_ge(sp.h, oh_mark[oh])
                                issue_wdma(oh + 2)


        ffn.store_cnt = 0
        ffn.simple_gate = False
        ffn.skip_transpose = False
        from contextlib import nullcontext
        scope = nc.named_scope if os.environ.get("KSCOPE", "") == "1" else (lambda n_: nullcontext())
        nc.vector.wait_ge(sw.h, base_loads["sw"])
        nc.scalar.wait_ge(sw.h, base_loads["sw"])
        nc.tensor.wait_ge(sw.h, base_loads["sw"])
        if stage == 0:
            nc.sync.wait_ge(sw.h, base_loads["sw"])
            ss.tick(nc.sync.dma_start(DBG, A[:, :]), 16)
            nc.sync.wait_ge(ss.h, ss.n)
            return nc
        if os.environ.get("KNOLN", "") == "1":
            for c in range(NCH):
                sv.tick(nc.vector.tensor_copy(Bb[:, c * T:(c + 1) * T], A[:, c * T:(c + 1) * T]))
        else:
            with scope("ln1"):
                ln(A, Bb)
        nc.tensor.wait_ge(sv.h, sv.n)
        if stage == 1:
            nc.sync.wait_ge(sv.h, sv.n)
            ss.tick(nc.sync.dma_start(DBG, Bb[:, :]), 16)
            nc.sync.wait_ge(ss.h, ss.n)
            return nc

        ffn.simple_gate = (stage in (18, 19))
        ffn.skip_transpose = (stage == 19)
        _parts = {16: 1, 17: 2, 19: 1}.get(stage, 3)
        with scope("ffn1"):
            ffn(Bb, A, False, "f1", parts=_parts)
        ffn.simple_gate = False

        if stage in (15, 16, 17, 18, 19):
            nc.sync.wait_ge(sv.h, sv.n)
            ss.tick(nc.sync.dma_start(DBG, A[:, :]), 16)
            nc.sync.wait_ge(ss.h, ss.n)
            return nc
        if stage == 2:
            nc.sync.wait_ge(sv.h, sv.n)
            fm = ss.tick(nc.sync.dma_start(DBG, A[:, :]), 16)
            nc.sync.wait_ge(ss.h, ss.n)
            return nc

        if stage == 40:
            ln(A, Bb)
            nc.tensor.wait_ge(sv.h, sv.n)
            ffn(Bb, A, False, "f2")
            nc.sync.wait_ge(sv.h, sv.n)
            ss.tick(nc.sync.dma_start(DBG, A[:, :]), 16)
            nc.sync.wait_ge(ss.h, ss.n)
            return nc

        with scope("ln2"):
            ln2_mark = ln(A, Bb)
        nc.tensor.wait_ge(sv.h, sv.n)
        nc.sync.wait_ge(sv.h, sv.n)
        spill_mark = ss.tick(nc.sync.dma_start(spill, A[:, :]), 16)

        barrier_all()
        kvscope = scope("kv"); kvscope.__enter__()
        with ExitStack() as kst:
            wbufk = wstream
            kstg = kst.enter_context(nc.sbuf_tensor("kstg", [128, 4096], BF16))
            vstg = kst.enter_context(nc.sbuf_tensor("vstg", [128, 4096], BF16))
            vtms = kst.enter_context(nc.sbuf_tensor("vtms", [128, 4096], BF16))
            qtmp0 = kst.enter_context(nc.sbuf_tensor("qtmp0", [128, T], BF16))
            qswap0 = kst.enter_context(nc.sbuf_tensor("qswap0", [128, T], BF16))
            with ExitStack() as pst:
                kp = [pst.enter_context(nc.psum_tensor(f"kp{i}", [128, T], F32)) for i in range(2)]
                vtp = [pst.enter_context(nc.psum_tensor(f"vtp{i}", [128, 128], BF16)) for i in range(2)]
                kp_free = [0, 0]
                kp_free_a = [0, 0]
                vtp_free = [0, 0]
                kg_mark = 0
                for kg in range(4):
                    cols0 = 4096 + kg * 512
                    is_v = kg >= 2
                    if kg >= 1:
                        nc.gpsimd.wait_ge(sp.h, kg_mark)
                        nc.vector.wait_ge(sg.h, sg.n)
                        nc.scalar.wait_ge(sg.h, sg.n)
                    for c in range(NCH):
                        sw.tick(nc.gpsimd.dma_start(
                            wbufk[:, c * 512:(c + 1) * 512],
                            WIN[c * 128:(c + 1) * 128, cols0:cols0 + 512]), 16)
                    wkm = sw.n
                    nc.tensor.wait_ge(sw.h, wkm)
                    stg_store = 0
                    for h4 in range(4):
                        pb = h4 % 2
                        if kp_free[pb]:
                            nc.tensor.wait_ge(sv.h, kp_free[pb])
                            nc.tensor.wait_ge(sa.h, kp_free_a[pb])
                        last = None
                        for tc in range(2):
                            for c in range(NCH):
                                last = nc.tensor.matmul(
                                    kp[pb][:, tc * 512:(tc + 1) * 512],
                                    lhsT=wbufk[:, c * 512 + h4 * 128: c * 512 + (h4 + 1) * 128],
                                    rhs=Bb[:, c * T + tc * 512: c * T + tc * 512 + 512],
                                    start=(c == 0), stop=(c == NCH - 1))
                        km = sp.tick(last)
                        if not is_v:
                            nc.scalar.wait_ge(sp.h, km)
                            rope_evac.t0 = 0
                            am, dm = rope_evac(kp[pb][:, 0:T], kstg[:, h4 * T:(h4 + 1) * T], T, qtmp0, qswap0)
                            kp_free[pb] = dm
                            kp_free_a[pb] = am
                        else:
                            nc.scalar.wait_ge(sp.h, km)
                            vm = sa.tick(nc.scalar.activation(vstg[:, h4 * T:(h4 + 1) * T], kp[pb][:, 0:T], AF.Copy))
                            kp_free[pb] = 0
                            kp_free_a[pb] = vm
                            nc.tensor.wait_ge(sa.h, vm)
                            for j in range(8):
                                tb = j % 2
                                if vtp_free[tb]:
                                    nc.tensor.wait_ge(sv.h, vtp_free[tb])
                                trm = sp.tick(nc.tensor.transpose(
                                    vtp[tb][:, :], vstg[:, h4 * T + j * 128: h4 * T + (j + 1) * 128], ident[:]))
                                nc.vector.wait_ge(sp.h, trm)
                                vtp_free[tb] = sv.tick(nc.vector.tensor_copy(
                                    vtms[:, h4 * T + j * 128: h4 * T + (j + 1) * 128], vtp[tb][:, :]))
                    kg_mark = sp.n
                    if not is_v:
                        nc.scalar.wait_ge(sv.h, sv.n)
                        sg.tick(nc.scalar.dma_start(ccKi.ap()[:, kg * 4096:(kg + 1) * 4096], kstg[:]), 16)
                    else:
                        nc.scalar.wait_ge(sv.h, sv.n)
                        sg.tick(nc.scalar.dma_start(ccVi.ap()[:, (kg - 2) * 4096:(kg - 1) * 4096], vtms[:]), 16)

        kvscope.__exit__(None, None, None)
        nc.gpsimd.wait_ge(sg.h, sg.n)
        groups = [[0, 1], [2, 3], [4, 5], [6, 7]]
        scc.tick(nc.gpsimd.collective_compute(
            "AllGather", ALU.bypass, replica_groups=groups,
            ins=[ccKi.ap().opt()], outs=[ccKo.ap().opt()]), 1)
        scc.tick(nc.gpsimd.collective_compute(
            "AllGather", ALU.bypass, replica_groups=groups,
            ins=[ccVi.ap().opt()], outs=[ccVo.ap().opt()]), 1)

        barrier_all()
        atscope = scope("attn"); atscope.__enter__()
        with ExitStack() as ast:
            asb = lambda n_, s, d: ast.enter_context(nc.sbuf_tensor(n_, s, d))
            kbuf = asb("kbuf", [128, 2048], BF16)
            vbuf = asb("vbuf", [128, 2048], BF16)
            wq = wstream
            qh = [asb(f"qh{i}", [128, T], BF16) for i in range(2)]
            pt = [asb(f"pt{i}", [128, T], BF16) for i in range(2)]
            qtmp = asb("qtmp", [128, T], BF16)
            qswap = asb("qswap", [128, T], BF16)
            rz_sb = asb("rz_sb", [128, T], F32)
            rzrow = asb("rzrow", [1, T], F32)
            rz2row = asb("rz2row", [1, T], BF16)

            def load_wq(g):
                for c in range(NCH):
                    sw.tick(nc.gpsimd.dma_start(
                        wq[:, c * 512:(c + 1) * 512],
                        WIN[c * 128:(c + 1) * 128, g * 512:(g + 1) * 512]), 16)
                return sw.n

            def q_mms(Qp, g2):
                last = None
                for tc in range(2):
                    for c in range(NCH):
                        last = nc.tensor.matmul(
                            Qp[:, tc * 512:(tc + 1) * 512],
                            lhsT=wq[:, c * 512 + g2 * 128: c * 512 + (g2 + 1) * 128],
                            rhs=Bb[:, c * T + tc * 512: c * T + tc * 512 + 512],
                            start=(c == 0), stop=(c == NCH - 1))
                return sp.tick(last)

            wq_mark = load_wq(0)
            kv_load_mark = 0
            qrope_marks = {}
            qrope_amarks = {}
            with ExitStack() as pst0:
                Qp0 = pst0.enter_context(nc.psum_tensor("Qpre", [128, T], F32))
                nc.tensor.wait_ge(sw.h, wq_mark)
                nc.tensor.wait_ge(sv.h, sv.n)
                nc.tensor.wait_ge(sa.h, sa.n)
                q0m = q_mms(Qp0, 0)
                nc.scalar.wait_ge(sp.h, q0m)
                rope_evac.t0 = 0
                am, dm = rope_evac(Qp0[:, 0:T], qh[0][:, 0:T], T, qtmp, qswap)
                qrope_marks[0] = dm
                qrope_amarks[0] = am

            aevac_mark = 0
            rzc_mark = 0
            ln_mark = 0
            wq_loaded_group = 0
            for h in range(NQH):
                kh, g2 = h // 4, h % 4
                cur = h % 2
                if g2 == 0:
                    nc.sync.wait_ge(scc.h, 2)
                    nc.sync.wait_ge(sp.h, sp.n)
                    ss.tick(nc.sync.dma_start(kbuf[:, 0:1024], ccKo.ap()[0:128, kh * 1024:(kh + 1) * 1024]), 16)
                    ss.tick(nc.sync.dma_start(kbuf[:, 1024:2048], ccKo.ap()[128:256, kh * 1024:(kh + 1) * 1024]), 16)
                    ss.tick(nc.sync.dma_start(vbuf[:, 0:1024], ccVo.ap()[0:128, kh * 1024:(kh + 1) * 1024]), 16)
                    kv_load_mark = ss.tick(nc.sync.dma_start(vbuf[:, 1024:2048], ccVo.ap()[128:256, kh * 1024:(kh + 1) * 1024]), 16)

                with ExitStack() as past:
                    attnp = past.enter_context(nc.psum_tensor(f"attnp{h}", [128, T], F32))
                    Zp = past.enter_context(nc.psum_tensor(f"Zp{h}", [1, T], F32))
                    exp_marks = {}
                    zpv_marks = {}
                    with ExitStack() as pst:
                        sT = [pst.enter_context(nc.psum_tensor(f"sT{i}_{h}", [128, T], F32)) for i in range(2)]
                        nc.tensor.wait_ge(sv.h, qrope_marks[h])
                        if rzc_mark:
                            nc.tensor.wait_ge(sv.h, rzc_mark)
                        if g2 == 0:
                            nc.tensor.wait_ge(ss.h, kv_load_mark)

                        def emit_zpv(idx):
                            i_, r_ = idx // 2, idx % 2
                            a_ = 128 * i_
                            mem = r_ * 8 + i_
                            nc.tensor.wait_ge(sa.h, exp_marks[idx])
                            if idx == 0:
                                if aevac_mark:
                                    nc.tensor.wait_ge(sv.h, aevac_mark)
                                if ln_mark:
                                    nc.tensor.wait_ge(sa.h, ln_mark)
                            last_ = None
                            for lo, hi in ((a_, 512), (max(a_, 512), 1024)):
                                if lo >= hi:
                                    continue
                                nc.tensor.matmul(Zp[0:1, lo:hi], lhsT=ones_c[:],
                                                 rhs=pt[idx % 2][:, lo - a_:hi - a_],
                                                 start=(idx == 0), stop=(idx == 15))
                                last_ = nc.tensor.matmul(attnp[:, lo:hi],
                                                         lhsT=vbuf[:, mem * 128:(mem + 1) * 128],
                                                         rhs=pt[idx % 2][:, lo - a_:hi - a_],
                                                         start=(idx == 0), stop=(idx == 15))
                            zpv_marks[idx] = sp.tick(last_)

                        for idx in range(16):
                            i, r = idx // 2, idx % 2
                            a = 128 * i
                            par = idx % 2
                            if idx >= 2:
                                nc.tensor.wait_ge(sa.h, exp_marks[idx - 2])
                            last = None
                            for lo, hi in ((a, 512), (max(a, 512), 1024)):
                                if lo >= hi:
                                    continue
                                last = nc.tensor.matmul(sT[par][:, lo:hi],
                                                        lhsT=kbuf[:, r * 1024 + a: r * 1024 + a + 128],
                                                        rhs=qh[cur][:, lo:hi], start=True, stop=True)
                            stm = sp.tick(last)
                            nc.vector.wait_ge(sp.h, stm)
                            mkm = sv.tick(nc.vector.tensor_tensor(
                                sT[par][:, a:a + 128], sT[par][:, a:a + 128],
                                (tri0 if r == 0 else tri1)[:], ALU.add))
                            nc.scalar.wait_ge(sv.h, mkm)
                            if idx >= 2:
                                nc.scalar.wait_ge(sp.h, zpv_marks[idx - 2])
                            exp_marks[idx] = sa.tick(nc.scalar.activation(
                                pt[par][:, 0:T - a], sT[par][:, a:T], AF.Exp, bias=EXPB, scale=SCALE))
                            if idx >= 1:
                                emit_zpv(idx - 1)
                        emit_zpv(15)

                    with ExitStack() as tst:
                        rzbp = tst.enter_context(nc.psum_tensor(f"rzbp{h}", [128, T], F32))
                        Qp = tst.enter_context(nc.psum_tensor(f"Qp{h}", [128, T], F32))
                        hn = h + 1
                        if hn < NQH:
                            nc.tensor.wait_ge(sw.h, wq_mark)
                            nc.tensor.wait_ge(sa.h, exp_marks[15])
                            qm = q_mms(Qp, hn % 4)
                            if hn % 4 == 3 and hn + 1 < NQH:
                                nc.gpsimd.wait_ge(sp.h, qm)
                                wq_mark = load_wq((hn + 1) // 4)
                                wq_loaded_group = (hn + 1) // 4
                        nc.scalar.wait_ge(sp.h, zpv_marks[15])
                        ln_mark = sa.tick(nc.scalar.activation(rzrow[0:1, :], Zp[0:1, :], AF.Ln))
                        nc.scalar.wait_ge(sa.h, ln_mark)
                        rzm = sa.tick(nc.scalar.activation(rz2row[0:1, :], rzrow[0:1, :], AF.Exp, scale=-1.0))
                        nc.tensor.wait_ge(sa.h, rzm)
                        last = None
                        for reg in range(2):
                            last = nc.tensor.matmul(rzbp[:, reg * 512:(reg + 1) * 512], lhsT=ones_r[:],
                                                    rhs=rz2row[0:1, reg * 512:(reg + 1) * 512], start=True, stop=True)
                        rzbm = sp.tick(last)
                        nc.vector.wait_ge(sp.h, rzbm)
                        rzc_mark = sv.tick(nc.vector.tensor_copy(rz_sb[:], rzbp[:, :]))
                        if h == 0:
                            nc.vector.wait_ge(ss.h, spill_mark)
                        aevac_mark = sv.tick(nc.vector.tensor_tensor(
                            A[:, h * T:(h + 1) * T], attnp[:, :], rz_sb[:], ALU.mult))
                        if hn < NQH:
                            nc.scalar.wait_ge(sp.h, qm)
                            rope_evac.t0 = 0
                            am, dm = rope_evac(Qp[:, 0:T], qh[hn % 2][:, 0:T], T, qtmp, qswap)
                            qrope_marks[hn] = dm
                            qrope_amarks[hn] = am

        atscope.__exit__(None, None, None)
        barrier_all()
        opscope = scope("outproj"); opscope.__enter__()
        with ExitStack() as ost:
            wbufo = wstream
            xold = [ost.enter_context(nc.sbuf_tensor(f"xold{i}", [128, 4096], BF16)) for i in range(2)]
            with ExitStack() as pst:
                psums = [pst.enter_context(nc.psum_tensor(f"po{o}", [128, 512], F32)) for o in range(4)]
                psum_free = [0, 0, 0, 0]
                xold_marks = [0, 0]
                xold_free = [0, 0]
                og_mark = {}
                nc.sync.wait_ge(ss.h, spill_mark)
                nc.tensor.wait_ge(sv.h, sv.n)
                nc.tensor.wait_ge(sa.h, sa.n)
                for og in range(8):
                    xb = og % 2
                    if og >= 1:
                        nc.gpsimd.wait_ge(sp.h, og_mark[og - 1])
                    if xold_free[xb]:
                        nc.sync.wait_ge(sv.h, xold_free[xb])
                    xold_marks[xb] = ss.tick(nc.sync.dma_start(
                        xold[xb][:], spill[:, og * 4096:(og + 1) * 4096]), 16)
                    for c in range(NCH):
                        sw.tick(nc.gpsimd.dma_start(
                            wbufo[:, c * 512:(c + 1) * 512],
                            WOUT[c * 128:(c + 1) * 128, og * 512:(og + 1) * 512]), 16)
                    wm = sw.n
                    nc.tensor.wait_ge(sw.h, wm)
                    for tc in range(2):
                        for o in range(4):
                            op = og * 4 + o
                            if psum_free[o]:
                                nc.tensor.wait_ge(sv.h, psum_free[o])
                            last = None
                            for c in range(NCH):
                                last = nc.tensor.matmul(
                                    psums[o][:, :],
                                    lhsT=wbufo[:, c * 512 + o * 128: c * 512 + (o + 1) * 128],
                                    rhs=A[:, c * T + tc * 512: c * T + tc * 512 + 512],
                                    start=(c == 0), stop=(c == NCH - 1))
                            gm = sp.tick(last)
                            nc.vector.wait_ge(sp.h, gm)
                            if tc == 0 and o == 0:
                                nc.vector.wait_ge(ss.h, xold_marks[xb])
                            dsl = slice(op * T + tc * 512, op * T + tc * 512 + 512)
                            lastv = nc.vector.tensor_tensor(
                                Bb[:, dsl], psums[o][:, :], xold[xb][:, (o * 2 + tc) * 512:(o * 2 + tc + 1) * 512],
                                ALU.add)
                            psum_free[o] = sv.tick(lastv)
                    xold_free[xb] = psum_free[0]
                    for o in range(4):
                        xold_free[xb] = max(xold_free[xb], psum_free[o])
                    og_mark[og] = sp.n

        if stage == 3:
            nc.sync.wait_ge(sv.h, sv.n)
            ss.tick(nc.sync.dma_start(DBG, Bb[:, :]), 16)
            nc.sync.wait_ge(ss.h, ss.n)
            return nc
        if stage == 34:
            nc.sync.wait_ge(sv.h, sv.n)
            ss.tick(nc.sync.dma_start(DBG, Bb[:, :]), 16)

        opscope.__exit__(None, None, None)
        with scope("ln3"):
            ln(Bb, A)
        nc.tensor.wait_ge(sv.h, sv.n)
        if stage == 35:
            nc.sync.wait_ge(sv.h, sv.n)
            ss.tick(nc.sync.dma_start(DBG, A[:, :]), 16)
        if stage == 36:
            ffn(A, Bb, False, "f2")
            nc.sync.wait_ge(sv.h, sv.n)
            ss.tick(nc.sync.dma_start(DBG, Bb[:, :]), 16)
            nc.sync.wait_ge(ss.h, ss.n)
            return nc
        with scope("ffn2"):
            ffn(A, Bb, True, "f2")

        nc.sync.wait_ge(ss.h, ss.n)
    return nc



def _host_tables():
    import jax
    import jax.numpy as jnp
    with jax.default_device(jax.devices("cpu")[0]):
        inv = 1.0 / (10000.0 ** (jnp.arange(0, HD, 2, dtype=jnp.float32) / HD))
        ang = jnp.arange(S, dtype=jnp.float32)[:, None] * inv[None, :]
        cosf = np.asarray(jnp.cos(ang), dtype=np.float32)
        sinf = np.asarray(jnp.sin(ang), dtype=np.float32)
    return cosf, sinf


def kernel(L0_attn_x, L0_attn_w_in, L0_attn_w_out, L0_ffn_base_W, L0_ffn_gate_W,
           L0_ffn_lora_A, L0_ffn_lora_B, _stage=4):
    x = np.asarray(L0_attn_x, np.float32)
    w_in = np.asarray(L0_attn_w_in, np.float32)
    w_out = np.asarray(L0_attn_w_out, np.float32)
    base_W = np.asarray(L0_ffn_base_W, np.float32)
    gate_W = np.asarray(L0_ffn_gate_W, np.float32)
    lora_A = np.asarray(L0_ffn_lora_A, np.float32)
    lora_B = np.asarray(L0_ffn_lora_B, np.float32)

    WIN = np.ascontiguousarray(w_in.T)
    WOUT = np.ascontiguousarray(w_out.T)
    BW = np.ascontiguousarray(base_W.T)
    AGW = np.ascontiguousarray(np.concatenate([lora_A.reshape(224, E).T, gate_W.T], axis=1))
    BRE = np.ascontiguousarray(lora_B.transpose(0, 2, 1).reshape(224, E))
    IDENT = np.eye(128, dtype=ml_dtypes.bfloat16)
    CONSTS = np.zeros((128, 8), np.float32)
    for i, v in enumerate([EXPB, SCALE, -1.0, EPS, -0.5]):
        CONSTS[:, i] = v

    cosf, sinf = _host_tables()
    qgrid, kgrid = np.meshgrid(np.arange(128), np.arange(128), indexing="xy")
    kk = np.arange(128)[:, None]
    qq = np.arange(128)[None, :]
    TRI = np.where(kk <= qq, 0.0, NEG).astype(np.float32)
    ZER = np.zeros((128, 128), np.float32)
    FULLM = np.full((128, 128), NEG, np.float32)

    nc = build_program(stage=_stage)
    in_maps = []
    pos_by_p = {}
    for p in range(2):
        pos = np.concatenate([np.arange(256 * j + 128 * p, 256 * j + 128 * p + 128) for j in range(8)])
        pos_by_p[p] = pos
    tables = {}
    for p in range(2):
        pos = pos_by_p[p]
        cd = np.empty((128, T), np.float32)
        sd = np.empty((128, T), np.float32)
        cd[0:64, :] = cosf[pos, :].T
        cd[64:128, :] = cosf[pos, :].T
        sd[0:64, :] = -sinf[pos, :].T
        sd[64:128, :] = sinf[pos, :].T
        tables[p] = (np.ascontiguousarray(cd), np.ascontiguousarray(sd))

    for c in range(8):
        b, p = c // 2, c % 2
        pos = pos_by_p[p]
        xT = np.ascontiguousarray(x[b][pos, :].T)
        cd, sd = tables[p]
        in_maps.append({
            "XT": xT, "WIN": WIN, "WOUT": WOUT, "BW": BW, "AGW": AGW, "BRE": BRE,
            "COS": cd, "SIN": sd,
            "MK0": TRI if p == 0 else ZER,
            "MK1": FULLM if p == 0 else TRI,
            "CONSTS": CONSTS, "IDENT": IDENT,
        })

    res = run_bass_kernel_spmd(nc, in_maps, core_ids=list(range(8)))

    if _stage != 4:
        return res

    out = np.empty((B, S, E), np.float32)
    for c in range(8):
        b, p = c // 2, c % 2
        o = np.asarray(res.results[c]["OUT"], np.float32)
        for j in range(8):
            out[b, 256 * j + 128 * p: 256 * j + 128 * p + 128, :] = o[:, j * 128:(j + 1) * 128].T
    return out


# version 03
# speedup vs baseline: 1.1759x; 1.0854x over previous
import os
import numpy as np
import ml_dtypes
from contextlib import ExitStack

import concourse.bass as bass
import concourse.mybir as mybir
from concourse.bass_utils import run_bass_kernel_spmd

F32 = mybir.dt.float32
BF16 = mybir.dt.bfloat16
AF = mybir.ActivationFunctionType
ALU = mybir.AluOpType
AX = mybir.AxisListType

B, S, E = 4, 2048, 4096
T = 1024
NCH = 32
NQH, NKV, HD = 32, 8, 128
NE, RANK = 28, 8
NEG = -1e30

SCALE = float(np.float32(1.0) / np.sqrt(np.float32(128.0)))
EXPB = -20.0
EPS = 1e-5
C_INV4096 = float(np.float32(1.0 / 4096.0))


class Sem:
    def __init__(self, nc, stack, name):
        self.h = stack.enter_context(nc.semaphore(name))
        self.n = 0

    def tick(self, ins, k=1):
        ins.then_inc(self.h, k)
        self.n += k
        return self.n


def build_program(stage=4):
    nc = bass.Bass("TRN2", target_bir_lowering=False)

    XT = nc.dram_tensor("XT", [E, T], F32, kind="ExternalInput").ap()
    WIN = nc.dram_tensor("WIN", [E, 6144], F32, kind="ExternalInput").ap()
    WOUT = nc.dram_tensor("WOUT", [E, E], F32, kind="ExternalInput").ap()
    BW = nc.dram_tensor("BW", [E, E], F32, kind="ExternalInput").ap()
    AGW = nc.dram_tensor("AGW", [E, 252], F32, kind="ExternalInput").ap()
    BRE = nc.dram_tensor("BRE", [224, E], F32, kind="ExternalInput").ap()
    COS = nc.dram_tensor("COS", [128, T], F32, kind="ExternalInput").ap()
    SIN = nc.dram_tensor("SIN", [128, T], F32, kind="ExternalInput").ap()
    MK0 = nc.dram_tensor("MK0", [128, 128], F32, kind="ExternalInput").ap()
    MK1 = nc.dram_tensor("MK1", [128, 128], F32, kind="ExternalInput").ap()
    CONSTS = nc.dram_tensor("CONSTS", [128, 8], F32, kind="ExternalInput").ap()
    IDENT = nc.dram_tensor("IDENT", [128, 128], BF16, kind="ExternalInput").ap()
    OUT = nc.dram_tensor("OUT", [E, T], F32, kind="ExternalOutput").ap() if stage >= 4 else None
    DBG = nc.dram_tensor("DBG", [128, NCH * T], BF16, kind="ExternalOutput").ap() if stage != 4 else None

    spill = nc.dram_tensor("spill", [128, NCH * T], BF16, kind="Internal").ap()
    ccKi = nc.dram_tensor("ccKi", [128, 8192], BF16, kind="Internal")
    ccVi = nc.dram_tensor("ccVi", [128, 8192], BF16, kind="Internal")
    ccKo = nc.dram_tensor("ccKo", [256, 8192], BF16, kind="Internal")
    ccVo = nc.dram_tensor("ccVo", [256, 8192], BF16, kind="Internal")

    with ExitStack() as st:
        def sb(name, shape, dt):
            return st.enter_context(nc.sbuf_tensor(name, shape, dt))

        A = sb("A", [128, NCH * T], BF16)
        Bb = sb("Bb", [128, NCH * T], BF16)
        cosq = sb("cosq", [128, T], BF16)
        sinq = sb("sinq", [128, T], BF16)
        tri0 = sb("tri0", [128, 128], F32)
        tri1 = sb("tri1", [128, 128], F32)
        consts = sb("consts", [128, 8], F32)
        ident = sb("ident", [128, 128], BF16)
        ones_c = sb("ones_c", [128, 1], BF16)
        ones_r = sb("ones_r", [1, 128], BF16)
        wstream = sb("wstream", [128, NCH * 512], BF16)

        nc.const_aps.register(F32, EXPB, consts[:, 0:1])
        nc.const_aps.register(F32, SCALE, consts[:, 1:2])
        nc.const_aps.register(F32, -1.0, consts[:, 2:3])
        nc.const_aps.register(F32, EPS, consts[:, 3:4])
        nc.const_aps.register(F32, -0.5, consts[:, 4:5])

        sw = Sem(nc, st, "sw")
        ss = Sem(nc, st, "ss")
        sg = Sem(nc, st, "sg")
        sp = Sem(nc, st, "sp")
        sv = Sem(nc, st, "sv")
        sa = Sem(nc, st, "sa")
        scc = Sem(nc, st, "scc")

        ss.tick(nc.sync.dma_start(consts[:], CONSTS), 16)
        ss.tick(nc.sync.dma_start(tri0[:], MK0), 16)
        ss.tick(nc.sync.dma_start(tri1[:], MK1), 16)
        ss.tick(nc.sync.dma_start(ident[:], IDENT), 16)
        sw.tick(nc.gpsimd.dma_start(cosq[:], COS), 16)
        sw.tick(nc.gpsimd.dma_start(sinq[:], SIN), 16)
        for c in range(NCH):
            sw.tick(nc.gpsimd.dma_start(A[:, c * T:(c + 1) * T], XT[c * 128:(c + 1) * 128, :]), 16)
        sv.tick(nc.vector.memset(ones_c[:], 1.0))
        sv.tick(nc.vector.memset(ones_r[:], 1.0))
        base_loads = dict(sw=sw.n, ss=ss.n, sv=sv.n)

        nc.scalar.wait_ge(ss.h, base_loads["ss"])
        nc.vector.wait_ge(ss.h, base_loads["ss"])
        nc.tensor.wait_ge(sv.h, base_loads["sv"])

        def barrier(*engs):
            for eng in engs:
                eng.wait_ge(sp.h, sp.n)
                eng.wait_ge(sv.h, sv.n)
                eng.wait_ge(sa.h, sa.n)
                eng.wait_ge(sw.h, sw.n)
                eng.wait_ge(ss.h, ss.n)
                eng.wait_ge(sg.h, sg.n)

        def barrier_all():
            barrier(nc.gpsimd, nc.sync, nc.scalar, nc.vector, nc.tensor)


        class _LnCnt:
            n = 0

        def ln(src, dst):
            barrier_all()
            _LnCnt.n += 1
            u = f"_l{_LnCnt.n}"
            with ExitStack() as lst:
                xsq = [lst.enter_context(nc.sbuf_tensor(f"xsq{i}{u}", [128, T], BF16)) for i in range(2)]
                m_row = lst.enter_context(nc.sbuf_tensor("m_row" + u, [1, T], F32))
                v_row = lst.enter_context(nc.sbuf_tensor("v_row" + u, [1, T], F32))
                t_row = lst.enter_context(nc.sbuf_tensor("t_row" + u, [1, T], F32))
                rstd_row = lst.enter_context(nc.sbuf_tensor("rstd_row" + u, [1, T], BF16))
                mr_row = lst.enter_context(nc.sbuf_tensor("mr_row" + u, [1, T], BF16))
                rstd_b = lst.enter_context(nc.sbuf_tensor("rstd_b" + u, [128, T], BF16))
                mrb_b = lst.enter_context(nc.sbuf_tensor("mrb_b" + u, [128, T], BF16))
                with ExitStack() as pst:
                    Sp = pst.enter_context(nc.psum_tensor("Sp" + u, [1, T], F32))
                    Sq = pst.enter_context(nc.psum_tensor("Sq" + u, [1, T], F32))
                    sq_marks = [0, 0]
                    for c in range(NCH):
                        if c >= 2:
                            nc.scalar.wait_ge(sp.h, sq_marks[c % 2])
                        sqm = sa.tick(nc.scalar.activation(xsq[c % 2][:], src[:, c * T:(c + 1) * T], AF.Square))
                        for reg in range(2):
                            nc.tensor.matmul(Sp[0:1, reg * 512:(reg + 1) * 512], lhsT=ones_c[:],
                                             rhs=src[:, c * T + reg * 512: c * T + (reg + 1) * 512],
                                             start=(c == 0), stop=(c == NCH - 1))
                        nc.tensor.wait_ge(sa.h, sqm)
                        last = None
                        for reg in range(2):
                            last = nc.tensor.matmul(Sq[0:1, reg * 512:(reg + 1) * 512], lhsT=ones_c[:],
                                                    rhs=xsq[c % 2][:, reg * 512:(reg + 1) * 512],
                                                    start=(c == 0), stop=(c == NCH - 1))
                        sq_marks[c % 2] = sp.tick(last)
                    stats_done = sp.n
                    nc.vector.wait_ge(sp.h, stats_done)
                    nc.vector.tensor_scalar_mul(m_row[0:1, :], Sp[0:1, :], C_INV4096)
                    nc.vector.tensor_scalar_mul(v_row[0:1, :], Sq[0:1, :], C_INV4096)
                    nc.vector.tensor_tensor(t_row[0:1, :], m_row[0:1, :], m_row[0:1, :], ALU.mult)
                    rows1 = sv.tick(nc.vector.tensor_tensor(v_row[0:1, :], v_row[0:1, :], t_row[0:1, :], ALU.subtract))
                    nc.scalar.wait_ge(sv.h, rows1)
                    lnm = sa.tick(nc.scalar.activation(t_row[0:1, :], v_row[0:1, :], AF.Ln, bias=EPS))
                    nc.scalar.wait_ge(sa.h, lnm)
                    rsm = sa.tick(nc.scalar.activation(rstd_row[0:1, :], t_row[0:1, :], AF.Exp, scale=-0.5))
                    nc.vector.wait_ge(sa.h, rsm)
                    mrm = sv.tick(nc.vector.tensor_tensor(mr_row[0:1, :], m_row[0:1, :], rstd_row[0:1, :], ALU.mult))
                with ExitStack() as pst:
                    bp0 = pst.enter_context(nc.psum_tensor("bp0" + u, [128, T], F32))
                    bp1 = pst.enter_context(nc.psum_tensor("bp1" + u, [128, T], F32))
                    nc.tensor.wait_ge(sv.h, mrm)
                    last = None
                    for reg in range(2):
                        nc.tensor.matmul(bp0[:, reg * 512:(reg + 1) * 512], lhsT=ones_r[:],
                                         rhs=rstd_row[0:1, reg * 512:(reg + 1) * 512], start=True, stop=True)
                        last = nc.tensor.matmul(bp1[:, reg * 512:(reg + 1) * 512], lhsT=ones_r[:],
                                                rhs=mr_row[0:1, reg * 512:(reg + 1) * 512], start=True, stop=True)
                    bm = sp.tick(last)
                    nc.vector.wait_ge(sp.h, bm)
                    nc.vector.tensor_copy(rstd_b[:], bp0[:, :])
                    bcm = sv.tick(nc.vector.tensor_copy(mrb_b[:], bp1[:, :]))
                    if os.environ.get("KDUMPLN", "") == "1" and _LnCnt.n == int(os.environ.get("KDUMPLN_N", "3")) and DBG is not None:
                        lrows = lst.enter_context(nc.sbuf_tensor("lrows" + u, [1, 3072], F32))
                        nc.vector.tensor_copy(lrows[0:1, 0:1024], rstd_row[0:1, :])
                        nc.vector.tensor_copy(lrows[0:1, 1024:2048], m_row[0:1, :])
                        lrm = sv.tick(nc.vector.tensor_copy(lrows[0:1, 2048:3072], v_row[0:1, :]))
                        nc.gpsimd.wait_ge(sv.h, lrm)
                        sw.tick(nc.gpsimd.dma_start(DBG[0:1, 8192:11264], lrows[0:1, :]), 16)
                        sw.tick(nc.gpsimd.dma_start(DBG[:, 11264:12288], rstd_b[:, :]), 16)
                    nc.vector.wait_ge(sv.h, bcm)
                    last = None
                    for c in range(NCH):
                        cs = slice(c * T, (c + 1) * T)
                        nc.vector.tensor_tensor(dst[:, cs], src[:, cs], rstd_b[:], ALU.mult)
                        last = nc.vector.tensor_tensor(dst[:, cs], dst[:, cs], mrb_b[:], ALU.subtract)
                    return sv.tick(last)

        def rope_evac(psum_ap, dst_ap, width, qtmp, qswap):
            t0 = rope_evac.t0
            qm = sa.tick(nc.scalar.activation(qtmp[:, 0:width], psum_ap, AF.Copy))
            nc.vector.wait_ge(sa.h, qm)
            nc.vector.tensor_copy(qswap[0:64, 0:width], qtmp[64:128, 0:width])
            nc.vector.tensor_copy(qswap[64:128, 0:width], qtmp[0:64, 0:width])
            nc.vector.tensor_tensor(qtmp[:, 0:width], qtmp[:, 0:width], cosq[:, t0:t0 + width], ALU.mult)
            nc.vector.tensor_tensor(qswap[:, 0:width], qswap[:, 0:width], sinq[:, t0:t0 + width], ALU.mult)
            dm = sv.tick(nc.vector.tensor_tensor(dst_ap, qtmp[:, 0:width], qswap[:, 0:width], ALU.add))
            return qm, dm
        rope_evac.t0 = 0

        def ffn(src, resid, is_final, tag, parts=3):
            barrier_all()
            with ExitStack() as fst:
                fsb = lambda n_, s, d: fst.enter_context(nc.sbuf_tensor(n_ + tag, s, d))
                bres0 = fsb("bres0", [112, E], BF16)
                bres1 = fsb("bres1", [112, E], BF16)
                hsT0 = fsb("hsT0", [112, T], BF16)
                hsT1 = fsb("hsT1", [112, T], BF16)
                g_sb = fsb("g_sb", [128, NE], F32)
                gx = fsb("gx", [128, 224], F32)
                eq = fsb("eq", [128, NE], F32)
                msk = fsb("msk", [128, NE], F32)
                ge_m = fsb("ge_m", [128, NE], F32)
                en = fsb("en", [128, NE], F32)
                hs = fsb("hs", [128, 224], BF16)
                m1 = fsb("m1", [128, 1], F32)
                negm1 = fsb("negm1", [128, 1], F32)
                m2 = fsb("m2", [128, 1], F32)
                e2 = fsb("e2", [128, 1], F32)
                den = fsb("den", [128, 1], F32)
                rec = fsb("rec", [128, 1], F32)

                if os.environ.get("KNOBRES", "") != "1":
                    sw.tick(nc.gpsimd.dma_start(bres0[:], BRE[0:112, :]), 16)
                    bres_mark = sw.tick(nc.gpsimd.dma_start(bres1[:], BRE[112:224, :]), 16)
                else:
                    bres_mark = sw.n

                if not (parts & 1):
                    hsT_mark = sv.tick(nc.vector.memset(hsT0[:], 0.0))
                    hsT_mark = sv.tick(nc.vector.memset(hsT1[:], 0.0))
                elif True:
                  with ExitStack() as hst:
                    agres = hst.enter_context(nc.sbuf_tensor("agres" + tag, [128, NCH * 252], BF16))
                    for c in range(NCH):
                        sw.tick(nc.gpsimd.dma_start(agres[:, c * 252:(c + 1) * 252],
                                                    AGW[c * 128:(c + 1) * 128, :]), 16)
                    ag_mark = sw.n
                    with ExitStack() as pst:
                        hps = [pst.enter_context(nc.psum_tensor(f"hp{i}" + tag, [128, 252], F32)) for i in range(2)]
                        hT0 = pst.enter_context(nc.psum_tensor("hT0" + tag, [112, 128], BF16))
                        hT1 = pst.enter_context(nc.psum_tensor("hT1" + tag, [112, 128], BF16))
                        nc.tensor.wait_ge(sw.h, ag_mark)
                        hp_frees = [0, 0]
                        hT_free = 0
                        for tt in range(int(os.environ.get("KTT", "8"))):
                            hp = hps[tt % 2]
                            if hp_frees[tt % 2]:
                                nc.tensor.wait_ge(sv.h, hp_frees[tt % 2])
                            last = None
                            for c in range(NCH):
                                last = nc.tensor.matmul(
                                    hp[:, 0:252],
                                    lhsT=src[:, c * T + tt * 128: c * T + (tt + 1) * 128],
                                    rhs=agres[:, c * 252:(c + 1) * 252],
                                    start=(c == 0), stop=(c == NCH - 1))
                            hpm = sp.tick(last)
                            logits = hp[:, 224:252]
                            nc.vector.wait_ge(sp.h, hpm)
                            if ffn.simple_gate:
                                nc.vector.memset(gx[:], 0.001)
                                hsm = sv.tick(nc.vector.tensor_tensor(hs[:], hp[:, 0:224], gx[:], ALU.mult))
                                hp_frees[tt % 2] = hsm
                                if ffn.skip_transpose:
                                    nc.vector.wait_ge(sv.h, hsm)
                                    nc.vector.tensor_copy(hsT0[:, tt * 128:(tt + 1) * 128], hs[0:112, 0:128])
                                    hT_free = sv.tick(nc.vector.tensor_copy(hsT1[:, tt * 128:(tt + 1) * 128], hs[0:112, 96:224]))
                                    continue
                                nc.tensor.wait_ge(sv.h, hsm)
                                if tt > 0:
                                    nc.tensor.wait_ge(sv.h, hT_free)
                                nc.tensor.transpose(hT0[:, :], hs[:, 0:112], ident[:])
                                trm = sp.tick(nc.tensor.transpose(hT1[:, :], hs[:, 112:224], ident[:]))
                                nc.vector.wait_ge(sp.h, trm)
                                nc.vector.tensor_copy(hsT0[:, tt * 128:(tt + 1) * 128], hT0[:, :])
                                hT_free = sv.tick(nc.vector.tensor_copy(hsT1[:, tt * 128:(tt + 1) * 128], hT1[:, :]))
                                continue
                            m1m = sv.tick(nc.vector.tensor_reduce(m1[:], logits, AX.X, ALU.max))
                            nc.vector.wait_ge(sv.h, m1m)
                            nc.vector.tensor_scalar(eq[:], logits, m1[:], None, ALU.is_equal)
                            nc.vector.tensor_scalar_mul(negm1[:], m1[:], -1.0)
                            nc.vector.scalar_tensor_tensor(msk[:], eq[:], NEG, logits, ALU.mult, ALU.add)
                            m2m = sv.tick(nc.vector.tensor_reduce(m2[:], msk[:], AX.X, ALU.max))
                            nc.vector.wait_ge(sv.h, m2m)
                            nm = sv.tick(nc.vector.tensor_scalar(ge_m[:], logits, m2[:], None, ALU.is_ge))
                            nc.scalar.wait_ge(sv.h, nm)
                            enm = sa.tick(nc.scalar.activation(en[:], logits, AF.Exp, bias=negm1[:, 0:1]))
                            e2m = sa.tick(nc.scalar.activation(e2[:], m2[:], AF.Exp, bias=negm1[:, 0:1]))
                            nc.vector.wait_ge(sa.h, e2m)
                            dnm = sv.tick(nc.vector.tensor_scalar_add(den[:], e2[:], 1.0))
                            nc.vector.wait_ge(sv.h, dnm)
                            rcm = sv.tick(nc.vector.reciprocal(rec[:], den[:]))
                            nc.vector.tensor_tensor(g_sb[:], en[:], ge_m[:], ALU.mult)
                            nc.vector.wait_ge(sv.h, rcm)
                            nc.vector.tensor_scalar(g_sb[:], g_sb[:], rec[:], None, ALU.mult)
                            for rr in range(8):
                                nc.vector.tensor_copy(gx[:, rr:224:8], g_sb[:])
                            if os.environ.get("KDUMPG", "") == "1" and tag == "f2" and tt == 0 and DBG is not None:
                                sw.tick(nc.gpsimd.dma_start(DBG[:, 4096:4096 + 252], agres[:, 0:252]), 16)
                                sw.tick(nc.gpsimd.dma_start(DBG[:, 4352:4352 + 1024], src[:, 0:1024]), 16)
                                gdump = hst.enter_context(nc.sbuf_tensor("gdump", [128, 64], F32))
                                nc.vector.tensor_copy(gdump[:, 0:28], logits)
                                nc.vector.tensor_copy(gdump[:, 28:56], g_sb[:])
                                nc.vector.tensor_copy(gdump[:, 56:57], m1[:])
                                nc.vector.tensor_copy(gdump[:, 57:58], m2[:])
                                nc.vector.tensor_copy(gdump[:, 58:59], rec[:])
                                gdm = sv.tick(nc.vector.tensor_copy(gdump[:, 59:60], negm1[:]))
                                nc.gpsimd.wait_ge(sv.h, gdm)
                                sw.tick(nc.gpsimd.dma_start(DBG[:, 2048:2112], gdump[:, :]), 16)
                            hsm = sv.tick(nc.vector.tensor_tensor(hs[:], hp[:, 0:224], gx[:], ALU.mult))
                            hp_frees[tt % 2] = hsm
                            nc.tensor.wait_ge(sv.h, hsm)
                            if tt > 0:
                                nc.tensor.wait_ge(sv.h, hT_free)
                            nc.tensor.transpose(hT0[:, :], hs[:, 0:112], ident[:])
                            trm = sp.tick(nc.tensor.transpose(hT1[:, :], hs[:, 112:224], ident[:]))
                            nc.vector.wait_ge(sp.h, trm)
                            nc.vector.tensor_copy(hsT0[:, tt * 128:(tt + 1) * 128], hT0[:, :])
                            hT_free = sv.tick(nc.vector.tensor_copy(hsT1[:, tt * 128:(tt + 1) * 128], hT1[:, :]))
                        hsT_mark = hT_free
                if os.environ.get("KDUMPHST", "") == "1" and tag == "f2" and DBG is not None:
                    nc.sync.wait_ge(sv.h, sv.n)
                    ss.tick(nc.sync.dma_start(DBG[0:112, 0:1024], hsT0[:, :]), 16)
                    ss.tick(nc.sync.dma_start(DBG[0:112, 1024:2048], hsT1[:, :]), 16)

                if not (parts & 2):
                    return
                with ExitStack() as bst:
                    barrier(nc.gpsimd)
                    ofp = [bst.enter_context(nc.sbuf_tensor("ofp" + tag, [128, 512], F32))] if is_final else []
                    with ExitStack() as pst:
                        psums = [pst.enter_context(nc.psum_tensor(f"pb{o}" + tag, [128, 512], F32)) for o in range(4)]
                        psum_free = [0, 0, 0, 0]
                        ofp_free = [0]
                        oh_mark = {}
                        w_marks = {}
                        wsrc = BW.rearrange("(c q) o -> q c o", q=128)
                        def issue_wdma(oh):
                            p_ = oh % 2
                            sw.tick(nc.gpsimd.dma_start(
                                wstream[:, p_ * 8192:(p_ + 1) * 8192].rearrange("q (c o) -> q c o", o=256),
                                wsrc[:, :, oh * 256:(oh + 1) * 256]), 16)
                            w_marks[oh] = sw.n
                        issue_wdma(0)
                        issue_wdma(1)
                        for oh in range(16):
                            p_ = oh % 2
                            if oh >= 2:
                                pass
                            nc.tensor.wait_ge(sw.h, w_marks[oh])
                            if oh == 0:
                                nc.tensor.wait_ge(sw.h, bres_mark)
                                nc.tensor.wait_ge(sv.h, hsT_mark)
                            for tc in range(2):
                                for o2 in range(2):
                                    o = o2
                                    op = oh * 2 + o2
                                    pi = (oh % 2) * 2 + o2
                                    if psum_free[pi]:
                                        nc.tensor.wait_ge(sv.h, psum_free[pi])
                                    for c in range(NCH):
                                        nc.tensor.matmul(
                                            psums[pi][:, :],
                                            lhsT=wstream[:, p_ * 8192 + c * 256 + o2 * 128: p_ * 8192 + c * 256 + (o2 + 1) * 128],
                                            rhs=src[:, c * T + tc * 512: c * T + tc * 512 + 512],
                                            start=(c == 0), stop=False)
                                    nc.tensor.matmul(psums[pi][:, :], lhsT=bres0[:, op * 128:(op + 1) * 128],
                                                     rhs=hsT0[:, tc * 512:(tc + 1) * 512], start=False, stop=False)
                                    last = nc.tensor.matmul(psums[pi][:, :], lhsT=bres1[:, op * 128:(op + 1) * 128],
                                                            rhs=hsT1[:, tc * 512:(tc + 1) * 512], start=False, stop=True)
                                    gm = sp.tick(last)
                                    nc.vector.wait_ge(sp.h, gm)
                                    dsl = slice(op * T + tc * 512, op * T + tc * 512 + 512)
                                    if not is_final:
                                        psum_free[pi] = sv.tick(nc.vector.scalar_tensor_tensor(
                                            resid[:, dsl], psums[pi][:, :], 0.5, resid[:, dsl], ALU.mult, ALU.add))
                                    else:
                                        if ofp_free[0]:
                                            nc.vector.wait_ge(ss.h, ofp_free[0])
                                        em = sv.tick(nc.vector.scalar_tensor_tensor(
                                            ofp[0][:], psums[pi][:, :], 0.5, resid[:, dsl], ALU.mult, ALU.add))
                                        psum_free[pi] = em
                                        nc.sync.wait_ge(sv.h, em)
                                        ofp_free[0] = ss.tick(nc.sync.dma_start(
                                            OUT[op * 128:(op + 1) * 128, tc * 512:tc * 512 + 512], ofp[0][:]), 16)
                            oh_mark[oh] = sp.n
                            if oh + 2 < 16:
                                nc.gpsimd.wait_ge(sp.h, oh_mark[oh])
                                issue_wdma(oh + 2)


        ffn.store_cnt = 0
        ffn.simple_gate = False
        ffn.skip_transpose = False
        from contextlib import nullcontext
        scope = nc.named_scope if os.environ.get("KSCOPE", "") == "1" else (lambda n_: nullcontext())
        nc.vector.wait_ge(sw.h, base_loads["sw"])
        nc.scalar.wait_ge(sw.h, base_loads["sw"])
        nc.tensor.wait_ge(sw.h, base_loads["sw"])
        if stage == 0:
            nc.sync.wait_ge(sw.h, base_loads["sw"])
            ss.tick(nc.sync.dma_start(DBG, A[:, :]), 16)
            nc.sync.wait_ge(ss.h, ss.n)
            return nc
        if os.environ.get("KNOLN", "") == "1":
            for c in range(NCH):
                sv.tick(nc.vector.tensor_copy(Bb[:, c * T:(c + 1) * T], A[:, c * T:(c + 1) * T]))
        else:
            with scope("ln1"):
                ln(A, Bb)
        nc.tensor.wait_ge(sv.h, sv.n)
        if stage == 1:
            nc.sync.wait_ge(sv.h, sv.n)
            ss.tick(nc.sync.dma_start(DBG, Bb[:, :]), 16)
            nc.sync.wait_ge(ss.h, ss.n)
            return nc

        ffn.simple_gate = (stage in (18, 19))
        ffn.skip_transpose = (stage == 19)
        _parts = {16: 1, 17: 2, 19: 1}.get(stage, 3)
        with scope("ffn1"):
            ffn(Bb, A, False, "f1", parts=_parts)
        ffn.simple_gate = False

        if stage in (15, 16, 17, 18, 19):
            nc.sync.wait_ge(sv.h, sv.n)
            ss.tick(nc.sync.dma_start(DBG, A[:, :]), 16)
            nc.sync.wait_ge(ss.h, ss.n)
            return nc
        if stage == 2:
            nc.sync.wait_ge(sv.h, sv.n)
            fm = ss.tick(nc.sync.dma_start(DBG, A[:, :]), 16)
            nc.sync.wait_ge(ss.h, ss.n)
            return nc

        if stage == 40:
            ln(A, Bb)
            nc.tensor.wait_ge(sv.h, sv.n)
            ffn(Bb, A, False, "f2")
            nc.sync.wait_ge(sv.h, sv.n)
            ss.tick(nc.sync.dma_start(DBG, A[:, :]), 16)
            nc.sync.wait_ge(ss.h, ss.n)
            return nc

        with scope("ln2"):
            ln2_mark = ln(A, Bb)
        nc.tensor.wait_ge(sv.h, sv.n)
        nc.sync.wait_ge(sv.h, sv.n)
        spill_mark = ss.tick(nc.sync.dma_start(spill, A[:, :]), 16)

        barrier_all()
        kvscope = scope("kv"); kvscope.__enter__()
        with ExitStack() as kst:
            kstg = kst.enter_context(nc.sbuf_tensor("kstg", [128, 4096], BF16))
            vstg = kst.enter_context(nc.sbuf_tensor("vstg", [128, 4096], BF16))
            vtms = kst.enter_context(nc.sbuf_tensor("vtms", [128, 4096], BF16))
            qtmp0 = kst.enter_context(nc.sbuf_tensor("qtmp0", [128, T], BF16))
            qswap0 = kst.enter_context(nc.sbuf_tensor("qswap0", [128, T], BF16))
            with ExitStack() as pst:
                kp = [pst.enter_context(nc.psum_tensor(f"kp{i}", [128, T], F32)) for i in range(2)]
                vtp = [pst.enter_context(nc.psum_tensor(f"vtp{i}", [128, 128], BF16)) for i in range(2)]
                kp_free = [0, 0]
                kp_free_a = [0, 0]
                vtp_free = [0, 0]
                khh_mark = {}
                wk_marks = {}
                wsrck = WIN.rearrange("(c q) o -> q c o", q=128)

                def issue_kdma(khh):
                    p_ = khh % 2
                    sw.tick(nc.gpsimd.dma_start(
                        wstream[:, p_ * 8192:(p_ + 1) * 8192].rearrange("q (c o) -> q c o", o=256),
                        wsrck[:, :, 4096 + khh * 256: 4096 + (khh + 1) * 256]), 16)
                    wk_marks[khh] = sw.n
                issue_kdma(0)
                issue_kdma(1)
                for khh in range(8):
                    p_ = khh % 2
                    kg = khh // 2
                    is_v = khh >= 4
                    if khh >= 2 and khh % 2 == 0:
                        nc.vector.wait_ge(sg.h, sg.n)
                        nc.scalar.wait_ge(sg.h, sg.n)
                    nc.tensor.wait_ge(sw.h, wk_marks[khh])
                    for hh2 in range(2):
                        h4 = (khh % 2) * 2 + hh2
                        pb = (khh * 2 + hh2) % 2
                        if kp_free[pb]:
                            nc.tensor.wait_ge(sv.h, kp_free[pb])
                        if kp_free_a[pb]:
                            nc.tensor.wait_ge(sa.h, kp_free_a[pb])
                        last = None
                        for tc in range(2):
                            for c in range(NCH):
                                last = nc.tensor.matmul(
                                    kp[pb][:, tc * 512:(tc + 1) * 512],
                                    lhsT=wstream[:, p_ * 8192 + c * 256 + hh2 * 128: p_ * 8192 + c * 256 + (hh2 + 1) * 128],
                                    rhs=Bb[:, c * T + tc * 512: c * T + tc * 512 + 512],
                                    start=(c == 0), stop=(c == NCH - 1))
                        km = sp.tick(last)
                        if not is_v:
                            nc.scalar.wait_ge(sp.h, km)
                            am, dm = rope_evac(kp[pb][:, 0:T], kstg[:, h4 * T:(h4 + 1) * T], T, qtmp0, qswap0)
                            kp_free[pb] = dm
                            kp_free_a[pb] = am
                        else:
                            nc.scalar.wait_ge(sp.h, km)
                            vm = sa.tick(nc.scalar.activation(vstg[:, h4 * T:(h4 + 1) * T], kp[pb][:, 0:T], AF.Copy))
                            kp_free[pb] = 0
                            kp_free_a[pb] = vm
                            nc.tensor.wait_ge(sa.h, vm)
                            for j in range(8):
                                tb = j % 2
                                if vtp_free[tb]:
                                    nc.tensor.wait_ge(sv.h, vtp_free[tb])
                                trm = sp.tick(nc.tensor.transpose(
                                    vtp[tb][:, :], vstg[:, h4 * T + j * 128: h4 * T + (j + 1) * 128], ident[:]))
                                nc.vector.wait_ge(sp.h, trm)
                                vtp_free[tb] = sv.tick(nc.vector.tensor_copy(
                                    vtms[:, h4 * T + j * 128: h4 * T + (j + 1) * 128], vtp[tb][:, :]))
                    khh_mark[khh] = sp.n
                    if khh + 2 < 8:
                        nc.gpsimd.wait_ge(sp.h, khh_mark[khh])
                        issue_kdma(khh + 2)
                    if khh % 2 == 1:
                        nc.scalar.wait_ge(sv.h, sv.n)
                        if not is_v:
                            sg.tick(nc.scalar.dma_start(ccKi.ap()[:, kg * 4096:(kg + 1) * 4096], kstg[:]), 16)
                        else:
                            sg.tick(nc.scalar.dma_start(ccVi.ap()[:, (kg - 2) * 4096:(kg - 1) * 4096], vtms[:]), 16)
        kvscope.__exit__(None, None, None)
        nc.gpsimd.wait_ge(sg.h, sg.n)
        groups = [[0, 1], [2, 3], [4, 5], [6, 7]]
        scc.tick(nc.gpsimd.collective_compute(
            "AllGather", ALU.bypass, replica_groups=groups,
            ins=[ccKi.ap().opt()], outs=[ccKo.ap().opt()]), 1)
        scc.tick(nc.gpsimd.collective_compute(
            "AllGather", ALU.bypass, replica_groups=groups,
            ins=[ccVi.ap().opt()], outs=[ccVo.ap().opt()]), 1)

        barrier_all()
        atscope = scope("attn"); atscope.__enter__()
        with ExitStack() as ast:
            asb = lambda n_, s, d: ast.enter_context(nc.sbuf_tensor(n_, s, d))
            kbufs = [asb(f"kbuf{i}", [128, 2048], BF16) for i in range(2)]
            vbufs = [asb(f"vbuf{i}", [128, 2048], BF16) for i in range(2)]
            wq = wstream
            qh = [asb(f"qh{i}", [128, T], BF16) for i in range(2)]
            pt = [asb(f"pt{i}", [128, T], BF16) for i in range(3)]
            qtmp = asb("qtmp", [128, T], BF16)
            qswap = asb("qswap", [128, T], BF16)
            rz_sb = asb("rz_sb", [128, T], BF16)
            rzrow = asb("rzrow", [1, T], F32)
            rz2row = asb("rz2row", [1, T], BF16)

            def load_wq(g):
                for c in range(NCH):
                    sw.tick(nc.gpsimd.dma_start(
                        wq[:, c * 512:(c + 1) * 512],
                        WIN[c * 128:(c + 1) * 128, g * 512:(g + 1) * 512]), 16)
                return sw.n

            def q_mms(Qp, g2):
                last = None
                for tc in range(2):
                    for c in range(NCH):
                        last = nc.tensor.matmul(
                            Qp[:, tc * 512:(tc + 1) * 512],
                            lhsT=wq[:, c * 512 + g2 * 128: c * 512 + (g2 + 1) * 128],
                            rhs=Bb[:, c * T + tc * 512: c * T + tc * 512 + 512],
                            start=(c == 0), stop=(c == NCH - 1))
                return sp.tick(last)

            wq_mark = load_wq(0)
            kv_marks = [0, 0]

            def load_kv(kh_):
                kb, vb = kbufs[kh_ % 2], vbufs[kh_ % 2]
                nc.sync.wait_ge(sp.h, sp.n)
                ss.tick(nc.sync.dma_start(kb[:, 0:1024], ccKo.ap()[0:128, kh_ * 1024:(kh_ + 1) * 1024]), 16)
                ss.tick(nc.sync.dma_start(kb[:, 1024:2048], ccKo.ap()[128:256, kh_ * 1024:(kh_ + 1) * 1024]), 16)
                ss.tick(nc.sync.dma_start(vb[:, 0:1024], ccVo.ap()[0:128, kh_ * 1024:(kh_ + 1) * 1024]), 16)
                kv_marks[kh_ % 2] = ss.tick(nc.sync.dma_start(vb[:, 1024:2048], ccVo.ap()[128:256, kh_ * 1024:(kh_ + 1) * 1024]), 16)
            qrope_marks = {}
            qrope_amarks = {}
            with ExitStack() as pst0:
                Qp0 = pst0.enter_context(nc.psum_tensor("Qpre", [128, T], F32))
                nc.tensor.wait_ge(sw.h, wq_mark)
                nc.tensor.wait_ge(sv.h, sv.n)
                nc.tensor.wait_ge(sa.h, sa.n)
                q0m = q_mms(Qp0, 0)
                nc.scalar.wait_ge(sp.h, q0m)
                rope_evac.t0 = 0
                am, dm = rope_evac(Qp0[:, 0:T], qh[0][:, 0:T], T, qtmp, qswap)
                qrope_marks[0] = dm
                qrope_amarks[0] = am

            aevac_mark = 0
            rzc_mark = 0
            ln_mark = 0
            wq_loaded_group = 0
            for h in range(NQH):
                kh, g2 = h // 4, h % 4
                cur = h % 2
                kbuf, vbuf = kbufs[kh % 2], vbufs[kh % 2]
                if h == 0:
                    nc.sync.wait_ge(scc.h, 2)
                    load_kv(0)
                    load_kv(1)
                if g2 == 0 and 1 <= kh < NKV - 1:
                    load_kv(kh + 1)

                with ExitStack() as past:
                    attnp = past.enter_context(nc.psum_tensor(f"attnp{h}", [128, T], F32))
                    Zp = past.enter_context(nc.psum_tensor(f"Zp{h}", [1, T], F32))
                    exp_marks = {}
                    zpv_marks = {}
                    with ExitStack() as pst:
                        sT = [pst.enter_context(nc.psum_tensor(f"sT{i}_{h}", [128, T], F32)) for i in range(2)]
                        nc.tensor.wait_ge(sv.h, qrope_marks[h])
                        if rzc_mark:
                            nc.tensor.wait_ge(sv.h, rzc_mark)
                        if g2 == 0:
                            nc.tensor.wait_ge(ss.h, kv_marks[kh % 2])

                        def emit_zpv(idx):
                            i_, r_ = idx // 2, idx % 2
                            a_ = 128 * i_
                            mem = r_ * 8 + i_
                            nc.tensor.wait_ge(sa.h, exp_marks[idx])
                            if idx == 0:
                                if aevac_mark:
                                    nc.tensor.wait_ge(sv.h, aevac_mark)
                                if ln_mark:
                                    nc.tensor.wait_ge(sa.h, ln_mark)
                            last_ = None
                            for lo, hi in ((a_, 512), (max(a_, 512), 1024)):
                                if lo >= hi:
                                    continue
                                nc.tensor.matmul(Zp[0:1, lo:hi], lhsT=ones_c[:],
                                                 rhs=pt[idx % 3][:, lo - a_:hi - a_],
                                                 start=(idx == 0), stop=(idx == 15))
                                last_ = nc.tensor.matmul(attnp[:, lo:hi],
                                                         lhsT=vbuf[:, mem * 128:(mem + 1) * 128],
                                                         rhs=pt[idx % 3][:, lo - a_:hi - a_],
                                                         start=(idx == 0), stop=(idx == 15))
                            zpv_marks[idx] = sp.tick(last_)

                        for idx in range(16):
                            i, r = idx // 2, idx % 2
                            a = 128 * i
                            par = idx % 2
                            if idx >= 2:
                                nc.tensor.wait_ge(sa.h, exp_marks[idx - 2])
                            last = None
                            for lo, hi in ((a, 512), (max(a, 512), 1024)):
                                if lo >= hi:
                                    continue
                                last = nc.tensor.matmul(sT[par][:, lo:hi],
                                                        lhsT=kbuf[:, r * 1024 + a: r * 1024 + a + 128],
                                                        rhs=qh[cur][:, lo:hi], start=True, stop=True)
                            stm = sp.tick(last)
                            nc.vector.wait_ge(sp.h, stm)
                            mkm = sv.tick(nc.vector.tensor_tensor(
                                sT[par][:, a:a + 128], sT[par][:, a:a + 128],
                                (tri0 if r == 0 else tri1)[:], ALU.add))
                            nc.scalar.wait_ge(sv.h, mkm)
                            if idx >= 3:
                                nc.scalar.wait_ge(sp.h, zpv_marks[idx - 3])
                            exp_marks[idx] = sa.tick(nc.scalar.activation(
                                pt[idx % 3][:, 0:T - a], sT[par][:, a:T], AF.Exp, bias=EXPB, scale=SCALE))
                            if idx >= 2:
                                emit_zpv(idx - 2)
                        emit_zpv(14)
                        emit_zpv(15)

                    with ExitStack() as tst:
                        rzbp = tst.enter_context(nc.psum_tensor(f"rzbp{h}", [128, T], F32))
                        Qp = tst.enter_context(nc.psum_tensor(f"Qp{h}", [128, T], F32))
                        hn = h + 1
                        if hn < NQH:
                            nc.tensor.wait_ge(sw.h, wq_mark)
                            nc.tensor.wait_ge(sa.h, exp_marks[15])
                            qm = q_mms(Qp, hn % 4)
                            if hn % 4 == 3 and hn + 1 < NQH:
                                nc.gpsimd.wait_ge(sp.h, qm)
                                wq_mark = load_wq((hn + 1) // 4)
                                wq_loaded_group = (hn + 1) // 4
                        nc.scalar.wait_ge(sp.h, zpv_marks[15])
                        ln_mark = sa.tick(nc.scalar.activation(rzrow[0:1, :], Zp[0:1, :], AF.Ln))
                        nc.scalar.wait_ge(sa.h, ln_mark)
                        rzm = sa.tick(nc.scalar.activation(rz2row[0:1, :], rzrow[0:1, :], AF.Exp, scale=-1.0))
                        nc.tensor.wait_ge(sa.h, rzm)
                        last = None
                        for reg in range(2):
                            last = nc.tensor.matmul(rzbp[:, reg * 512:(reg + 1) * 512], lhsT=ones_r[:],
                                                    rhs=rz2row[0:1, reg * 512:(reg + 1) * 512], start=True, stop=True)
                        rzbm = sp.tick(last)
                        nc.vector.wait_ge(sp.h, rzbm)
                        rzc_mark = sv.tick(nc.vector.tensor_copy(rz_sb[:], rzbp[:, :]))
                        if h == 0:
                            nc.vector.wait_ge(ss.h, spill_mark)
                        aevac_mark = sv.tick(nc.vector.tensor_tensor(
                            A[:, h * T:(h + 1) * T], attnp[:, :], rz_sb[:], ALU.mult))
                        if hn < NQH:
                            nc.scalar.wait_ge(sp.h, qm)
                            rope_evac.t0 = 0
                            am, dm = rope_evac(Qp[:, 0:T], qh[hn % 2][:, 0:T], T, qtmp, qswap)
                            qrope_marks[hn] = dm
                            qrope_amarks[hn] = am

        atscope.__exit__(None, None, None)
        barrier_all()
        opscope = scope("outproj"); opscope.__enter__()
        with ExitStack() as ost:
            xold = [ost.enter_context(nc.sbuf_tensor(f"xold{i}", [128, 4096], BF16)) for i in range(2)]
            with ExitStack() as pst:
                psums = [pst.enter_context(nc.psum_tensor(f"po{o}", [128, 512], F32)) for o in range(4)]
                psum_free = [0, 0, 0, 0]
                xold_marks = [0, 0]
                xold_free = [0, 0]
                oh_mark = {}
                w_marks = {}
                nc.sync.wait_ge(ss.h, spill_mark)
                nc.tensor.wait_ge(sv.h, sv.n)
                nc.tensor.wait_ge(sa.h, sa.n)
                wsrco = WOUT.rearrange("(c q) o -> q c o", q=128)
                def issue_odma(oh):
                    p_ = oh % 2
                    sw.tick(nc.gpsimd.dma_start(
                        wstream[:, p_ * 8192:(p_ + 1) * 8192].rearrange("q (c o) -> q c o", o=256),
                        wsrco[:, :, oh * 256:(oh + 1) * 256]), 16)
                    w_marks[oh] = sw.n
                issue_odma(0)
                issue_odma(1)
                for oh in range(16):
                    p_ = oh % 2
                    og = oh // 2
                    xb = og % 2
                    if oh % 2 == 0:
                        if xold_free[xb]:
                            nc.sync.wait_ge(sv.h, xold_free[xb])
                        xold_marks[xb] = ss.tick(nc.sync.dma_start(
                            xold[xb][:], spill[:, og * 4096:(og + 1) * 4096]), 16)
                    nc.tensor.wait_ge(sw.h, w_marks[oh])
                    for tc in range(2):
                        for o2 in range(2):
                            op = oh * 2 + o2
                            o = (op % 4)
                            pi = (oh % 2) * 2 + o2
                            if psum_free[pi]:
                                nc.tensor.wait_ge(sv.h, psum_free[pi])
                            last = None
                            for c in range(NCH):
                                last = nc.tensor.matmul(
                                    psums[pi][:, :],
                                    lhsT=wstream[:, p_ * 8192 + c * 256 + o2 * 128: p_ * 8192 + c * 256 + (o2 + 1) * 128],
                                    rhs=A[:, c * T + tc * 512: c * T + tc * 512 + 512],
                                    start=(c == 0), stop=(c == NCH - 1))
                            gm = sp.tick(last)
                            nc.vector.wait_ge(sp.h, gm)
                            if tc == 0 and o2 == 0 and oh % 2 == 0:
                                nc.vector.wait_ge(ss.h, xold_marks[xb])
                            dsl = slice(op * T + tc * 512, op * T + tc * 512 + 512)
                            lastv = nc.vector.tensor_tensor(
                                Bb[:, dsl], psums[pi][:, :], xold[xb][:, (o * 2 + tc) * 512:(o * 2 + tc + 1) * 512],
                                ALU.add)
                            psum_free[pi] = sv.tick(lastv)
                    for pi2 in range(4):
                        xold_free[xb] = max(xold_free[xb], psum_free[pi2])
                    oh_mark[oh] = sp.n
                    if oh + 2 < 16:
                        nc.gpsimd.wait_ge(sp.h, oh_mark[oh])
                        issue_odma(oh + 2)

        if stage == 3:
            nc.sync.wait_ge(sv.h, sv.n)
            ss.tick(nc.sync.dma_start(DBG, Bb[:, :]), 16)
            nc.sync.wait_ge(ss.h, ss.n)
            return nc
        if stage == 34:
            nc.sync.wait_ge(sv.h, sv.n)
            ss.tick(nc.sync.dma_start(DBG, Bb[:, :]), 16)

        opscope.__exit__(None, None, None)
        with scope("ln3"):
            ln(Bb, A)
        nc.tensor.wait_ge(sv.h, sv.n)
        if stage == 35:
            nc.sync.wait_ge(sv.h, sv.n)
            ss.tick(nc.sync.dma_start(DBG, A[:, :]), 16)
        if stage == 36:
            ffn(A, Bb, False, "f2")
            nc.sync.wait_ge(sv.h, sv.n)
            ss.tick(nc.sync.dma_start(DBG, Bb[:, :]), 16)
            nc.sync.wait_ge(ss.h, ss.n)
            return nc
        with scope("ffn2"):
            ffn(A, Bb, True, "f2")

        nc.sync.wait_ge(ss.h, ss.n)
    return nc



def _host_tables():
    import jax
    import jax.numpy as jnp
    with jax.default_device(jax.devices("cpu")[0]):
        inv = 1.0 / (10000.0 ** (jnp.arange(0, HD, 2, dtype=jnp.float32) / HD))
        ang = jnp.arange(S, dtype=jnp.float32)[:, None] * inv[None, :]
        cosf = np.asarray(jnp.cos(ang), dtype=np.float32)
        sinf = np.asarray(jnp.sin(ang), dtype=np.float32)
    return cosf, sinf


def kernel(L0_attn_x, L0_attn_w_in, L0_attn_w_out, L0_ffn_base_W, L0_ffn_gate_W,
           L0_ffn_lora_A, L0_ffn_lora_B, _stage=4):
    x = np.asarray(L0_attn_x, np.float32)
    w_in = np.asarray(L0_attn_w_in, np.float32)
    w_out = np.asarray(L0_attn_w_out, np.float32)
    base_W = np.asarray(L0_ffn_base_W, np.float32)
    gate_W = np.asarray(L0_ffn_gate_W, np.float32)
    lora_A = np.asarray(L0_ffn_lora_A, np.float32)
    lora_B = np.asarray(L0_ffn_lora_B, np.float32)

    WIN = np.ascontiguousarray(w_in.T)
    WOUT = np.ascontiguousarray(w_out.T)
    BW = np.ascontiguousarray(base_W.T)
    AGW = np.ascontiguousarray(np.concatenate([lora_A.reshape(224, E).T, gate_W.T], axis=1))
    BRE = np.ascontiguousarray(lora_B.transpose(0, 2, 1).reshape(224, E))
    IDENT = np.eye(128, dtype=ml_dtypes.bfloat16)
    CONSTS = np.zeros((128, 8), np.float32)
    for i, v in enumerate([EXPB, SCALE, -1.0, EPS, -0.5]):
        CONSTS[:, i] = v

    cosf, sinf = _host_tables()
    qgrid, kgrid = np.meshgrid(np.arange(128), np.arange(128), indexing="xy")
    kk = np.arange(128)[:, None]
    qq = np.arange(128)[None, :]
    TRI = np.where(kk <= qq, 0.0, NEG).astype(np.float32)
    ZER = np.zeros((128, 128), np.float32)
    FULLM = np.full((128, 128), NEG, np.float32)

    nc = build_program(stage=_stage)
    in_maps = []
    pos_by_p = {}
    for p in range(2):
        pos = np.concatenate([np.arange(256 * j + 128 * p, 256 * j + 128 * p + 128) for j in range(8)])
        pos_by_p[p] = pos
    tables = {}
    for p in range(2):
        pos = pos_by_p[p]
        cd = np.empty((128, T), np.float32)
        sd = np.empty((128, T), np.float32)
        cd[0:64, :] = cosf[pos, :].T
        cd[64:128, :] = cosf[pos, :].T
        sd[0:64, :] = -sinf[pos, :].T
        sd[64:128, :] = sinf[pos, :].T
        tables[p] = (np.ascontiguousarray(cd), np.ascontiguousarray(sd))

    for c in range(8):
        b, p = c // 2, c % 2
        pos = pos_by_p[p]
        xT = np.ascontiguousarray(x[b][pos, :].T)
        cd, sd = tables[p]
        in_maps.append({
            "XT": xT, "WIN": WIN, "WOUT": WOUT, "BW": BW, "AGW": AGW, "BRE": BRE,
            "COS": cd, "SIN": sd,
            "MK0": TRI if p == 0 else ZER,
            "MK1": FULLM if p == 0 else TRI,
            "CONSTS": CONSTS, "IDENT": IDENT,
        })

    res = run_bass_kernel_spmd(nc, in_maps, core_ids=list(range(8)))

    if _stage != 4:
        return res

    out = np.empty((B, S, E), np.float32)
    for c in range(8):
        b, p = c // 2, c % 2
        o = np.asarray(res.results[c]["OUT"], np.float32)
        for j in range(8):
            out[b, 256 * j + 128 * p: 256 * j + 128 * p + 128, :] = o[:, j * 128:(j + 1) * 128].T
    return out


# version 04 (final)
# speedup vs baseline: 1.2731x; 1.0827x over previous
import os
import numpy as np
import ml_dtypes
from contextlib import ExitStack

import concourse.bass as bass
import concourse.mybir as mybir
from concourse.bass_utils import run_bass_kernel_spmd

F32 = mybir.dt.float32
BF16 = mybir.dt.bfloat16
AF = mybir.ActivationFunctionType
ALU = mybir.AluOpType
AX = mybir.AxisListType

B, S, E = 4, 2048, 4096
T = 1024
NCH = 32
NQH, NKV, HD = 32, 8, 128
NE, RANK = 28, 8
NEG = -1e30

SCALE = float(np.float32(1.0) / np.sqrt(np.float32(128.0)))
EXPB = -20.0
EPS = 1e-5
C_INV4096 = float(np.float32(1.0 / 4096.0))


class Sem:
    def __init__(self, nc, stack, name):
        self.h = stack.enter_context(nc.semaphore(name))
        self.n = 0

    def tick(self, ins, k=1):
        ins.then_inc(self.h, k)
        self.n += k
        return self.n


def build_program(stage=4):
    nc = bass.Bass("TRN2", target_bir_lowering=False)

    XT = nc.dram_tensor("XT", [E, T], F32, kind="ExternalInput").ap()
    WIN = nc.dram_tensor("WIN", [E, 6144], F32, kind="ExternalInput").ap()
    WOUT = nc.dram_tensor("WOUT", [E, E], F32, kind="ExternalInput").ap()
    BW = nc.dram_tensor("BW", [E, E], F32, kind="ExternalInput").ap()
    AGW = nc.dram_tensor("AGW", [E, 252], F32, kind="ExternalInput").ap()
    BRE = nc.dram_tensor("BRE", [224, E], F32, kind="ExternalInput").ap()
    COS = nc.dram_tensor("COS", [128, T], F32, kind="ExternalInput").ap()
    SIN = nc.dram_tensor("SIN", [128, T], F32, kind="ExternalInput").ap()
    MK0 = nc.dram_tensor("MK0", [128, 128], F32, kind="ExternalInput").ap()
    MK1 = nc.dram_tensor("MK1", [128, 128], F32, kind="ExternalInput").ap()
    CONSTS = nc.dram_tensor("CONSTS", [128, 8], F32, kind="ExternalInput").ap()
    IDENT = nc.dram_tensor("IDENT", [128, 128], BF16, kind="ExternalInput").ap()
    OUT = nc.dram_tensor("OUT", [E, T], F32, kind="ExternalOutput").ap() if stage >= 4 else None
    DBG = nc.dram_tensor("DBG", [128, NCH * T], BF16, kind="ExternalOutput").ap() if stage != 4 else None

    spill = nc.dram_tensor("spill", [128, NCH * T], BF16, kind="Internal").ap()
    ccKi = nc.dram_tensor("ccKi", [128, 8192], BF16, kind="Internal")
    ccVi = nc.dram_tensor("ccVi", [128, 8192], BF16, kind="Internal")
    ccKo = nc.dram_tensor("ccKo", [256, 8192], BF16, kind="Internal")
    ccVo = nc.dram_tensor("ccVo", [256, 8192], BF16, kind="Internal")

    with ExitStack() as st:
        def sb(name, shape, dt):
            return st.enter_context(nc.sbuf_tensor(name, shape, dt))

        A = sb("A", [128, NCH * T], BF16)
        Bb = sb("Bb", [128, NCH * T], BF16)
        cosq = sb("cosq", [128, T], BF16)
        sinq = sb("sinq", [128, T], BF16)
        tri0 = sb("tri0", [128, 128], F32)
        tri1 = sb("tri1", [128, 128], F32)
        consts = sb("consts", [128, 8], F32)
        ident = sb("ident", [128, 128], BF16)
        ones_c = sb("ones_c", [128, 1], BF16)
        ones_r = sb("ones_r", [1, 128], BF16)
        wstream = sb("wstream", [128, NCH * 512], BF16)

        nc.const_aps.register(F32, EXPB, consts[:, 0:1])
        nc.const_aps.register(F32, SCALE, consts[:, 1:2])
        nc.const_aps.register(F32, -1.0, consts[:, 2:3])
        nc.const_aps.register(F32, EPS, consts[:, 3:4])
        nc.const_aps.register(F32, -0.5, consts[:, 4:5])

        sw = Sem(nc, st, "sw")
        ss = Sem(nc, st, "ss")
        sg = Sem(nc, st, "sg")
        sp = Sem(nc, st, "sp")
        sv = Sem(nc, st, "sv")
        sa = Sem(nc, st, "sa")
        scc = Sem(nc, st, "scc")

        ss.tick(nc.sync.dma_start(consts[:], CONSTS), 16)
        ss.tick(nc.sync.dma_start(tri0[:], MK0), 16)
        ss.tick(nc.sync.dma_start(tri1[:], MK1), 16)
        ss.tick(nc.sync.dma_start(ident[:], IDENT), 16)
        sw.tick(nc.gpsimd.dma_start(cosq[:], COS), 16)
        sw.tick(nc.gpsimd.dma_start(sinq[:], SIN), 16)
        for c in range(NCH):
            sw.tick(nc.gpsimd.dma_start(A[:, c * T:(c + 1) * T], XT[c * 128:(c + 1) * 128, :]), 16)
        sv.tick(nc.vector.memset(ones_c[:], 1.0))
        sv.tick(nc.vector.memset(ones_r[:], 1.0))
        base_loads = dict(sw=sw.n, ss=ss.n, sv=sv.n)

        nc.scalar.wait_ge(ss.h, base_loads["ss"])
        nc.vector.wait_ge(ss.h, base_loads["ss"])
        nc.tensor.wait_ge(sv.h, base_loads["sv"])

        def barrier(*engs):
            for eng in engs:
                eng.wait_ge(sp.h, sp.n)
                eng.wait_ge(sv.h, sv.n)
                eng.wait_ge(sa.h, sa.n)
                eng.wait_ge(sw.h, sw.n)
                eng.wait_ge(ss.h, ss.n)
                eng.wait_ge(sg.h, sg.n)

        def barrier_all():
            barrier(nc.gpsimd, nc.sync, nc.scalar, nc.vector, nc.tensor)


        class _LnCnt:
            n = 0

        def ln(src, dst):
            barrier_all()
            _LnCnt.n += 1
            u = f"_l{_LnCnt.n}"
            with ExitStack() as lst:
                xsq = [lst.enter_context(nc.sbuf_tensor(f"xsq{i}{u}", [128, T], BF16)) for i in range(2)]
                m_row = lst.enter_context(nc.sbuf_tensor("m_row" + u, [1, T], F32))
                v_row = lst.enter_context(nc.sbuf_tensor("v_row" + u, [1, T], F32))
                t_row = lst.enter_context(nc.sbuf_tensor("t_row" + u, [1, T], F32))
                rstd_row = lst.enter_context(nc.sbuf_tensor("rstd_row" + u, [1, T], BF16))
                mr_row = lst.enter_context(nc.sbuf_tensor("mr_row" + u, [1, T], BF16))
                rstd_b = lst.enter_context(nc.sbuf_tensor("rstd_b" + u, [128, T], BF16))
                mrb_b = lst.enter_context(nc.sbuf_tensor("mrb_b" + u, [128, T], BF16))
                with ExitStack() as pst:
                    Sp = pst.enter_context(nc.psum_tensor("Sp" + u, [1, T], F32))
                    Sq = pst.enter_context(nc.psum_tensor("Sq" + u, [1, T], F32))
                    sq_marks = [0, 0]
                    for c in range(NCH):
                        if c >= 2:
                            nc.scalar.wait_ge(sp.h, sq_marks[c % 2])
                        sqm = sa.tick(nc.scalar.activation(xsq[c % 2][:], src[:, c * T:(c + 1) * T], AF.Square))
                        for reg in range(2):
                            nc.tensor.matmul(Sp[0:1, reg * 512:(reg + 1) * 512], lhsT=ones_c[:],
                                             rhs=src[:, c * T + reg * 512: c * T + (reg + 1) * 512],
                                             start=(c == 0), stop=(c == NCH - 1))
                        nc.tensor.wait_ge(sa.h, sqm)
                        last = None
                        for reg in range(2):
                            last = nc.tensor.matmul(Sq[0:1, reg * 512:(reg + 1) * 512], lhsT=ones_c[:],
                                                    rhs=xsq[c % 2][:, reg * 512:(reg + 1) * 512],
                                                    start=(c == 0), stop=(c == NCH - 1))
                        sq_marks[c % 2] = sp.tick(last)
                    stats_done = sp.n
                    nc.vector.wait_ge(sp.h, stats_done)
                    nc.vector.tensor_scalar_mul(m_row[0:1, :], Sp[0:1, :], C_INV4096)
                    nc.vector.tensor_scalar_mul(v_row[0:1, :], Sq[0:1, :], C_INV4096)
                    nc.vector.tensor_tensor(t_row[0:1, :], m_row[0:1, :], m_row[0:1, :], ALU.mult)
                    rows1 = sv.tick(nc.vector.tensor_tensor(v_row[0:1, :], v_row[0:1, :], t_row[0:1, :], ALU.subtract))
                    nc.scalar.wait_ge(sv.h, rows1)
                    lnm = sa.tick(nc.scalar.activation(t_row[0:1, :], v_row[0:1, :], AF.Ln, bias=EPS))
                    nc.scalar.wait_ge(sa.h, lnm)
                    rsm = sa.tick(nc.scalar.activation(rstd_row[0:1, :], t_row[0:1, :], AF.Exp, scale=-0.5))
                    nc.vector.wait_ge(sa.h, rsm)
                    mrm = sv.tick(nc.vector.tensor_tensor(mr_row[0:1, :], m_row[0:1, :], rstd_row[0:1, :], ALU.mult))
                with ExitStack() as pst:
                    bp0 = pst.enter_context(nc.psum_tensor("bp0" + u, [128, T], F32))
                    bp1 = pst.enter_context(nc.psum_tensor("bp1" + u, [128, T], F32))
                    nc.tensor.wait_ge(sv.h, mrm)
                    last = None
                    for reg in range(2):
                        nc.tensor.matmul(bp0[:, reg * 512:(reg + 1) * 512], lhsT=ones_r[:],
                                         rhs=rstd_row[0:1, reg * 512:(reg + 1) * 512], start=True, stop=True)
                        last = nc.tensor.matmul(bp1[:, reg * 512:(reg + 1) * 512], lhsT=ones_r[:],
                                                rhs=mr_row[0:1, reg * 512:(reg + 1) * 512], start=True, stop=True)
                    bm = sp.tick(last)
                    nc.vector.wait_ge(sp.h, bm)
                    nc.vector.tensor_copy(rstd_b[:], bp0[:, :])
                    bcm = sv.tick(nc.vector.tensor_copy(mrb_b[:], bp1[:, :]))
                    if os.environ.get("KDUMPLN", "") == "1" and _LnCnt.n == int(os.environ.get("KDUMPLN_N", "3")) and DBG is not None:
                        lrows = lst.enter_context(nc.sbuf_tensor("lrows" + u, [1, 3072], F32))
                        nc.vector.tensor_copy(lrows[0:1, 0:1024], rstd_row[0:1, :])
                        nc.vector.tensor_copy(lrows[0:1, 1024:2048], m_row[0:1, :])
                        lrm = sv.tick(nc.vector.tensor_copy(lrows[0:1, 2048:3072], v_row[0:1, :]))
                        nc.gpsimd.wait_ge(sv.h, lrm)
                        sw.tick(nc.gpsimd.dma_start(DBG[0:1, 8192:11264], lrows[0:1, :]), 16)
                        sw.tick(nc.gpsimd.dma_start(DBG[:, 11264:12288], rstd_b[:, :]), 16)
                    nc.vector.wait_ge(sv.h, bcm)
                    last = None
                    for c in range(NCH):
                        cs = slice(c * T, (c + 1) * T)
                        nc.vector.tensor_tensor(dst[:, cs], src[:, cs], rstd_b[:], ALU.mult)
                        last = nc.vector.tensor_tensor(dst[:, cs], dst[:, cs], mrb_b[:], ALU.subtract)
                    return sv.tick(last)

        def rope_evac(psum_ap, dst_ap, width, qtmp, qswap, t0=0):
            qm = sa.tick(nc.scalar.activation(qtmp[:, t0:t0 + width], psum_ap, AF.Copy))
            nc.vector.wait_ge(sa.h, qm)
            nc.vector.tensor_copy(qswap[0:64, t0:t0 + width], qtmp[64:128, t0:t0 + width])
            nc.vector.tensor_copy(qswap[64:128, t0:t0 + width], qtmp[0:64, t0:t0 + width])
            nc.vector.tensor_tensor(qtmp[:, t0:t0 + width], qtmp[:, t0:t0 + width], cosq[:, t0:t0 + width], ALU.mult)
            nc.vector.tensor_tensor(qswap[:, t0:t0 + width], qswap[:, t0:t0 + width], sinq[:, t0:t0 + width], ALU.mult)
            dm = sv.tick(nc.vector.tensor_tensor(dst_ap, qtmp[:, t0:t0 + width], qswap[:, t0:t0 + width], ALU.add))
            return qm, dm

        def ffn(src, resid, is_final, tag, parts=3):
            barrier_all()
            with ExitStack() as fst:
                fsb = lambda n_, s, d: fst.enter_context(nc.sbuf_tensor(n_ + tag, s, d))
                bres0 = fsb("bres0", [112, E], BF16)
                bres1 = fsb("bres1", [112, E], BF16)
                hsT0 = fsb("hsT0", [112, T], BF16)
                hsT1 = fsb("hsT1", [112, T], BF16)
                g_sb = fsb("g_sb", [128, NE], F32)
                gx = fsb("gx", [128, 224], F32)
                eq = fsb("eq", [128, NE], F32)
                msk = fsb("msk", [128, NE], F32)
                ge_m = fsb("ge_m", [128, NE], F32)
                en = fsb("en", [128, NE], F32)
                hs = fsb("hs", [128, 224], BF16)
                m1 = fsb("m1", [128, 1], F32)
                negm1 = fsb("negm1", [128, 1], F32)
                m2 = fsb("m2", [128, 1], F32)
                e2 = fsb("e2", [128, 1], F32)
                den = fsb("den", [128, 1], F32)
                rec = fsb("rec", [128, 1], F32)

                if os.environ.get("KNOBRES", "") != "1":
                    sw.tick(nc.gpsimd.dma_start(bres0[:], BRE[0:112, :]), 16)
                    bres_mark = sw.tick(nc.gpsimd.dma_start(bres1[:], BRE[112:224, :]), 16)
                else:
                    bres_mark = sw.n

                if not (parts & 1):
                    hsT_mark = sv.tick(nc.vector.memset(hsT0[:], 0.0))
                    hsT_mark = sv.tick(nc.vector.memset(hsT1[:], 0.0))
                elif True:
                  with ExitStack() as hst:
                    agres = hst.enter_context(nc.sbuf_tensor("agres" + tag, [128, NCH * 252], BF16))
                    for c in range(NCH):
                        sw.tick(nc.gpsimd.dma_start(agres[:, c * 252:(c + 1) * 252],
                                                    AGW[c * 128:(c + 1) * 128, :]), 16)
                    ag_mark = sw.n
                    with ExitStack() as pst:
                        hps = [pst.enter_context(nc.psum_tensor(f"hp{i}" + tag, [128, 252], F32)) for i in range(2)]
                        hT0 = pst.enter_context(nc.psum_tensor("hT0" + tag, [112, 128], BF16))
                        hT1 = pst.enter_context(nc.psum_tensor("hT1" + tag, [112, 128], BF16))
                        nc.tensor.wait_ge(sw.h, ag_mark)
                        hp_frees = [0, 0]
                        hT_free = 0
                        for tt in range(int(os.environ.get("KTT", "8"))):
                            hp = hps[tt % 2]
                            if hp_frees[tt % 2]:
                                nc.tensor.wait_ge(sv.h, hp_frees[tt % 2])
                            last = None
                            for c in range(NCH):
                                last = nc.tensor.matmul(
                                    hp[:, 0:252],
                                    lhsT=src[:, c * T + tt * 128: c * T + (tt + 1) * 128],
                                    rhs=agres[:, c * 252:(c + 1) * 252],
                                    start=(c == 0), stop=(c == NCH - 1))
                            hpm = sp.tick(last)
                            logits = hp[:, 224:252]
                            nc.vector.wait_ge(sp.h, hpm)
                            if ffn.simple_gate:
                                nc.vector.memset(gx[:], 0.001)
                                hsm = sv.tick(nc.vector.tensor_tensor(hs[:], hp[:, 0:224], gx[:], ALU.mult))
                                hp_frees[tt % 2] = hsm
                                if ffn.skip_transpose:
                                    nc.vector.wait_ge(sv.h, hsm)
                                    nc.vector.tensor_copy(hsT0[:, tt * 128:(tt + 1) * 128], hs[0:112, 0:128])
                                    hT_free = sv.tick(nc.vector.tensor_copy(hsT1[:, tt * 128:(tt + 1) * 128], hs[0:112, 96:224]))
                                    continue
                                nc.tensor.wait_ge(sv.h, hsm)
                                if tt > 0:
                                    nc.tensor.wait_ge(sv.h, hT_free)
                                nc.tensor.transpose(hT0[:, :], hs[:, 0:112], ident[:])
                                trm = sp.tick(nc.tensor.transpose(hT1[:, :], hs[:, 112:224], ident[:]))
                                nc.vector.wait_ge(sp.h, trm)
                                nc.vector.tensor_copy(hsT0[:, tt * 128:(tt + 1) * 128], hT0[:, :])
                                hT_free = sv.tick(nc.vector.tensor_copy(hsT1[:, tt * 128:(tt + 1) * 128], hT1[:, :]))
                                continue
                            m1m = sv.tick(nc.vector.tensor_reduce(m1[:], logits, AX.X, ALU.max))
                            nc.vector.wait_ge(sv.h, m1m)
                            nc.vector.tensor_scalar(eq[:], logits, m1[:], None, ALU.is_equal)
                            nc.vector.tensor_scalar_mul(negm1[:], m1[:], -1.0)
                            nc.vector.scalar_tensor_tensor(msk[:], eq[:], NEG, logits, ALU.mult, ALU.add)
                            m2m = sv.tick(nc.vector.tensor_reduce(m2[:], msk[:], AX.X, ALU.max))
                            nc.vector.wait_ge(sv.h, m2m)
                            nm = sv.tick(nc.vector.tensor_scalar(ge_m[:], logits, m2[:], None, ALU.is_ge))
                            nc.scalar.wait_ge(sv.h, nm)
                            enm = sa.tick(nc.scalar.activation(en[:], logits, AF.Exp, bias=negm1[:, 0:1]))
                            e2m = sa.tick(nc.scalar.activation(e2[:], m2[:], AF.Exp, bias=negm1[:, 0:1]))
                            nc.vector.wait_ge(sa.h, e2m)
                            dnm = sv.tick(nc.vector.tensor_scalar_add(den[:], e2[:], 1.0))
                            nc.vector.wait_ge(sv.h, dnm)
                            rcm = sv.tick(nc.vector.reciprocal(rec[:], den[:]))
                            nc.vector.tensor_tensor(g_sb[:], en[:], ge_m[:], ALU.mult)
                            nc.vector.wait_ge(sv.h, rcm)
                            nc.vector.tensor_scalar(g_sb[:], g_sb[:], rec[:], None, ALU.mult)
                            for rr in range(8):
                                nc.vector.tensor_copy(gx[:, rr:224:8], g_sb[:])
                            if os.environ.get("KDUMPG", "") == "1" and tag == "f2" and tt == 0 and DBG is not None:
                                sw.tick(nc.gpsimd.dma_start(DBG[:, 4096:4096 + 252], agres[:, 0:252]), 16)
                                sw.tick(nc.gpsimd.dma_start(DBG[:, 4352:4352 + 1024], src[:, 0:1024]), 16)
                                gdump = hst.enter_context(nc.sbuf_tensor("gdump", [128, 64], F32))
                                nc.vector.tensor_copy(gdump[:, 0:28], logits)
                                nc.vector.tensor_copy(gdump[:, 28:56], g_sb[:])
                                nc.vector.tensor_copy(gdump[:, 56:57], m1[:])
                                nc.vector.tensor_copy(gdump[:, 57:58], m2[:])
                                nc.vector.tensor_copy(gdump[:, 58:59], rec[:])
                                gdm = sv.tick(nc.vector.tensor_copy(gdump[:, 59:60], negm1[:]))
                                nc.gpsimd.wait_ge(sv.h, gdm)
                                sw.tick(nc.gpsimd.dma_start(DBG[:, 2048:2112], gdump[:, :]), 16)
                            hsm = sv.tick(nc.vector.tensor_tensor(hs[:], hp[:, 0:224], gx[:], ALU.mult))
                            hp_frees[tt % 2] = hsm
                            nc.tensor.wait_ge(sv.h, hsm)
                            if tt > 0:
                                nc.tensor.wait_ge(sv.h, hT_free)
                            nc.tensor.transpose(hT0[:, :], hs[:, 0:112], ident[:])
                            trm = sp.tick(nc.tensor.transpose(hT1[:, :], hs[:, 112:224], ident[:]))
                            nc.vector.wait_ge(sp.h, trm)
                            nc.vector.tensor_copy(hsT0[:, tt * 128:(tt + 1) * 128], hT0[:, :])
                            hT_free = sv.tick(nc.vector.tensor_copy(hsT1[:, tt * 128:(tt + 1) * 128], hT1[:, :]))
                        hsT_mark = hT_free
                if os.environ.get("KDUMPHST", "") == "1" and tag == "f2" and DBG is not None:
                    nc.sync.wait_ge(sv.h, sv.n)
                    ss.tick(nc.sync.dma_start(DBG[0:112, 0:1024], hsT0[:, :]), 16)
                    ss.tick(nc.sync.dma_start(DBG[0:112, 1024:2048], hsT1[:, :]), 16)

                if not (parts & 2):
                    return
                with ExitStack() as bst:
                    barrier(nc.gpsimd)
                    ofp = [bst.enter_context(nc.sbuf_tensor("ofp" + tag, [128, 512], F32))] if is_final else []
                    with ExitStack() as pst:
                        psums = [pst.enter_context(nc.psum_tensor(f"pb{o}" + tag, [128, 512], F32)) for o in range(4)]
                        psum_free = [0, 0, 0, 0]
                        ofp_free = [0]
                        oh_mark = {}
                        w_marks = {}
                        wsrc = BW.rearrange("(c q) o -> q c o", q=128)
                        def issue_wdma(oh):
                            p_ = oh % 2
                            sw.tick(nc.gpsimd.dma_start(
                                wstream[:, p_ * 8192:(p_ + 1) * 8192].rearrange("q (c o) -> q c o", o=256),
                                wsrc[:, :, oh * 256:(oh + 1) * 256]), 16)
                            w_marks[oh] = sw.n
                        issue_wdma(0)
                        issue_wdma(1)
                        for oh in range(16):
                            p_ = oh % 2
                            if oh >= 2:
                                pass
                            nc.tensor.wait_ge(sw.h, w_marks[oh])
                            if oh == 0:
                                nc.tensor.wait_ge(sw.h, bres_mark)
                                nc.tensor.wait_ge(sv.h, hsT_mark)
                            for tc in range(2):
                                for o2 in range(2):
                                    o = o2
                                    op = oh * 2 + o2
                                    pi = (oh % 2) * 2 + o2
                                    if psum_free[pi]:
                                        nc.tensor.wait_ge(sv.h, psum_free[pi])
                                    for c in range(NCH):
                                        nc.tensor.matmul(
                                            psums[pi][:, :],
                                            lhsT=wstream[:, p_ * 8192 + c * 256 + o2 * 128: p_ * 8192 + c * 256 + (o2 + 1) * 128],
                                            rhs=src[:, c * T + tc * 512: c * T + tc * 512 + 512],
                                            start=(c == 0), stop=False)
                                    nc.tensor.matmul(psums[pi][:, :], lhsT=bres0[:, op * 128:(op + 1) * 128],
                                                     rhs=hsT0[:, tc * 512:(tc + 1) * 512], start=False, stop=False)
                                    last = nc.tensor.matmul(psums[pi][:, :], lhsT=bres1[:, op * 128:(op + 1) * 128],
                                                            rhs=hsT1[:, tc * 512:(tc + 1) * 512], start=False, stop=True)
                                    gm = sp.tick(last)
                                    nc.vector.wait_ge(sp.h, gm)
                                    dsl = slice(op * T + tc * 512, op * T + tc * 512 + 512)
                                    if not is_final:
                                        psum_free[pi] = sv.tick(nc.vector.scalar_tensor_tensor(
                                            resid[:, dsl], psums[pi][:, :], 0.5, resid[:, dsl], ALU.mult, ALU.add))
                                    else:
                                        if ofp_free[0]:
                                            nc.vector.wait_ge(ss.h, ofp_free[0])
                                        em = sv.tick(nc.vector.scalar_tensor_tensor(
                                            ofp[0][:], psums[pi][:, :], 0.5, resid[:, dsl], ALU.mult, ALU.add))
                                        psum_free[pi] = em
                                        nc.sync.wait_ge(sv.h, em)
                                        ofp_free[0] = ss.tick(nc.sync.dma_start(
                                            OUT[op * 128:(op + 1) * 128, tc * 512:tc * 512 + 512], ofp[0][:]), 16)
                            oh_mark[oh] = sp.n
                            if oh + 2 < 16:
                                nc.gpsimd.wait_ge(sp.h, oh_mark[oh])
                                issue_wdma(oh + 2)


        ffn.store_cnt = 0
        ffn.simple_gate = False
        ffn.skip_transpose = False
        from contextlib import nullcontext
        scope = nc.named_scope if os.environ.get("KSCOPE", "") == "1" else (lambda n_: nullcontext())
        nc.vector.wait_ge(sw.h, base_loads["sw"])
        nc.scalar.wait_ge(sw.h, base_loads["sw"])
        nc.tensor.wait_ge(sw.h, base_loads["sw"])
        if stage == 0:
            nc.sync.wait_ge(sw.h, base_loads["sw"])
            ss.tick(nc.sync.dma_start(DBG, A[:, :]), 16)
            nc.sync.wait_ge(ss.h, ss.n)
            return nc
        if os.environ.get("KNOLN", "") == "1":
            for c in range(NCH):
                sv.tick(nc.vector.tensor_copy(Bb[:, c * T:(c + 1) * T], A[:, c * T:(c + 1) * T]))
        else:
            with scope("ln1"):
                ln(A, Bb)
        nc.tensor.wait_ge(sv.h, sv.n)
        if stage == 1:
            nc.sync.wait_ge(sv.h, sv.n)
            ss.tick(nc.sync.dma_start(DBG, Bb[:, :]), 16)
            nc.sync.wait_ge(ss.h, ss.n)
            return nc

        ffn.simple_gate = (stage in (18, 19))
        ffn.skip_transpose = (stage == 19)
        _parts = {16: 1, 17: 2, 19: 1}.get(stage, 3)
        with scope("ffn1"):
            ffn(Bb, A, False, "f1", parts=_parts)
        ffn.simple_gate = False

        if stage in (15, 16, 17, 18, 19):
            nc.sync.wait_ge(sv.h, sv.n)
            ss.tick(nc.sync.dma_start(DBG, A[:, :]), 16)
            nc.sync.wait_ge(ss.h, ss.n)
            return nc
        if stage == 2:
            nc.sync.wait_ge(sv.h, sv.n)
            fm = ss.tick(nc.sync.dma_start(DBG, A[:, :]), 16)
            nc.sync.wait_ge(ss.h, ss.n)
            return nc

        if stage == 40:
            ln(A, Bb)
            nc.tensor.wait_ge(sv.h, sv.n)
            ffn(Bb, A, False, "f2")
            nc.sync.wait_ge(sv.h, sv.n)
            ss.tick(nc.sync.dma_start(DBG, A[:, :]), 16)
            nc.sync.wait_ge(ss.h, ss.n)
            return nc

        with scope("ln2"):
            ln2_mark = ln(A, Bb)
        nc.tensor.wait_ge(sv.h, sv.n)
        nc.sync.wait_ge(sv.h, sv.n)
        spill_mark = ss.tick(nc.sync.dma_start(spill, A[:, :]), 16)

        barrier_all()
        kvscope = scope("kv"); kvscope.__enter__()
        with ExitStack() as kst:
            kstg = kst.enter_context(nc.sbuf_tensor("kstg", [128, 4096], BF16))
            vstg = kst.enter_context(nc.sbuf_tensor("vstg", [128, 4096], BF16))
            vtms = kst.enter_context(nc.sbuf_tensor("vtms", [128, 4096], BF16))
            qtmp0 = kst.enter_context(nc.sbuf_tensor("qtmp0", [128, T], BF16))
            qswap0 = kst.enter_context(nc.sbuf_tensor("qswap0", [128, T], BF16))
            with ExitStack() as pst:
                kp = [pst.enter_context(nc.psum_tensor(f"kp{i}", [128, T], F32)) for i in range(2)]
                vtp = [pst.enter_context(nc.psum_tensor(f"vtp{i}", [128, 128], BF16)) for i in range(2)]
                kp_free = [0, 0]
                kp_free_a = [0, 0]
                vtp_free = [0, 0]
                khh_mark = {}
                wk_marks = {}
                wsrck = WIN.rearrange("(c q) o -> q c o", q=128)

                def issue_kdma(khh):
                    p_ = khh % 2
                    sw.tick(nc.gpsimd.dma_start(
                        wstream[:, p_ * 8192:(p_ + 1) * 8192].rearrange("q (c o) -> q c o", o=256),
                        wsrck[:, :, 4096 + khh * 256: 4096 + (khh + 1) * 256]), 16)
                    wk_marks[khh] = sw.n
                issue_kdma(0)
                issue_kdma(1)
                for khh in range(8):
                    p_ = khh % 2
                    kg = khh // 2
                    is_v = khh >= 4
                    if khh >= 2 and khh % 2 == 0:
                        nc.vector.wait_ge(sg.h, sg.n)
                        nc.scalar.wait_ge(sg.h, sg.n)
                    nc.tensor.wait_ge(sw.h, wk_marks[khh])
                    for hh2 in range(2):
                        h4 = (khh % 2) * 2 + hh2
                        pb = (khh * 2 + hh2) % 2
                        if kp_free[pb]:
                            nc.tensor.wait_ge(sv.h, kp_free[pb])
                        if kp_free_a[pb]:
                            nc.tensor.wait_ge(sa.h, kp_free_a[pb])
                        last = None
                        for tc in range(2):
                            for c in range(NCH):
                                last = nc.tensor.matmul(
                                    kp[pb][:, tc * 512:(tc + 1) * 512],
                                    lhsT=wstream[:, p_ * 8192 + c * 256 + hh2 * 128: p_ * 8192 + c * 256 + (hh2 + 1) * 128],
                                    rhs=Bb[:, c * T + tc * 512: c * T + tc * 512 + 512],
                                    start=(c == 0), stop=(c == NCH - 1))
                        km = sp.tick(last)
                        if not is_v:
                            nc.scalar.wait_ge(sp.h, km)
                            am, dm = rope_evac(kp[pb][:, 0:T], kstg[:, h4 * T:(h4 + 1) * T], T, qtmp0, qswap0)
                            kp_free[pb] = dm
                            kp_free_a[pb] = am
                        else:
                            nc.scalar.wait_ge(sp.h, km)
                            vm = sa.tick(nc.scalar.activation(vstg[:, h4 * T:(h4 + 1) * T], kp[pb][:, 0:T], AF.Copy))
                            kp_free[pb] = 0
                            kp_free_a[pb] = vm
                            nc.tensor.wait_ge(sa.h, vm)
                            for j in range(8):
                                tb = j % 2
                                if vtp_free[tb]:
                                    nc.tensor.wait_ge(sv.h, vtp_free[tb])
                                trm = sp.tick(nc.tensor.transpose(
                                    vtp[tb][:, :], vstg[:, h4 * T + j * 128: h4 * T + (j + 1) * 128], ident[:]))
                                nc.vector.wait_ge(sp.h, trm)
                                vtp_free[tb] = sv.tick(nc.vector.tensor_copy(
                                    vtms[:, h4 * T + j * 128: h4 * T + (j + 1) * 128], vtp[tb][:, :]))
                    khh_mark[khh] = sp.n
                    if khh + 2 < 8:
                        nc.gpsimd.wait_ge(sp.h, khh_mark[khh])
                        issue_kdma(khh + 2)
                    if khh % 2 == 1:
                        nc.scalar.wait_ge(sv.h, sv.n)
                        if not is_v:
                            sg.tick(nc.scalar.dma_start(ccKi.ap()[:, kg * 4096:(kg + 1) * 4096], kstg[:]), 16)
                        else:
                            sg.tick(nc.scalar.dma_start(ccVi.ap()[:, (kg - 2) * 4096:(kg - 1) * 4096], vtms[:]), 16)
        kvscope.__exit__(None, None, None)
        nc.gpsimd.wait_ge(sg.h, sg.n)
        groups = [[0, 1], [2, 3], [4, 5], [6, 7]]
        scc.tick(nc.gpsimd.collective_compute(
            "AllGather", ALU.bypass, replica_groups=groups,
            ins=[ccKi.ap().opt()], outs=[ccKo.ap().opt()]), 1)
        scc.tick(nc.gpsimd.collective_compute(
            "AllGather", ALU.bypass, replica_groups=groups,
            ins=[ccVi.ap().opt()], outs=[ccVo.ap().opt()]), 1)

        barrier_all()
        atscope = scope("attn"); atscope.__enter__()
        with ExitStack() as ast:
            asb = lambda n_, s, d: ast.enter_context(nc.sbuf_tensor(n_, s, d))
            kbufs = [asb(f"kbuf{i}", [128, 2048], BF16) for i in range(2)]
            vbufs = [asb(f"vbuf{i}", [128, 2048], BF16) for i in range(2)]
            wq = wstream
            qh = [asb(f"qh{i}", [128, T], BF16) for i in range(2)]
            pt = [asb(f"pt{i}", [128, T], BF16) for i in range(3)]
            qtmp = asb("qtmp", [128, T], BF16)
            qswap = asb("qswap", [128, T], BF16)
            rz_sb = asb("rz_sb", [128, T], BF16)
            rzrow = asb("rzrow", [1, T], F32)
            rz2row = asb("rz2row", [1, T], BF16)

            def load_wq(g):
                for c in range(NCH):
                    sw.tick(nc.gpsimd.dma_start(
                        wq[:, c * 512:(c + 1) * 512],
                        WIN[c * 128:(c + 1) * 128, g * 512:(g + 1) * 512]), 16)
                return sw.n

            def q_make(Qp, g2, dst):
                qm_ = dm_ = 0
                for tc in range(2):
                    last = None
                    for c in range(NCH):
                        last = nc.tensor.matmul(
                            Qp[:, tc * 512:(tc + 1) * 512],
                            lhsT=wq[:, c * 512 + g2 * 128: c * 512 + (g2 + 1) * 128],
                            rhs=Bb[:, c * T + tc * 512: c * T + tc * 512 + 512],
                            start=(c == 0), stop=(c == NCH - 1))
                    qm_ = sp.tick(last)
                    nc.scalar.wait_ge(sp.h, qm_)
                    am_, dm_ = rope_evac(Qp[:, tc * 512:(tc + 1) * 512], dst[:, tc * 512:(tc + 1) * 512],
                                         512, qtmp, qswap, t0=tc * 512)
                return qm_, dm_

            wq_mark = load_wq(0)
            kv_marks = [0, 0]

            def load_kv(kh_):
                kb, vb = kbufs[kh_ % 2], vbufs[kh_ % 2]
                nc.sync.wait_ge(sp.h, sp.n)
                ss.tick(nc.sync.dma_start(kb[:, 0:1024], ccKo.ap()[0:128, kh_ * 1024:(kh_ + 1) * 1024]), 16)
                ss.tick(nc.sync.dma_start(kb[:, 1024:2048], ccKo.ap()[128:256, kh_ * 1024:(kh_ + 1) * 1024]), 16)
                ss.tick(nc.sync.dma_start(vb[:, 0:1024], ccVo.ap()[0:128, kh_ * 1024:(kh_ + 1) * 1024]), 16)
                kv_marks[kh_ % 2] = ss.tick(nc.sync.dma_start(vb[:, 1024:2048], ccVo.ap()[128:256, kh_ * 1024:(kh_ + 1) * 1024]), 16)
            qrope_marks = {}
            qrope_amarks = {}
            with ExitStack() as pst0:
                Qp0 = pst0.enter_context(nc.psum_tensor("Qpre", [128, T], F32))
                nc.tensor.wait_ge(sw.h, wq_mark)
                nc.tensor.wait_ge(sv.h, sv.n)
                nc.tensor.wait_ge(sa.h, sa.n)
                _, dm = q_make(Qp0, 0, qh[0])
                qrope_marks[0] = dm

            aevac_mark = 0
            rzc_mark = 0
            ln_mark = 0
            wq_loaded_group = 0
            for h in range(NQH):
                kh, g2 = h // 4, h % 4
                cur = h % 2
                kbuf, vbuf = kbufs[kh % 2], vbufs[kh % 2]
                if h == 0:
                    nc.sync.wait_ge(scc.h, 2)
                    load_kv(0)
                    load_kv(1)
                if g2 == 0 and 1 <= kh < NKV - 1:
                    load_kv(kh + 1)

                with ExitStack() as past:
                    attnp = past.enter_context(nc.psum_tensor(f"attnp{h}", [128, T], F32))
                    Zp = past.enter_context(nc.psum_tensor(f"Zp{h}", [1, T], F32))
                    exp_marks = {}
                    zpv_marks = {}
                    with ExitStack() as pst:
                        sT = [pst.enter_context(nc.psum_tensor(f"sT{i}_{h}", [128, T], F32)) for i in range(2)]
                        nc.tensor.wait_ge(sv.h, qrope_marks[h])
                        if rzc_mark:
                            nc.tensor.wait_ge(sv.h, rzc_mark)
                        if g2 == 0:
                            nc.tensor.wait_ge(ss.h, kv_marks[kh % 2])

                        def emit_zpv(idx):
                            i_, r_ = idx // 2, idx % 2
                            a_ = 128 * i_
                            mem = r_ * 8 + i_
                            nc.tensor.wait_ge(sa.h, exp_marks[idx])
                            if idx == 0:
                                if aevac_mark:
                                    nc.tensor.wait_ge(sv.h, aevac_mark)
                                if ln_mark:
                                    nc.tensor.wait_ge(sa.h, ln_mark)
                            last_ = None
                            for lo, hi in ((a_, 512), (max(a_, 512), 1024)):
                                if lo >= hi:
                                    continue
                                nc.tensor.matmul(Zp[0:1, lo:hi], lhsT=ones_c[:],
                                                 rhs=pt[idx % 3][:, lo - a_:hi - a_],
                                                 start=(idx == 0), stop=(idx == 15))
                                last_ = nc.tensor.matmul(attnp[:, lo:hi],
                                                         lhsT=vbuf[:, mem * 128:(mem + 1) * 128],
                                                         rhs=pt[idx % 3][:, lo - a_:hi - a_],
                                                         start=(idx == 0), stop=(idx == 15))
                            zpv_marks[idx] = sp.tick(last_)

                        for idx in range(16):
                            i, r = idx // 2, idx % 2
                            a = 128 * i
                            par = idx % 2
                            if idx >= 2:
                                nc.tensor.wait_ge(sa.h, exp_marks[idx - 2])
                            last = None
                            for lo, hi in ((a, 512), (max(a, 512), 1024)):
                                if lo >= hi:
                                    continue
                                last = nc.tensor.matmul(sT[par][:, lo:hi],
                                                        lhsT=kbuf[:, r * 1024 + a: r * 1024 + a + 128],
                                                        rhs=qh[cur][:, lo:hi], start=True, stop=True)
                            stm = sp.tick(last)
                            nc.vector.wait_ge(sp.h, stm)
                            mkm = sv.tick(nc.vector.tensor_tensor(
                                sT[par][:, a:a + 128], sT[par][:, a:a + 128],
                                (tri0 if r == 0 else tri1)[:], ALU.add))
                            nc.scalar.wait_ge(sv.h, mkm)
                            if idx >= 3:
                                nc.scalar.wait_ge(sp.h, zpv_marks[idx - 3])
                            exp_marks[idx] = sa.tick(nc.scalar.activation(
                                pt[idx % 3][:, 0:T - a], sT[par][:, a:T], AF.Exp, bias=EXPB, scale=SCALE))
                            if idx >= 2:
                                emit_zpv(idx - 2)
                        emit_zpv(14)
                        emit_zpv(15)

                    with ExitStack() as tst:
                        rzbp = tst.enter_context(nc.psum_tensor(f"rzbp{h}", [128, T], F32))
                        Qp = tst.enter_context(nc.psum_tensor(f"Qp{h}", [128, T], F32))
                        nc.scalar.wait_ge(sp.h, zpv_marks[15])
                        ln_mark = sa.tick(nc.scalar.activation(rzrow[0:1, :], Zp[0:1, :], AF.Ln))
                        nc.scalar.wait_ge(sa.h, ln_mark)
                        rzm = sa.tick(nc.scalar.activation(rz2row[0:1, :], rzrow[0:1, :], AF.Exp, scale=-1.0))
                        hn = h + 1
                        if hn < NQH:
                            nc.tensor.wait_ge(sw.h, wq_mark)
                            nc.tensor.wait_ge(sa.h, exp_marks[15])
                            qm, qdm = q_make(Qp, hn % 4, qh[hn % 2])
                            qrope_marks[hn] = qdm
                            if hn % 4 == 3 and hn + 1 < NQH:
                                nc.gpsimd.wait_ge(sp.h, qm)
                                wq_mark = load_wq((hn + 1) // 4)
                                wq_loaded_group = (hn + 1) // 4
                        nc.tensor.wait_ge(sa.h, rzm)
                        last = None
                        for reg in range(2):
                            last = nc.tensor.matmul(rzbp[:, reg * 512:(reg + 1) * 512], lhsT=ones_r[:],
                                                    rhs=rz2row[0:1, reg * 512:(reg + 1) * 512], start=True, stop=True)
                        rzbm = sp.tick(last)
                        nc.vector.wait_ge(sp.h, rzbm)
                        rzc_mark = sv.tick(nc.vector.tensor_copy(rz_sb[:], rzbp[:, :]))
                        if h == 0:
                            nc.vector.wait_ge(ss.h, spill_mark)
                        aevac_mark = sv.tick(nc.vector.tensor_tensor(
                            A[:, h * T:(h + 1) * T], attnp[:, :], rz_sb[:], ALU.mult))


        atscope.__exit__(None, None, None)
        barrier_all()
        opscope = scope("outproj"); opscope.__enter__()
        with ExitStack() as ost:
            xold = [ost.enter_context(nc.sbuf_tensor(f"xold{i}", [128, 4096], BF16)) for i in range(2)]
            with ExitStack() as pst:
                psums = [pst.enter_context(nc.psum_tensor(f"po{o}", [128, 512], F32)) for o in range(4)]
                psum_free = [0, 0, 0, 0]
                xold_marks = [0, 0]
                xold_free = [0, 0]
                oh_mark = {}
                w_marks = {}
                nc.sync.wait_ge(ss.h, spill_mark)
                nc.tensor.wait_ge(sv.h, sv.n)
                nc.tensor.wait_ge(sa.h, sa.n)
                wsrco = WOUT.rearrange("(c q) o -> q c o", q=128)
                def issue_odma(oh):
                    p_ = oh % 2
                    sw.tick(nc.gpsimd.dma_start(
                        wstream[:, p_ * 8192:(p_ + 1) * 8192].rearrange("q (c o) -> q c o", o=256),
                        wsrco[:, :, oh * 256:(oh + 1) * 256]), 16)
                    w_marks[oh] = sw.n
                issue_odma(0)
                issue_odma(1)
                for oh in range(16):
                    p_ = oh % 2
                    og = oh // 2
                    xb = og % 2
                    if oh % 2 == 0:
                        if xold_free[xb]:
                            nc.sync.wait_ge(sv.h, xold_free[xb])
                        xold_marks[xb] = ss.tick(nc.sync.dma_start(
                            xold[xb][:], spill[:, og * 4096:(og + 1) * 4096]), 16)
                    nc.tensor.wait_ge(sw.h, w_marks[oh])
                    for tc in range(2):
                        for o2 in range(2):
                            op = oh * 2 + o2
                            o = (op % 4)
                            pi = (oh % 2) * 2 + o2
                            if psum_free[pi]:
                                nc.tensor.wait_ge(sv.h, psum_free[pi])
                            last = None
                            for c in range(NCH):
                                last = nc.tensor.matmul(
                                    psums[pi][:, :],
                                    lhsT=wstream[:, p_ * 8192 + c * 256 + o2 * 128: p_ * 8192 + c * 256 + (o2 + 1) * 128],
                                    rhs=A[:, c * T + tc * 512: c * T + tc * 512 + 512],
                                    start=(c == 0), stop=(c == NCH - 1))
                            gm = sp.tick(last)
                            nc.vector.wait_ge(sp.h, gm)
                            if tc == 0 and o2 == 0 and oh % 2 == 0:
                                nc.vector.wait_ge(ss.h, xold_marks[xb])
                            dsl = slice(op * T + tc * 512, op * T + tc * 512 + 512)
                            lastv = nc.vector.tensor_tensor(
                                Bb[:, dsl], psums[pi][:, :], xold[xb][:, (o * 2 + tc) * 512:(o * 2 + tc + 1) * 512],
                                ALU.add)
                            psum_free[pi] = sv.tick(lastv)
                    for pi2 in range(4):
                        xold_free[xb] = max(xold_free[xb], psum_free[pi2])
                    oh_mark[oh] = sp.n
                    if oh + 2 < 16:
                        nc.gpsimd.wait_ge(sp.h, oh_mark[oh])
                        issue_odma(oh + 2)

        if stage == 3:
            nc.sync.wait_ge(sv.h, sv.n)
            ss.tick(nc.sync.dma_start(DBG, Bb[:, :]), 16)
            nc.sync.wait_ge(ss.h, ss.n)
            return nc
        if stage == 34:
            nc.sync.wait_ge(sv.h, sv.n)
            ss.tick(nc.sync.dma_start(DBG, Bb[:, :]), 16)

        opscope.__exit__(None, None, None)
        with scope("ln3"):
            ln(Bb, A)
        nc.tensor.wait_ge(sv.h, sv.n)
        if stage == 35:
            nc.sync.wait_ge(sv.h, sv.n)
            ss.tick(nc.sync.dma_start(DBG, A[:, :]), 16)
        if stage == 36:
            ffn(A, Bb, False, "f2")
            nc.sync.wait_ge(sv.h, sv.n)
            ss.tick(nc.sync.dma_start(DBG, Bb[:, :]), 16)
            nc.sync.wait_ge(ss.h, ss.n)
            return nc
        with scope("ffn2"):
            ffn(A, Bb, True, "f2")

        nc.sync.wait_ge(ss.h, ss.n)
    return nc



def _host_tables():
    import jax
    import jax.numpy as jnp
    with jax.default_device(jax.devices("cpu")[0]):
        inv = 1.0 / (10000.0 ** (jnp.arange(0, HD, 2, dtype=jnp.float32) / HD))
        ang = jnp.arange(S, dtype=jnp.float32)[:, None] * inv[None, :]
        cosf = np.asarray(jnp.cos(ang), dtype=np.float32)
        sinf = np.asarray(jnp.sin(ang), dtype=np.float32)
    return cosf, sinf


def kernel(L0_attn_x, L0_attn_w_in, L0_attn_w_out, L0_ffn_base_W, L0_ffn_gate_W,
           L0_ffn_lora_A, L0_ffn_lora_B, _stage=4):
    x = np.asarray(L0_attn_x, np.float32)
    w_in = np.asarray(L0_attn_w_in, np.float32)
    w_out = np.asarray(L0_attn_w_out, np.float32)
    base_W = np.asarray(L0_ffn_base_W, np.float32)
    gate_W = np.asarray(L0_ffn_gate_W, np.float32)
    lora_A = np.asarray(L0_ffn_lora_A, np.float32)
    lora_B = np.asarray(L0_ffn_lora_B, np.float32)

    WIN = np.ascontiguousarray(w_in.T)
    WOUT = np.ascontiguousarray(w_out.T)
    BW = np.ascontiguousarray(base_W.T)
    AGW = np.ascontiguousarray(np.concatenate([lora_A.reshape(224, E).T, gate_W.T], axis=1))
    BRE = np.ascontiguousarray(lora_B.transpose(0, 2, 1).reshape(224, E))
    IDENT = np.eye(128, dtype=ml_dtypes.bfloat16)
    CONSTS = np.zeros((128, 8), np.float32)
    for i, v in enumerate([EXPB, SCALE, -1.0, EPS, -0.5]):
        CONSTS[:, i] = v

    cosf, sinf = _host_tables()
    qgrid, kgrid = np.meshgrid(np.arange(128), np.arange(128), indexing="xy")
    kk = np.arange(128)[:, None]
    qq = np.arange(128)[None, :]
    TRI = np.where(kk <= qq, 0.0, NEG).astype(np.float32)
    ZER = np.zeros((128, 128), np.float32)
    FULLM = np.full((128, 128), NEG, np.float32)

    nc = build_program(stage=_stage)
    in_maps = []
    pos_by_p = {}
    for p in range(2):
        pos = np.concatenate([np.arange(256 * j + 128 * p, 256 * j + 128 * p + 128) for j in range(8)])
        pos_by_p[p] = pos
    tables = {}
    for p in range(2):
        pos = pos_by_p[p]
        cd = np.empty((128, T), np.float32)
        sd = np.empty((128, T), np.float32)
        cd[0:64, :] = cosf[pos, :].T
        cd[64:128, :] = cosf[pos, :].T
        sd[0:64, :] = -sinf[pos, :].T
        sd[64:128, :] = sinf[pos, :].T
        tables[p] = (np.ascontiguousarray(cd), np.ascontiguousarray(sd))

    for c in range(8):
        b, p = c // 2, c % 2
        pos = pos_by_p[p]
        xT = np.ascontiguousarray(x[b][pos, :].T)
        cd, sd = tables[p]
        in_maps.append({
            "XT": xT, "WIN": WIN, "WOUT": WOUT, "BW": BW, "AGW": AGW, "BRE": BRE,
            "COS": cd, "SIN": sd,
            "MK0": TRI if p == 0 else ZER,
            "MK1": FULLM if p == 0 else TRI,
            "CONSTS": CONSTS, "IDENT": IDENT,
        })

    res = run_bass_kernel_spmd(nc, in_maps, core_ids=list(range(8)))

    if _stage != 4:
        return res

    out = np.empty((B, S, E), np.float32)
    for c in range(8):
        b, p = c // 2, c % 2
        o = np.asarray(res.results[c]["OUT"], np.float32)
        for j in range(8):
            out[b, 256 * j + 128 * p: 256 * j + 128 * p + 128, :] = o[:, j * 128:(j + 1) * 128].T
    return out


# version 05
# speedup vs baseline: 1.3128x; 1.0312x over previous
import os
import numpy as np
import ml_dtypes
from contextlib import ExitStack

import concourse.bass as bass
import concourse.mybir as mybir
from concourse.bass_utils import run_bass_kernel_spmd

F32 = mybir.dt.float32
BF16 = mybir.dt.bfloat16
AF = mybir.ActivationFunctionType
ALU = mybir.AluOpType
AX = mybir.AxisListType

B, S, E = 4, 2048, 4096
T = 1024
NCH = 32
NQH, NKV, HD = 32, 8, 128
NE, RANK = 28, 8
NEG = -1e30

SCALE = float(np.float32(1.0) / np.sqrt(np.float32(128.0)))
EXPB = -20.0
EPS = 1e-5
C_INV4096 = float(np.float32(1.0 / 4096.0))


class Sem:
    def __init__(self, nc, stack, name):
        self.h = stack.enter_context(nc.semaphore(name))
        self.n = 0

    def tick(self, ins, k=1):
        ins.then_inc(self.h, k)
        self.n += k
        return self.n


def build_program(stage=4):
    nc = bass.Bass("TRN2", target_bir_lowering=False)

    XT = nc.dram_tensor("XT", [E, T], F32, kind="ExternalInput").ap()
    WIN = nc.dram_tensor("WIN", [E, 6144], F32, kind="ExternalInput").ap()
    WOUT = nc.dram_tensor("WOUT", [E, E], F32, kind="ExternalInput").ap()
    BW = nc.dram_tensor("BW", [E, E], F32, kind="ExternalInput").ap()
    AGW = nc.dram_tensor("AGW", [E, 252], F32, kind="ExternalInput").ap()
    BRE = nc.dram_tensor("BRE", [224, E], F32, kind="ExternalInput").ap()
    COS = nc.dram_tensor("COS", [128, T], F32, kind="ExternalInput").ap()
    SIN = nc.dram_tensor("SIN", [128, T], F32, kind="ExternalInput").ap()
    MK0 = nc.dram_tensor("MK0", [128, 128], F32, kind="ExternalInput").ap()
    MK1 = nc.dram_tensor("MK1", [128, 128], F32, kind="ExternalInput").ap()
    CONSTS = nc.dram_tensor("CONSTS", [128, 8], F32, kind="ExternalInput").ap()
    IDENT = nc.dram_tensor("IDENT", [128, 128], BF16, kind="ExternalInput").ap()
    OUT = nc.dram_tensor("OUT", [E, T], F32, kind="ExternalOutput").ap() if stage >= 4 else None
    DBG = nc.dram_tensor("DBG", [128, NCH * T], BF16, kind="ExternalOutput").ap() if stage != 4 else None

    spill = nc.dram_tensor("spill", [128, NCH * T], BF16, kind="Internal").ap()
    ccKi = nc.dram_tensor("ccKi", [128, 8192], BF16, kind="Internal")
    ccVi = nc.dram_tensor("ccVi", [128, 8192], BF16, kind="Internal")
    ccKo = nc.dram_tensor("ccKo", [256, 8192], BF16, kind="Internal")
    ccVo = nc.dram_tensor("ccVo", [256, 8192], BF16, kind="Internal")

    with ExitStack() as st:
        def sb(name, shape, dt):
            return st.enter_context(nc.sbuf_tensor(name, shape, dt))

        A = sb("A", [128, NCH * T], BF16)
        Bb = sb("Bb", [128, NCH * T], BF16)
        cosq = sb("cosq", [128, T], BF16)
        sinq = sb("sinq", [128, T], BF16)
        tri0 = sb("tri0", [128, 128], F32)
        tri1 = sb("tri1", [128, 128], F32)
        consts = sb("consts", [128, 8], F32)
        ident = sb("ident", [128, 128], BF16)
        ones_c = sb("ones_c", [128, 1], BF16)
        ones_r = sb("ones_r", [1, 128], BF16)
        wstream = sb("wstream", [128, NCH * 512], BF16)

        nc.const_aps.register(F32, EXPB, consts[:, 0:1])
        nc.const_aps.register(F32, SCALE, consts[:, 1:2])
        nc.const_aps.register(F32, -1.0, consts[:, 2:3])
        nc.const_aps.register(F32, EPS, consts[:, 3:4])
        nc.const_aps.register(F32, -0.5, consts[:, 4:5])

        sw = Sem(nc, st, "sw")
        ss = Sem(nc, st, "ss")
        sg = Sem(nc, st, "sg")
        sp = Sem(nc, st, "sp")
        sv = Sem(nc, st, "sv")
        sa = Sem(nc, st, "sa")
        scc = Sem(nc, st, "scc")

        ss.tick(nc.sync.dma_start(consts[:], CONSTS), 16)
        ss.tick(nc.sync.dma_start(tri0[:], MK0), 16)
        ss.tick(nc.sync.dma_start(tri1[:], MK1), 16)
        ss.tick(nc.sync.dma_start(ident[:], IDENT), 16)
        sw.tick(nc.gpsimd.dma_start(cosq[:], COS), 16)
        sw.tick(nc.gpsimd.dma_start(sinq[:], SIN), 16)
        x_marks = []
        for c in range(NCH):
            sw.tick(nc.gpsimd.dma_start(A[:, c * T:(c + 1) * T], XT[c * 128:(c + 1) * 128, :]), 16)
            x_marks.append(sw.n)
        sv.tick(nc.vector.memset(ones_c[:], 1.0))
        sv.tick(nc.vector.memset(ones_r[:], 1.0))
        base_loads = dict(sw=sw.n, ss=ss.n, sv=sv.n)

        nc.scalar.wait_ge(ss.h, base_loads["ss"])
        nc.vector.wait_ge(ss.h, base_loads["ss"])
        nc.tensor.wait_ge(sv.h, base_loads["sv"])

        def barrier(*engs):
            for eng in engs:
                eng.wait_ge(sp.h, sp.n)
                eng.wait_ge(sv.h, sv.n)
                eng.wait_ge(sa.h, sa.n)
                eng.wait_ge(sw.h, sw.n)
                eng.wait_ge(ss.h, ss.n)
                eng.wait_ge(sg.h, sg.n)

        def barrier_all():
            barrier(nc.gpsimd, nc.sync, nc.scalar, nc.vector, nc.tensor)


        class _LnCnt:
            n = 0

        def ln(src, dst, src_marks=None):
            barrier(nc.tensor, nc.vector, nc.scalar)
            _LnCnt.n += 1
            u = f"_l{_LnCnt.n}"
            with ExitStack() as lst:
                xsq = [lst.enter_context(nc.sbuf_tensor(f"xsq{i}{u}", [128, T], BF16)) for i in range(2)]
                m_row = lst.enter_context(nc.sbuf_tensor("m_row" + u, [1, T], F32))
                v_row = lst.enter_context(nc.sbuf_tensor("v_row" + u, [1, T], F32))
                t_row = lst.enter_context(nc.sbuf_tensor("t_row" + u, [1, T], F32))
                rstd_row = lst.enter_context(nc.sbuf_tensor("rstd_row" + u, [1, T], BF16))
                mr_row = lst.enter_context(nc.sbuf_tensor("mr_row" + u, [1, T], BF16))
                rstd_b = lst.enter_context(nc.sbuf_tensor("rstd_b" + u, [128, T], BF16))
                mrb_b = lst.enter_context(nc.sbuf_tensor("mrb_b" + u, [128, T], BF16))
                with ExitStack() as pst:
                    Sp = pst.enter_context(nc.psum_tensor("Sp" + u, [1, T], F32))
                    Sq = pst.enter_context(nc.psum_tensor("Sq" + u, [1, T], F32))
                    sq_marks = [0, 0]
                    for c in range(NCH):
                        if c >= 2:
                            nc.scalar.wait_ge(sp.h, sq_marks[c % 2])
                        if src_marks is not None:
                            nc.scalar.wait_ge(src_marks[0].h, src_marks[1][c])
                            nc.tensor.wait_ge(src_marks[0].h, src_marks[1][c])
                        sqm = sa.tick(nc.scalar.activation(xsq[c % 2][:], src[:, c * T:(c + 1) * T], AF.Square))
                        for reg in range(2):
                            nc.tensor.matmul(Sp[0:1, reg * 512:(reg + 1) * 512], lhsT=ones_c[:],
                                             rhs=src[:, c * T + reg * 512: c * T + (reg + 1) * 512],
                                             start=(c == 0), stop=(c == NCH - 1))
                        nc.tensor.wait_ge(sa.h, sqm)
                        last = None
                        for reg in range(2):
                            last = nc.tensor.matmul(Sq[0:1, reg * 512:(reg + 1) * 512], lhsT=ones_c[:],
                                                    rhs=xsq[c % 2][:, reg * 512:(reg + 1) * 512],
                                                    start=(c == 0), stop=(c == NCH - 1))
                        sq_marks[c % 2] = sp.tick(last)
                    stats_done = sp.n
                    nc.vector.wait_ge(sp.h, stats_done)
                    nc.vector.tensor_scalar_mul(m_row[0:1, :], Sp[0:1, :], C_INV4096)
                    nc.vector.tensor_scalar_mul(v_row[0:1, :], Sq[0:1, :], C_INV4096)
                    nc.vector.tensor_tensor(t_row[0:1, :], m_row[0:1, :], m_row[0:1, :], ALU.mult)
                    rows1 = sv.tick(nc.vector.tensor_tensor(v_row[0:1, :], v_row[0:1, :], t_row[0:1, :], ALU.subtract))
                    nc.scalar.wait_ge(sv.h, rows1)
                    lnm = sa.tick(nc.scalar.activation(t_row[0:1, :], v_row[0:1, :], AF.Ln, bias=EPS))
                    nc.scalar.wait_ge(sa.h, lnm)
                    rsm = sa.tick(nc.scalar.activation(rstd_row[0:1, :], t_row[0:1, :], AF.Exp, scale=-0.5))
                    nc.vector.wait_ge(sa.h, rsm)
                    mrm = sv.tick(nc.vector.tensor_tensor(mr_row[0:1, :], m_row[0:1, :], rstd_row[0:1, :], ALU.mult))
                with ExitStack() as pst:
                    bp0 = pst.enter_context(nc.psum_tensor("bp0" + u, [128, T], F32))
                    bp1 = pst.enter_context(nc.psum_tensor("bp1" + u, [128, T], F32))
                    nc.tensor.wait_ge(sv.h, mrm)
                    last = None
                    for reg in range(2):
                        nc.tensor.matmul(bp0[:, reg * 512:(reg + 1) * 512], lhsT=ones_r[:],
                                         rhs=rstd_row[0:1, reg * 512:(reg + 1) * 512], start=True, stop=True)
                        last = nc.tensor.matmul(bp1[:, reg * 512:(reg + 1) * 512], lhsT=ones_r[:],
                                                rhs=mr_row[0:1, reg * 512:(reg + 1) * 512], start=True, stop=True)
                    bm = sp.tick(last)
                    nc.vector.wait_ge(sp.h, bm)
                    nc.vector.tensor_copy(rstd_b[:], bp0[:, :])
                    bcm = sv.tick(nc.vector.tensor_copy(mrb_b[:], bp1[:, :]))
                    if os.environ.get("KDUMPLN", "") == "1" and _LnCnt.n == int(os.environ.get("KDUMPLN_N", "3")) and DBG is not None:
                        lrows = lst.enter_context(nc.sbuf_tensor("lrows" + u, [1, 3072], F32))
                        nc.vector.tensor_copy(lrows[0:1, 0:1024], rstd_row[0:1, :])
                        nc.vector.tensor_copy(lrows[0:1, 1024:2048], m_row[0:1, :])
                        lrm = sv.tick(nc.vector.tensor_copy(lrows[0:1, 2048:3072], v_row[0:1, :]))
                        nc.gpsimd.wait_ge(sv.h, lrm)
                        sw.tick(nc.gpsimd.dma_start(DBG[0:1, 8192:11264], lrows[0:1, :]), 16)
                        sw.tick(nc.gpsimd.dma_start(DBG[:, 11264:12288], rstd_b[:, :]), 16)
                    nc.vector.wait_ge(sv.h, bcm)
                    nslice = []
                    for sl in range(4):
                        last = None
                        for c in range(NCH):
                            cs = slice(c * T + sl * 256, c * T + (sl + 1) * 256)
                            ts_ = slice(sl * 256, (sl + 1) * 256)
                            nc.vector.tensor_tensor(dst[:, cs], src[:, cs], rstd_b[:, ts_], ALU.mult)
                            last = nc.vector.tensor_tensor(dst[:, cs], dst[:, cs], mrb_b[:, ts_], ALU.subtract)
                        nslice.append(sv.tick(last))
                    return dict(bcm=bcm, nslice=nslice, final=nslice[-1])

        def rope_evac(psum_ap, dst_ap, width, qtmp, qswap, t0=0):
            qm = sa.tick(nc.scalar.activation(qtmp[:, t0:t0 + width], psum_ap, AF.Copy))
            nc.vector.wait_ge(sa.h, qm)
            nc.vector.tensor_copy(qswap[0:64, t0:t0 + width], qtmp[64:128, t0:t0 + width])
            nc.vector.tensor_copy(qswap[64:128, t0:t0 + width], qtmp[0:64, t0:t0 + width])
            nc.vector.tensor_tensor(qtmp[:, t0:t0 + width], qtmp[:, t0:t0 + width], cosq[:, t0:t0 + width], ALU.mult)
            nc.vector.tensor_tensor(qswap[:, t0:t0 + width], qswap[:, t0:t0 + width], sinq[:, t0:t0 + width], ALU.mult)
            dm = sv.tick(nc.vector.tensor_tensor(dst_ap, qtmp[:, t0:t0 + width], qswap[:, t0:t0 + width], ALU.add))
            return qm, dm

        def ffn(src, resid, is_final, tag, parts=3, ym=None):
            if ym is None:
                barrier_all()
            else:
                nc.tensor.wait_ge(sv.h, ym["bcm"])
                nc.tensor.wait_ge(sa.h, sa.n)
                nc.gpsimd.wait_ge(sp.h, sp.n)
                nc.gpsimd.wait_ge(sa.h, sa.n)
                nc.gpsimd.wait_ge(sv.h, ym["bcm"])
            with ExitStack() as fst:
                fsb = lambda n_, s, d: fst.enter_context(nc.sbuf_tensor(n_ + tag, s, d))
                bres0 = fsb("bres0", [112, E], BF16)
                bres1 = fsb("bres1", [112, E], BF16)
                hsT0 = fsb("hsT0", [112, T], BF16)
                hsT1 = fsb("hsT1", [112, T], BF16)
                g_sb = fsb("g_sb", [128, NE], F32)
                gx = fsb("gx", [128, 224], F32)
                eq = fsb("eq", [128, NE], F32)
                msk = fsb("msk", [128, NE], F32)
                ge_m = fsb("ge_m", [128, NE], F32)
                en = fsb("en", [128, NE], F32)
                hs = fsb("hs", [128, 224], BF16)
                m1 = fsb("m1", [128, 1], F32)
                negm1 = fsb("negm1", [128, 1], F32)
                m2 = fsb("m2", [128, 1], F32)
                e2 = fsb("e2", [128, 1], F32)
                den = fsb("den", [128, 1], F32)
                rec = fsb("rec", [128, 1], F32)

                agres = wstream[:, 0:NCH * 252]
                sw.tick(nc.gpsimd.dma_start(
                    agres.rearrange("q (c o) -> q c o", o=252),
                    AGW.rearrange("(c q) o -> q c o", q=128)), 16)
                ag_mark = sw.n
                if ym is not None:
                    nc.gpsimd.wait_ge(sv.h, ym["final"])
                sw.tick(nc.gpsimd.dma_start(bres0[:], BRE[0:112, :]), 16)
                bres_mark = sw.tick(nc.gpsimd.dma_start(bres1[:], BRE[112:224, :]), 16)

                if not (parts & 1):
                    hsT_mark = sv.tick(nc.vector.memset(hsT0[:], 0.0))
                    hsT_mark = sv.tick(nc.vector.memset(hsT1[:], 0.0))
                elif True:
                  with ExitStack() as hst:
                    with ExitStack() as pst:
                        hps = [pst.enter_context(nc.psum_tensor(f"hp{i}" + tag, [128, 252], F32)) for i in range(2)]
                        hT0 = pst.enter_context(nc.psum_tensor("hT0" + tag, [112, 128], BF16))
                        hT1 = pst.enter_context(nc.psum_tensor("hT1" + tag, [112, 128], BF16))
                        nc.tensor.wait_ge(sw.h, ag_mark)
                        hp_frees = [0, 0]
                        hT_free = 0
                        for tt in range(int(os.environ.get("KTT", "8"))):
                            hp = hps[tt % 2]
                            if ym is not None:
                                nc.tensor.wait_ge(sv.h, ym["nslice"][tt // 2])
                            if hp_frees[tt % 2]:
                                nc.tensor.wait_ge(sv.h, hp_frees[tt % 2])
                            last = None
                            for c in range(NCH):
                                last = nc.tensor.matmul(
                                    hp[:, 0:252],
                                    lhsT=src[:, c * T + tt * 128: c * T + (tt + 1) * 128],
                                    rhs=agres[:, c * 252:(c + 1) * 252],
                                    start=(c == 0), stop=(c == NCH - 1))
                            hpm = sp.tick(last)
                            logits = hp[:, 224:252]
                            nc.vector.wait_ge(sp.h, hpm)
                            if ffn.simple_gate:
                                nc.vector.memset(gx[:], 0.001)
                                hsm = sv.tick(nc.vector.tensor_tensor(hs[:], hp[:, 0:224], gx[:], ALU.mult))
                                hp_frees[tt % 2] = hsm
                                if ffn.skip_transpose:
                                    nc.vector.wait_ge(sv.h, hsm)
                                    nc.vector.tensor_copy(hsT0[:, tt * 128:(tt + 1) * 128], hs[0:112, 0:128])
                                    hT_free = sv.tick(nc.vector.tensor_copy(hsT1[:, tt * 128:(tt + 1) * 128], hs[0:112, 96:224]))
                                    continue
                                nc.tensor.wait_ge(sv.h, hsm)
                                if tt > 0:
                                    nc.tensor.wait_ge(sv.h, hT_free)
                                nc.tensor.transpose(hT0[:, :], hs[:, 0:112], ident[:])
                                trm = sp.tick(nc.tensor.transpose(hT1[:, :], hs[:, 112:224], ident[:]))
                                nc.vector.wait_ge(sp.h, trm)
                                nc.vector.tensor_copy(hsT0[:, tt * 128:(tt + 1) * 128], hT0[:, :])
                                hT_free = sv.tick(nc.vector.tensor_copy(hsT1[:, tt * 128:(tt + 1) * 128], hT1[:, :]))
                                continue
                            m1m = sv.tick(nc.vector.tensor_reduce(m1[:], logits, AX.X, ALU.max))
                            nc.vector.wait_ge(sv.h, m1m)
                            nc.vector.tensor_scalar(eq[:], logits, m1[:], None, ALU.is_equal)
                            nc.vector.tensor_scalar_mul(negm1[:], m1[:], -1.0)
                            nc.vector.scalar_tensor_tensor(msk[:], eq[:], NEG, logits, ALU.mult, ALU.add)
                            m2m = sv.tick(nc.vector.tensor_reduce(m2[:], msk[:], AX.X, ALU.max))
                            nc.vector.wait_ge(sv.h, m2m)
                            nm = sv.tick(nc.vector.tensor_scalar(ge_m[:], logits, m2[:], None, ALU.is_ge))
                            nc.scalar.wait_ge(sv.h, nm)
                            enm = sa.tick(nc.scalar.activation(en[:], logits, AF.Exp, bias=negm1[:, 0:1]))
                            e2m = sa.tick(nc.scalar.activation(e2[:], m2[:], AF.Exp, bias=negm1[:, 0:1]))
                            nc.vector.wait_ge(sa.h, e2m)
                            dnm = sv.tick(nc.vector.tensor_scalar_add(den[:], e2[:], 1.0))
                            nc.vector.wait_ge(sv.h, dnm)
                            rcm = sv.tick(nc.vector.reciprocal(rec[:], den[:]))
                            nc.vector.tensor_tensor(g_sb[:], en[:], ge_m[:], ALU.mult)
                            nc.vector.wait_ge(sv.h, rcm)
                            nc.vector.tensor_scalar(g_sb[:], g_sb[:], rec[:], None, ALU.mult)
                            for rr in range(8):
                                nc.vector.tensor_copy(gx[:, rr:224:8], g_sb[:])
                            if os.environ.get("KDUMPG", "") == "1" and tag == "f2" and tt == 0 and DBG is not None:
                                sw.tick(nc.gpsimd.dma_start(DBG[:, 4096:4096 + 252], agres[:, 0:252]), 16)
                                sw.tick(nc.gpsimd.dma_start(DBG[:, 4352:4352 + 1024], src[:, 0:1024]), 16)
                                gdump = hst.enter_context(nc.sbuf_tensor("gdump", [128, 64], F32))
                                nc.vector.tensor_copy(gdump[:, 0:28], logits)
                                nc.vector.tensor_copy(gdump[:, 28:56], g_sb[:])
                                nc.vector.tensor_copy(gdump[:, 56:57], m1[:])
                                nc.vector.tensor_copy(gdump[:, 57:58], m2[:])
                                nc.vector.tensor_copy(gdump[:, 58:59], rec[:])
                                gdm = sv.tick(nc.vector.tensor_copy(gdump[:, 59:60], negm1[:]))
                                nc.gpsimd.wait_ge(sv.h, gdm)
                                sw.tick(nc.gpsimd.dma_start(DBG[:, 2048:2112], gdump[:, :]), 16)
                            hsm = sv.tick(nc.vector.tensor_tensor(hs[:], hp[:, 0:224], gx[:], ALU.mult))
                            hp_frees[tt % 2] = hsm
                            nc.tensor.wait_ge(sv.h, hsm)
                            if tt > 0:
                                nc.tensor.wait_ge(sv.h, hT_free)
                            nc.tensor.transpose(hT0[:, :], hs[:, 0:112], ident[:])
                            trm = sp.tick(nc.tensor.transpose(hT1[:, :], hs[:, 112:224], ident[:]))
                            nc.vector.wait_ge(sp.h, trm)
                            nc.vector.tensor_copy(hsT0[:, tt * 128:(tt + 1) * 128], hT0[:, :])
                            hT_free = sv.tick(nc.vector.tensor_copy(hsT1[:, tt * 128:(tt + 1) * 128], hT1[:, :]))
                        hsT_mark = hT_free
                if os.environ.get("KDUMPHST", "") == "1" and tag == "f2" and DBG is not None:
                    nc.sync.wait_ge(sv.h, sv.n)
                    ss.tick(nc.sync.dma_start(DBG[0:112, 0:1024], hsT0[:, :]), 16)
                    ss.tick(nc.sync.dma_start(DBG[0:112, 1024:2048], hsT1[:, :]), 16)

                if not (parts & 2):
                    return
                with ExitStack() as bst:
                    barrier(nc.gpsimd)
                    ofp = [bst.enter_context(nc.sbuf_tensor("ofp" + tag, [128, 512], F32))] if is_final else []
                    with ExitStack() as pst:
                        psums = [pst.enter_context(nc.psum_tensor(f"pb{o}" + tag, [128, 512], F32)) for o in range(4)]
                        psum_free = [0, 0, 0, 0]
                        ofp_free = [0]
                        oh_mark = {}
                        w_marks = {}
                        wsrc = BW.rearrange("(c q) o -> q c o", q=128)
                        def issue_wdma(oh):
                            p_ = oh % 2
                            sw.tick(nc.gpsimd.dma_start(
                                wstream[:, p_ * 8192:(p_ + 1) * 8192].rearrange("q (c o) -> q c o", o=256),
                                wsrc[:, :, oh * 256:(oh + 1) * 256]), 16)
                            w_marks[oh] = sw.n
                        issue_wdma(0)
                        issue_wdma(1)
                        for oh in range(16):
                            p_ = oh % 2
                            if oh >= 2:
                                pass
                            nc.tensor.wait_ge(sw.h, w_marks[oh])
                            if oh == 0:
                                nc.tensor.wait_ge(sw.h, bres_mark)
                                nc.tensor.wait_ge(sv.h, hsT_mark)
                            for tc in range(2):
                                for o2 in range(2):
                                    o = o2
                                    op = oh * 2 + o2
                                    pi = (oh % 2) * 2 + o2
                                    if psum_free[pi]:
                                        nc.tensor.wait_ge(sv.h, psum_free[pi])
                                    for c in range(NCH):
                                        nc.tensor.matmul(
                                            psums[pi][:, :],
                                            lhsT=wstream[:, p_ * 8192 + c * 256 + o2 * 128: p_ * 8192 + c * 256 + (o2 + 1) * 128],
                                            rhs=src[:, c * T + tc * 512: c * T + tc * 512 + 512],
                                            start=(c == 0), stop=False)
                                    nc.tensor.matmul(psums[pi][:, :], lhsT=bres0[:, op * 128:(op + 1) * 128],
                                                     rhs=hsT0[:, tc * 512:(tc + 1) * 512], start=False, stop=False)
                                    last = nc.tensor.matmul(psums[pi][:, :], lhsT=bres1[:, op * 128:(op + 1) * 128],
                                                            rhs=hsT1[:, tc * 512:(tc + 1) * 512], start=False, stop=True)
                                    gm = sp.tick(last)
                                    nc.vector.wait_ge(sp.h, gm)
                                    dsl = slice(op * T + tc * 512, op * T + tc * 512 + 512)
                                    if not is_final:
                                        psum_free[pi] = sv.tick(nc.vector.scalar_tensor_tensor(
                                            resid[:, dsl], psums[pi][:, :], 0.5, resid[:, dsl], ALU.mult, ALU.add))
                                    else:
                                        if ofp_free[0]:
                                            nc.vector.wait_ge(ss.h, ofp_free[0])
                                        em = sv.tick(nc.vector.scalar_tensor_tensor(
                                            ofp[0][:], psums[pi][:, :], 0.5, resid[:, dsl], ALU.mult, ALU.add))
                                        psum_free[pi] = em
                                        nc.sync.wait_ge(sv.h, em)
                                        ofp_free[0] = ss.tick(nc.sync.dma_start(
                                            OUT[op * 128:(op + 1) * 128, tc * 512:tc * 512 + 512], ofp[0][:]), 16)
                            oh_mark[oh] = sp.n
                            if oh + 2 < 16:
                                nc.gpsimd.wait_ge(sp.h, oh_mark[oh])
                                issue_wdma(oh + 2)


        ffn.store_cnt = 0
        ffn.simple_gate = False
        ffn.skip_transpose = False
        from contextlib import nullcontext
        scope = nc.named_scope if os.environ.get("KSCOPE", "") == "1" else (lambda n_: nullcontext())
        nc.vector.wait_ge(sw.h, base_loads["sw"])
        nc.scalar.wait_ge(sw.h, base_loads["sw"])
        nc.tensor.wait_ge(sw.h, base_loads["sw"])
        if stage == 0:
            nc.sync.wait_ge(sw.h, base_loads["sw"])
            ss.tick(nc.sync.dma_start(DBG, A[:, :]), 16)
            nc.sync.wait_ge(ss.h, ss.n)
            return nc
        if os.environ.get("KNOLN", "") == "1":
            for c in range(NCH):
                sv.tick(nc.vector.tensor_copy(Bb[:, c * T:(c + 1) * T], A[:, c * T:(c + 1) * T]))
        else:
            with scope("ln1"):
                lnm1 = ln(A, Bb, src_marks=(sw, x_marks))
        if stage == 1:
            nc.sync.wait_ge(sv.h, sv.n)
            ss.tick(nc.sync.dma_start(DBG, Bb[:, :]), 16)
            nc.sync.wait_ge(ss.h, ss.n)
            return nc

        ffn.simple_gate = (stage in (18, 19))
        ffn.skip_transpose = (stage == 19)
        _parts = {16: 1, 17: 2, 19: 1}.get(stage, 3)
        with scope("ffn1"):
            ffn(Bb, A, False, "f1", parts=_parts, ym=(lnm1 if os.environ.get("KNOLN", "") != "1" else None))
        ffn.simple_gate = False

        if stage in (15, 16, 17, 18, 19):
            nc.sync.wait_ge(sv.h, sv.n)
            ss.tick(nc.sync.dma_start(DBG, A[:, :]), 16)
            nc.sync.wait_ge(ss.h, ss.n)
            return nc
        if stage == 2:
            nc.sync.wait_ge(sv.h, sv.n)
            fm = ss.tick(nc.sync.dma_start(DBG, A[:, :]), 16)
            nc.sync.wait_ge(ss.h, ss.n)
            return nc

        if stage == 40:
            ln(A, Bb)
            nc.tensor.wait_ge(sv.h, sv.n)
            ffn(Bb, A, False, "f2")
            nc.sync.wait_ge(sv.h, sv.n)
            ss.tick(nc.sync.dma_start(DBG, A[:, :]), 16)
            nc.sync.wait_ge(ss.h, ss.n)
            return nc

        with scope("ln2"):
            lnm2 = ln(A, Bb)
        nc.sync.wait_ge(sv.h, lnm2["bcm"])
        spill_mark = ss.tick(nc.sync.dma_start(spill, A[:, :]), 16)

        nc.tensor.wait_ge(sv.h, lnm2["bcm"])
        nc.tensor.wait_ge(sa.h, sa.n)
        nc.gpsimd.wait_ge(sp.h, sp.n)
        nc.gpsimd.wait_ge(sa.h, sa.n)
        nc.gpsimd.wait_ge(sv.h, lnm2["bcm"])
        nc.scalar.wait_ge(sv.h, lnm2["bcm"])
        nc.vector.wait_ge(sv.h, lnm2["bcm"])
        kvscope = scope("kv"); kvscope.__enter__()
        with ExitStack() as kst:
            kstg = kst.enter_context(nc.sbuf_tensor("kstg", [128, 4096], BF16))
            vstg = kst.enter_context(nc.sbuf_tensor("vstg", [128, 4096], BF16))
            vtms = kst.enter_context(nc.sbuf_tensor("vtms", [128, 4096], BF16))
            qtmp0 = kst.enter_context(nc.sbuf_tensor("qtmp0", [128, T], BF16))
            qswap0 = kst.enter_context(nc.sbuf_tensor("qswap0", [128, T], BF16))
            with ExitStack() as pst:
                kp = [pst.enter_context(nc.psum_tensor(f"kp{i}", [128, T], F32)) for i in range(2)]
                vtp = [pst.enter_context(nc.psum_tensor(f"vtp{i}", [128, 128], BF16)) for i in range(2)]
                kp_free = [0, 0]
                kp_free_a = [0, 0]
                vtp_free = [0, 0]
                khh_mark = {}
                wk_marks = {}
                wsrck = WIN.rearrange("(c q) o -> q c o", q=128)

                def issue_kdma(khh):
                    p_ = khh % 2
                    sw.tick(nc.gpsimd.dma_start(
                        wstream[:, p_ * 8192:(p_ + 1) * 8192].rearrange("q (c o) -> q c o", o=256),
                        wsrck[:, :, 4096 + khh * 256: 4096 + (khh + 1) * 256]), 16)
                    wk_marks[khh] = sw.n
                issue_kdma(0)
                issue_kdma(1)
                for khh in range(8):
                    p_ = khh % 2
                    kg = khh // 2
                    is_v = khh >= 4
                    if khh >= 2 and khh % 2 == 0:
                        nc.vector.wait_ge(sg.h, sg.n)
                        nc.scalar.wait_ge(sg.h, sg.n)
                    nc.tensor.wait_ge(sw.h, wk_marks[khh])
                    for hh2 in range(2):
                        h4 = (khh % 2) * 2 + hh2
                        pb = (khh * 2 + hh2) % 2
                        if kp_free[pb]:
                            nc.tensor.wait_ge(sv.h, kp_free[pb])
                        if kp_free_a[pb]:
                            nc.tensor.wait_ge(sa.h, kp_free_a[pb])
                        last = None
                        for tc in range(2):
                            nc.tensor.wait_ge(sv.h, lnm2["nslice"][tc * 2 + 1])
                            for c in range(NCH):
                                last = nc.tensor.matmul(
                                    kp[pb][:, tc * 512:(tc + 1) * 512],
                                    lhsT=wstream[:, p_ * 8192 + c * 256 + hh2 * 128: p_ * 8192 + c * 256 + (hh2 + 1) * 128],
                                    rhs=Bb[:, c * T + tc * 512: c * T + tc * 512 + 512],
                                    start=(c == 0), stop=(c == NCH - 1))
                        km = sp.tick(last)
                        if not is_v:
                            nc.scalar.wait_ge(sp.h, km)
                            am, dm = rope_evac(kp[pb][:, 0:T], kstg[:, h4 * T:(h4 + 1) * T], T, qtmp0, qswap0)
                            kp_free[pb] = dm
                            kp_free_a[pb] = am
                        else:
                            nc.scalar.wait_ge(sp.h, km)
                            vm = sa.tick(nc.scalar.activation(vstg[:, h4 * T:(h4 + 1) * T], kp[pb][:, 0:T], AF.Copy))
                            kp_free[pb] = 0
                            kp_free_a[pb] = vm
                            nc.tensor.wait_ge(sa.h, vm)
                            for j in range(8):
                                tb = j % 2
                                if vtp_free[tb]:
                                    nc.tensor.wait_ge(sv.h, vtp_free[tb])
                                trm = sp.tick(nc.tensor.transpose(
                                    vtp[tb][:, :], vstg[:, h4 * T + j * 128: h4 * T + (j + 1) * 128], ident[:]))
                                nc.vector.wait_ge(sp.h, trm)
                                vtp_free[tb] = sv.tick(nc.vector.tensor_copy(
                                    vtms[:, h4 * T + j * 128: h4 * T + (j + 1) * 128], vtp[tb][:, :]))
                    khh_mark[khh] = sp.n
                    if khh + 2 < 8:
                        nc.gpsimd.wait_ge(sp.h, khh_mark[khh])
                        issue_kdma(khh + 2)
                    if khh % 2 == 1:
                        nc.scalar.wait_ge(sv.h, sv.n)
                        if not is_v:
                            sg.tick(nc.scalar.dma_start(ccKi.ap()[:, kg * 4096:(kg + 1) * 4096], kstg[:]), 16)
                        else:
                            sg.tick(nc.scalar.dma_start(ccVi.ap()[:, (kg - 2) * 4096:(kg - 1) * 4096], vtms[:]), 16)
                    if khh == 3:
                        nc.gpsimd.wait_ge(sg.h, sg.n)
                        scc.tick(nc.gpsimd.collective_compute(
                            "AllGather", ALU.bypass, replica_groups=[[0, 1], [2, 3], [4, 5], [6, 7]],
                            ins=[ccKi.ap().opt()], outs=[ccKo.ap().opt()]), 1)
        kvscope.__exit__(None, None, None)
        nc.gpsimd.wait_ge(sg.h, sg.n)
        scc.tick(nc.gpsimd.collective_compute(
            "AllGather", ALU.bypass, replica_groups=[[0, 1], [2, 3], [4, 5], [6, 7]],
            ins=[ccVi.ap().opt()], outs=[ccVo.ap().opt()]), 1)

        barrier_all()
        atscope = scope("attn"); atscope.__enter__()
        with ExitStack() as ast:
            asb = lambda n_, s, d: ast.enter_context(nc.sbuf_tensor(n_, s, d))
            kbufs = [asb(f"kbuf{i}", [128, 2048], BF16) for i in range(2)]
            vbufs = [asb(f"vbuf{i}", [128, 2048], BF16) for i in range(2)]
            wq = wstream
            qh = [asb(f"qh{i}", [128, T], BF16) for i in range(2)]
            pt = [asb(f"pt{i}", [128, T], BF16) for i in range(3)]
            qtmp = asb("qtmp", [128, T], BF16)
            qswap = asb("qswap", [128, T], BF16)
            rz_sb = asb("rz_sb", [128, T], BF16)
            rzrow = asb("rzrow", [1, T], F32)
            rz2row = asb("rz2row", [1, T], BF16)

            def load_wq(g):
                for c in range(NCH):
                    sw.tick(nc.gpsimd.dma_start(
                        wq[:, c * 512:(c + 1) * 512],
                        WIN[c * 128:(c + 1) * 128, g * 512:(g + 1) * 512]), 16)
                return sw.n

            def q_make(Qp, g2, dst):
                qm_ = dm_ = 0
                for tc in range(2):
                    last = None
                    for c in range(NCH):
                        last = nc.tensor.matmul(
                            Qp[:, tc * 512:(tc + 1) * 512],
                            lhsT=wq[:, c * 512 + g2 * 128: c * 512 + (g2 + 1) * 128],
                            rhs=Bb[:, c * T + tc * 512: c * T + tc * 512 + 512],
                            start=(c == 0), stop=(c == NCH - 1))
                    qm_ = sp.tick(last)
                    nc.scalar.wait_ge(sp.h, qm_)
                    am_, dm_ = rope_evac(Qp[:, tc * 512:(tc + 1) * 512], dst[:, tc * 512:(tc + 1) * 512],
                                         512, qtmp, qswap, t0=tc * 512)
                return qm_, dm_

            wq_mark = load_wq(0)
            kv_marks = [0, 0]

            def load_kv(kh_):
                kb, vb = kbufs[kh_ % 2], vbufs[kh_ % 2]
                nc.sync.wait_ge(sp.h, sp.n)
                ss.tick(nc.sync.dma_start(kb[:, 0:1024], ccKo.ap()[0:128, kh_ * 1024:(kh_ + 1) * 1024]), 16)
                ss.tick(nc.sync.dma_start(kb[:, 1024:2048], ccKo.ap()[128:256, kh_ * 1024:(kh_ + 1) * 1024]), 16)
                ss.tick(nc.sync.dma_start(vb[:, 0:1024], ccVo.ap()[0:128, kh_ * 1024:(kh_ + 1) * 1024]), 16)
                kv_marks[kh_ % 2] = ss.tick(nc.sync.dma_start(vb[:, 1024:2048], ccVo.ap()[128:256, kh_ * 1024:(kh_ + 1) * 1024]), 16)
            qrope_marks = {}
            qrope_amarks = {}
            with ExitStack() as pst0:
                Qp0 = pst0.enter_context(nc.psum_tensor("Qpre", [128, T], F32))
                nc.tensor.wait_ge(sw.h, wq_mark)
                nc.tensor.wait_ge(sv.h, sv.n)
                nc.tensor.wait_ge(sa.h, sa.n)
                _, dm = q_make(Qp0, 0, qh[0])
                qrope_marks[0] = dm

            aevac_mark = 0
            rzc_mark = 0
            ln_mark = 0
            wq_loaded_group = 0
            for h in range(NQH):
                kh, g2 = h // 4, h % 4
                cur = h % 2
                kbuf, vbuf = kbufs[kh % 2], vbufs[kh % 2]
                if h == 0:
                    nc.sync.wait_ge(scc.h, 2)
                    load_kv(0)
                    load_kv(1)
                if g2 == 0 and 1 <= kh < NKV - 1:
                    load_kv(kh + 1)

                with ExitStack() as past:
                    attnp = past.enter_context(nc.psum_tensor(f"attnp{h}", [128, T], F32))
                    Zp = past.enter_context(nc.psum_tensor(f"Zp{h}", [1, T], F32))
                    exp_marks = {}
                    zpv_marks = {}
                    with ExitStack() as pst:
                        sT = [pst.enter_context(nc.psum_tensor(f"sT{i}_{h}", [128, T], F32)) for i in range(2)]
                        nc.tensor.wait_ge(sv.h, qrope_marks[h])
                        if rzc_mark:
                            nc.tensor.wait_ge(sv.h, rzc_mark)
                        if g2 == 0:
                            nc.tensor.wait_ge(ss.h, kv_marks[kh % 2])

                        def emit_zpv(idx):
                            i_, r_ = idx // 2, idx % 2
                            a_ = 128 * i_
                            mem = r_ * 8 + i_
                            nc.tensor.wait_ge(sa.h, exp_marks[idx])
                            if idx == 0:
                                if aevac_mark:
                                    nc.tensor.wait_ge(sv.h, aevac_mark)
                                if ln_mark:
                                    nc.tensor.wait_ge(sa.h, ln_mark)
                            last_ = None
                            for lo, hi in ((a_, 512), (max(a_, 512), 1024)):
                                if lo >= hi:
                                    continue
                                nc.tensor.matmul(Zp[0:1, lo:hi], lhsT=ones_c[:],
                                                 rhs=pt[idx % 3][:, lo - a_:hi - a_],
                                                 start=(idx == 0), stop=(idx == 15))
                                last_ = nc.tensor.matmul(attnp[:, lo:hi],
                                                         lhsT=vbuf[:, mem * 128:(mem + 1) * 128],
                                                         rhs=pt[idx % 3][:, lo - a_:hi - a_],
                                                         start=(idx == 0), stop=(idx == 15))
                            zpv_marks[idx] = sp.tick(last_)

                        for idx in range(16):
                            i, r = idx // 2, idx % 2
                            a = 128 * i
                            par = idx % 2
                            if idx >= 2:
                                nc.tensor.wait_ge(sa.h, exp_marks[idx - 2])
                            last = None
                            for lo, hi in ((a, 512), (max(a, 512), 1024)):
                                if lo >= hi:
                                    continue
                                last = nc.tensor.matmul(sT[par][:, lo:hi],
                                                        lhsT=kbuf[:, r * 1024 + a: r * 1024 + a + 128],
                                                        rhs=qh[cur][:, lo:hi], start=True, stop=True)
                            stm = sp.tick(last)
                            nc.vector.wait_ge(sp.h, stm)
                            mkm = sv.tick(nc.vector.tensor_tensor(
                                sT[par][:, a:a + 128], sT[par][:, a:a + 128],
                                (tri0 if r == 0 else tri1)[:], ALU.add))
                            nc.scalar.wait_ge(sv.h, mkm)
                            if idx >= 3:
                                nc.scalar.wait_ge(sp.h, zpv_marks[idx - 3])
                            exp_marks[idx] = sa.tick(nc.scalar.activation(
                                pt[idx % 3][:, 0:T - a], sT[par][:, a:T], AF.Exp, bias=EXPB, scale=SCALE))
                            if idx >= 2:
                                emit_zpv(idx - 2)
                        emit_zpv(14)
                        emit_zpv(15)

                    with ExitStack() as tst:
                        rzbp = tst.enter_context(nc.psum_tensor(f"rzbp{h}", [128, T], F32))
                        Qp = tst.enter_context(nc.psum_tensor(f"Qp{h}", [128, T], F32))
                        nc.scalar.wait_ge(sp.h, zpv_marks[15])
                        ln_mark = sa.tick(nc.scalar.activation(rzrow[0:1, :], Zp[0:1, :], AF.Ln))
                        nc.scalar.wait_ge(sa.h, ln_mark)
                        rzm = sa.tick(nc.scalar.activation(rz2row[0:1, :], rzrow[0:1, :], AF.Exp, scale=-1.0))
                        hn = h + 1
                        if hn < NQH:
                            nc.tensor.wait_ge(sw.h, wq_mark)
                            nc.tensor.wait_ge(sa.h, exp_marks[15])
                            qm, qdm = q_make(Qp, hn % 4, qh[hn % 2])
                            qrope_marks[hn] = qdm
                            if hn % 4 == 3 and hn + 1 < NQH:
                                nc.gpsimd.wait_ge(sp.h, qm)
                                wq_mark = load_wq((hn + 1) // 4)
                                wq_loaded_group = (hn + 1) // 4
                        nc.tensor.wait_ge(sa.h, rzm)
                        last = None
                        for reg in range(2):
                            last = nc.tensor.matmul(rzbp[:, reg * 512:(reg + 1) * 512], lhsT=ones_r[:],
                                                    rhs=rz2row[0:1, reg * 512:(reg + 1) * 512], start=True, stop=True)
                        rzbm = sp.tick(last)
                        nc.vector.wait_ge(sp.h, rzbm)
                        rzc_mark = sv.tick(nc.vector.tensor_copy(rz_sb[:], rzbp[:, :]))
                        if h == 0:
                            nc.vector.wait_ge(ss.h, spill_mark)
                        aevac_mark = sv.tick(nc.vector.tensor_tensor(
                            A[:, h * T:(h + 1) * T], attnp[:, :], rz_sb[:], ALU.mult))


        atscope.__exit__(None, None, None)
        barrier_all()
        opscope = scope("outproj"); opscope.__enter__()
        with ExitStack() as ost:
            xold = [ost.enter_context(nc.sbuf_tensor(f"xold{i}", [128, 4096], BF16)) for i in range(2)]
            with ExitStack() as pst:
                psums = [pst.enter_context(nc.psum_tensor(f"po{o}", [128, 512], F32)) for o in range(4)]
                psum_free = [0, 0, 0, 0]
                xold_marks = [0, 0]
                xold_free = [0, 0]
                oh_mark = {}
                w_marks = {}
                nc.sync.wait_ge(ss.h, spill_mark)
                nc.tensor.wait_ge(sv.h, sv.n)
                nc.tensor.wait_ge(sa.h, sa.n)
                wsrco = WOUT.rearrange("(c q) o -> q c o", q=128)
                def issue_odma(oh):
                    p_ = oh % 2
                    sw.tick(nc.gpsimd.dma_start(
                        wstream[:, p_ * 8192:(p_ + 1) * 8192].rearrange("q (c o) -> q c o", o=256),
                        wsrco[:, :, oh * 256:(oh + 1) * 256]), 16)
                    w_marks[oh] = sw.n
                issue_odma(0)
                issue_odma(1)
                for oh in range(16):
                    p_ = oh % 2
                    og = oh // 2
                    xb = og % 2
                    if oh % 2 == 0:
                        if xold_free[xb]:
                            nc.sync.wait_ge(sv.h, xold_free[xb])
                        xold_marks[xb] = ss.tick(nc.sync.dma_start(
                            xold[xb][:], spill[:, og * 4096:(og + 1) * 4096]), 16)
                    nc.tensor.wait_ge(sw.h, w_marks[oh])
                    for tc in range(2):
                        for o2 in range(2):
                            op = oh * 2 + o2
                            o = (op % 4)
                            pi = (oh % 2) * 2 + o2
                            if psum_free[pi]:
                                nc.tensor.wait_ge(sv.h, psum_free[pi])
                            last = None
                            for c in range(NCH):
                                last = nc.tensor.matmul(
                                    psums[pi][:, :],
                                    lhsT=wstream[:, p_ * 8192 + c * 256 + o2 * 128: p_ * 8192 + c * 256 + (o2 + 1) * 128],
                                    rhs=A[:, c * T + tc * 512: c * T + tc * 512 + 512],
                                    start=(c == 0), stop=(c == NCH - 1))
                            gm = sp.tick(last)
                            nc.vector.wait_ge(sp.h, gm)
                            if tc == 0 and o2 == 0 and oh % 2 == 0:
                                nc.vector.wait_ge(ss.h, xold_marks[xb])
                            dsl = slice(op * T + tc * 512, op * T + tc * 512 + 512)
                            lastv = nc.vector.tensor_tensor(
                                Bb[:, dsl], psums[pi][:, :], xold[xb][:, (o * 2 + tc) * 512:(o * 2 + tc + 1) * 512],
                                ALU.add)
                            psum_free[pi] = sv.tick(lastv)
                    for pi2 in range(4):
                        xold_free[xb] = max(xold_free[xb], psum_free[pi2])
                    oh_mark[oh] = sp.n
                    if oh + 2 < 16:
                        nc.gpsimd.wait_ge(sp.h, oh_mark[oh])
                        issue_odma(oh + 2)

        if stage == 3:
            nc.sync.wait_ge(sv.h, sv.n)
            ss.tick(nc.sync.dma_start(DBG, Bb[:, :]), 16)
            nc.sync.wait_ge(ss.h, ss.n)
            return nc
        if stage == 34:
            nc.sync.wait_ge(sv.h, sv.n)
            ss.tick(nc.sync.dma_start(DBG, Bb[:, :]), 16)

        opscope.__exit__(None, None, None)
        with scope("ln3"):
            lnm3 = ln(Bb, A)
        if stage == 35:
            nc.sync.wait_ge(sv.h, sv.n)
            ss.tick(nc.sync.dma_start(DBG, A[:, :]), 16)
        if stage == 36:
            ffn(A, Bb, False, "f2", ym=lnm3)
            nc.sync.wait_ge(sv.h, sv.n)
            ss.tick(nc.sync.dma_start(DBG, Bb[:, :]), 16)
            nc.sync.wait_ge(ss.h, ss.n)
            return nc
        with scope("ffn2"):
            ffn(A, Bb, True, "f2", ym=lnm3)

        nc.sync.wait_ge(ss.h, ss.n)
    return nc



def _host_tables():
    import jax
    import jax.numpy as jnp
    with jax.default_device(jax.devices("cpu")[0]):
        inv = 1.0 / (10000.0 ** (jnp.arange(0, HD, 2, dtype=jnp.float32) / HD))
        ang = jnp.arange(S, dtype=jnp.float32)[:, None] * inv[None, :]
        cosf = np.asarray(jnp.cos(ang), dtype=np.float32)
        sinf = np.asarray(jnp.sin(ang), dtype=np.float32)
    return cosf, sinf


def kernel(L0_attn_x, L0_attn_w_in, L0_attn_w_out, L0_ffn_base_W, L0_ffn_gate_W,
           L0_ffn_lora_A, L0_ffn_lora_B, _stage=4):
    x = np.asarray(L0_attn_x, np.float32)
    w_in = np.asarray(L0_attn_w_in, np.float32)
    w_out = np.asarray(L0_attn_w_out, np.float32)
    base_W = np.asarray(L0_ffn_base_W, np.float32)
    gate_W = np.asarray(L0_ffn_gate_W, np.float32)
    lora_A = np.asarray(L0_ffn_lora_A, np.float32)
    lora_B = np.asarray(L0_ffn_lora_B, np.float32)

    WIN = np.ascontiguousarray(w_in.T)
    WOUT = np.ascontiguousarray(w_out.T)
    BW = np.ascontiguousarray(base_W.T)
    AGW = np.ascontiguousarray(np.concatenate([lora_A.reshape(224, E).T, gate_W.T], axis=1))
    BRE = np.ascontiguousarray(lora_B.transpose(0, 2, 1).reshape(224, E))
    IDENT = np.eye(128, dtype=ml_dtypes.bfloat16)
    CONSTS = np.zeros((128, 8), np.float32)
    for i, v in enumerate([EXPB, SCALE, -1.0, EPS, -0.5]):
        CONSTS[:, i] = v

    cosf, sinf = _host_tables()
    qgrid, kgrid = np.meshgrid(np.arange(128), np.arange(128), indexing="xy")
    kk = np.arange(128)[:, None]
    qq = np.arange(128)[None, :]
    TRI = np.where(kk <= qq, 0.0, NEG).astype(np.float32)
    ZER = np.zeros((128, 128), np.float32)
    FULLM = np.full((128, 128), NEG, np.float32)

    nc = build_program(stage=_stage)
    in_maps = []
    pos_by_p = {}
    for p in range(2):
        pos = np.concatenate([np.arange(256 * j + 128 * p, 256 * j + 128 * p + 128) for j in range(8)])
        pos_by_p[p] = pos
    tables = {}
    for p in range(2):
        pos = pos_by_p[p]
        cd = np.empty((128, T), np.float32)
        sd = np.empty((128, T), np.float32)
        cd[0:64, :] = cosf[pos, :].T
        cd[64:128, :] = cosf[pos, :].T
        sd[0:64, :] = -sinf[pos, :].T
        sd[64:128, :] = sinf[pos, :].T
        tables[p] = (np.ascontiguousarray(cd), np.ascontiguousarray(sd))

    for c in range(8):
        b, p = c // 2, c % 2
        pos = pos_by_p[p]
        xT = np.ascontiguousarray(x[b][pos, :].T)
        cd, sd = tables[p]
        in_maps.append({
            "XT": xT, "WIN": WIN, "WOUT": WOUT, "BW": BW, "AGW": AGW, "BRE": BRE,
            "COS": cd, "SIN": sd,
            "MK0": TRI if p == 0 else ZER,
            "MK1": FULLM if p == 0 else TRI,
            "CONSTS": CONSTS, "IDENT": IDENT,
        })

    res = run_bass_kernel_spmd(nc, in_maps, core_ids=list(range(8)))

    if _stage != 4:
        return res

    out = np.empty((B, S, E), np.float32)
    for c in range(8):
        b, p = c // 2, c % 2
        o = np.asarray(res.results[c]["OUT"], np.float32)
        for j in range(8):
            out[b, 256 * j + 128 * p: 256 * j + 128 * p + 128, :] = o[:, j * 128:(j + 1) * 128].T
    return out
